# version 19
# baseline (speedup 1.0000x reference)
import numpy as np
import ml_dtypes
from contextlib import ExitStack

import concourse.bass as bass
import concourse.mybir as mybir
from concourse.bass_utils import run_bass_kernel_spmd

F32 = mybir.dt.float32
BF16 = mybir.dt.bfloat16
AF = mybir.ActivationFunctionType
ALU = mybir.AluOpType
AX = mybir.AxisListType
bf16 = ml_dtypes.bfloat16

NCORES = 8
H = 4096
S = 4096
B = 2
TPC = 1024
NH = 32
NKV = 8
HD = 128
NF = H // 128
NT = TPC // 512
SCALE = float(1.0 / np.sqrt(128.0))


def core_meta(c):
    b = c & 1
    mq = c >> 1
    rows = np.array([128 * (4 * li + mq) + r for li in range(8) for r in range(128)])
    return b, mq, rows


def rope_tables(rows):
    inv_freq = 1.0 / (10000.0 ** (np.arange(0, 128, 2, dtype=np.float64) / 128.0))
    ang = rows[None, :].astype(np.float64) * np.concatenate([inv_freq, inv_freq])[:, None]
    cos = np.cos(ang).astype(np.float32)
    sin = np.sin(ang).astype(np.float32)
    sin[0:64] *= -1.0
    return np.ascontiguousarray(cos), np.ascontiguousarray(sin)


def attn_masks(mq):
    m = np.zeros((4, 128, 128), np.float32)
    for kloc in range(4):
        if kloc < mq:
            m[kloc] = 1.0
        elif kloc == mq:
            kr = np.arange(128)[:, None]
            r = np.arange(128)[None, :]
            m[kloc] = (kr <= r).astype(np.float32)
    return np.ascontiguousarray(m.reshape(4 * 128, 128)).astype(bf16)


class Sem:
    __slots__ = ("h", "exp")

    def __init__(self, h):
        self.h = h
        self.exp = 0

    def inc(self, instr, n=1):
        instr.then_inc(self.h, n)
        self.exp += n
        return self.exp

    def fence(self, e, instr):
        self.inc(instr)
        e.wait_ge(self.h, self.exp)


class P:
    def __init__(self):
        self.nc = bass.Bass("TRN2", target_bir_lowering=False)
        self.es = ExitStack()
        self._scopes = [self.es]
        self._nsems = {}
        self._n = 0

    def scope(self):
        from contextlib import contextmanager

        @contextmanager
        def _s():
            st = ExitStack()
            self._scopes.append(st)
            try:
                yield
            finally:
                self._scopes.pop()
                st.close()
        return _s()

    def nsem(self, name):
        if name not in self._nsems:
            self._nsems[name] = self.sem()
        return self._nsems[name]

    def name(self, p):
        self._n += 1
        return f"{p}{self._n}"

    def sb(self, shape, dtype):
        return self._scopes[-1].enter_context(
            self.nc.sbuf_tensor(self.name("sb"), shape, dtype))

    def ps(self, shape, dtype=F32):
        return self._scopes[-1].enter_context(
            self.nc.psum_tensor(self.name("ps"), shape, dtype))

    def sem(self):
        return Sem(self.es.enter_context(self.nc.semaphore(self.name("sem"))))

    def inp(self, name, shape, dtype):
        return self.nc.dram_tensor(name, list(shape), dtype, kind="ExternalInput").ap()

    def out(self, name, shape, dtype):
        return self.nc.dram_tensor(name, list(shape), dtype, kind="ExternalOutput").ap()

    def close(self):
        self.es.close()


def wchunks(W):
    return W.rearrange("(f p) o -> p f o", p=128)


_A_STOP = None


def build_phase_A():
    p = P()
    nc = p.nc
    x = p.inp("x_fm", (H, TPC), F32)
    Wq = p.inp("Wq", (H, H), F32)
    Wk = p.inp("Wk", (H, 1024), F32)
    Wv = p.inp("Wv", (H, 1024), F32)
    WrT = p.inp("WrT", (H, 64), F32)
    rbias = p.inp("rbias", (1, 64), F32)
    cosT = p.inp("cosT", (HD, TPC), F32)
    sinT = p.inp("sinT", (HD, TPC), F32)
    identw = p.inp("ident", (128, 128), BF16)
    o_xn = p.out("xn", (H, TPC), BF16)
    o_q = p.out("Q", (H, TPC), BF16)
    o_k = p.out("K", (1024, TPC), BF16)
    o_v = p.out("Vtm", (TPC, 1024), BF16)
    o_kn = p.out("knormsq", (NKV, TPC), F32)
    o_qm = p.out("qmaxsq", (1, NH), F32)
    o_ws = p.out("wsum", (TPC, 1), F32)
    o_wd = p.out("wdenseT", (64, TPC), BF16)

    xn = p.sb([128, NF * TPC], BF16)
    xf32 = p.sb([128, NF * 512], F32)
    ones1 = p.sb([128, 1], BF16)
    ones1f = p.sb([128, 1], F32)
    ones64 = p.sb([128, 64], F32)
    onesr = p.sb([1, 128], F32)
    ident = p.sb([128, 128], BF16)
    cos_s = p.sb([128, TPC], F32)
    sin_s = p.sb([128, TPC], F32)
    stat = p.sb([1, 512], F32)
    mrow = p.sb([1, 512], F32)
    rrow = p.sb([1, 512], F32)
    mB = p.sb([128, 512], F32)
    rB = p.sb([128, 512], F32)
    sq = [p.sb([128, 512], F32) for _ in range(2)]
    band = p.sb([128, NF * 128], BF16)
    qf = p.sb([128, 512], F32)
    ta = p.sb([128, 512], F32)
    tb = p.sb([128, 512], F32)
    qbf = p.sb([128, 512], BF16)
    vstage = p.sb([128, 512], BF16)
    qn = p.sb([1, 512], F32)
    qmx = p.sb([1, NH], F32)
    knrow = p.sb([1, 512], F32)

    s_ld = p.sem()
    s_wb = p.sem()
    s_mm = p.sem()
    s_ve = p.sem()
    s_sc = p.sem()
    s_st = p.sem()

    ps_sum = p.ps([128, 512])[0:1, :]
    ps_sq = p.ps([128, 512])[0:1, :]

    def _finish():
        with nc.Block() as blk:
            @blk.sync
            def _(e):
                e.wait_ge(s_st.h, s_st.exp)
        p.close()
        return nc
    _ps_b_t = p.ps([128, 512])
    ps_b = _ps_b_t[:]
    ps_b_64 = _ps_b_t[:, 0:64]
    ps_mm = p.ps([128, 512])[:]
    ps_n = p.ps([128, 512])[0:1, :]
    _ps_tb_t = p.ps([128, 1024], BF16)
    ps_tb = _ps_tb_t[:, 0:512]
    ps_tb_k4 = [_ps_tb_t[:, k * 128:(k + 1) * 128] for k in range(4)]

    with nc.Block() as blk:
        @blk.sync
        def _(e):
            e.dma_start(out=cos_s[:], in_=cosT[:, :]).then_inc(s_ld.h, 16)
            e.dma_start(out=sin_s[:], in_=sinT[:, :]).then_inc(s_ld.h, 16)
        @blk.scalar
        def _(e):
            e.dma_start(out=ident[:], in_=identw[:, :]).then_inc(s_ld.h, 16)
        @blk.vector
        def _(e):
            e.memset(ones1[:], 1.0)
            e.memset(ones1f[:], 1.0)
            e.memset(ones64[:], 1.0)
            e.memset(onesr[:], 1.0)
            e.memset(qmx[:], 0.0)
        s_ld.exp += 48

    for T in range(NT):
        with nc.Block() as blk:
            @blk.sync
            def _(e):
                e.dma_start(
                    out=xf32[:].rearrange("p (f t) -> p f t", f=NF),
                    in_=x.rearrange("(f p) t -> p f t", p=128)[:, :, T * 512:(T + 1) * 512],
                ).then_inc(s_ld.h, 16)
            s_ld.exp += 16
            ldt = s_ld.exp

            @blk.scalar
            def _(e):
                e.wait_ge(s_ld.h, ldt)
                base_mm = s_mm.exp
                for f in range(NF):
                    if f >= 2:
                        e.wait_ge(s_mm.h, base_mm + f - 1)
                    s_sc.inc(e.square(out=sq[f % 2][:], in_=xf32[:, f * 512:(f + 1) * 512]))

            @blk.tensor
            def _(e):
                e.wait_ge(s_ld.h, ldt)
                base_sc = s_sc.exp - NF
                for f in range(NF):
                    e.matmul(ps_sum, ones1f[:], xf32[:, f * 512:(f + 1) * 512],
                             start=(f == 0), stop=(f == NF - 1))
                    e.wait_ge(s_sc.h, base_sc + f + 1)
                    s_mm.inc(e.matmul(ps_sq, ones1f[:], sq[f % 2][:],
                             start=(f == 0), stop=(f == NF - 1)))

        with nc.Block() as blk:
            @blk.vector
            def _(e):
                e.tensor_scalar(out=mrow[:], in0=ps_sum[0:1, :], scalar1=1.0 / H,
                                scalar2=None, op0=ALU.mult)
                e.tensor_scalar(out=stat[:], in0=ps_sq, scalar1=1.0 / H,
                                scalar2=None, op0=ALU.mult)
                e.tensor_tensor(out=rrow[:], in0=mrow[:], in1=mrow[:], op=ALU.mult)
                e.tensor_tensor(out=stat[:], in0=stat[:], in1=rrow[:], op=ALU.subtract)
                s_ve.inc(e.tensor_scalar(out=stat[:], in0=stat[:], scalar1=1e-5,
                                scalar2=None, op0=ALU.add))
            vt = s_ve.exp
            @blk.scalar
            def _(e):
                e.wait_ge(s_ve.h, vt)
                s_sc.fence(e, e.activation(out=stat[:], in_=stat[:], func=AF.Ln))
                s_sc.inc(e.activation(out=rrow[:], in_=stat[:], func=AF.Exp, scale=-0.5))
            st_ = s_sc.exp
            @blk.tensor
            def _(e):
                e.wait_ge(s_sc.h, st_)
                s_mm.inc(e.matmul(ps_b, onesr[:], mrow[:], start=True, stop=True))
            mt = s_mm.exp
            @blk.vector
            def _(e):
                e.wait_ge(s_mm.h, mt)
                s_ve.inc(e.tensor_copy(out=mB[:], in_=ps_b))
            vt2 = s_ve.exp
            @blk.tensor
            def _(e):
                e.wait_ge(s_ve.h, vt2)
                s_mm.inc(e.matmul(ps_b, onesr[:], rrow[:], start=True, stop=True))
            mt2 = s_mm.exp
            @blk.vector
            def _(e):
                e.wait_ge(s_mm.h, mt2)
                e.tensor_copy(out=rB[:], in_=ps_b)
                for f in range(NF):
                    e.tensor_tensor(out=sq[0][:], in0=xf32[:, f * 512:(f + 1) * 512],
                                    in1=mB[:], op=ALU.subtract)
                    last = e.tensor_tensor(
                        out=xn[:, f * 1024 + T * 512: f * 1024 + T * 512 + 512],
                        in0=sq[0][:], in1=rB[:], op=ALU.mult)
                s_ve.inc(last)

    with nc.Block() as blk:
        @blk.sync
        def _(e):
            e.dma_start(out=o_xn.rearrange("(f p) t -> p f t", p=128),
                        in_=xn[:].rearrange("p (f t) -> p f t", f=NF)).then_inc(s_st.h, 16)
            s_st.exp += 16

    if _A_STOP == "ln":
        return _finish()

    def proj(W, o, do_rope, do_norm, norm_out_ap, store_fn):
        with nc.Block() as blk:
            @blk.gpsimd
            def _(e):
                e.dma_start(out=band[:].rearrange("p (f o) -> p f o", o=128),
                            in_=wchunks(W)[:, :, o * 128:(o + 1) * 128]
                            ).then_inc(s_wb.h, 16)
            s_wb.exp += 16
            wbt = s_wb.exp
            for T in range(NT):
                @blk.tensor
                def _(e):
                    e.wait_ge(s_wb.h, wbt)
                    if T == 1:
                        e.wait_ge(s_ve.h, s_ve.exp)
                        e.wait_ge(s_sc.h, s_sc.exp)
                    for f in range(NF):
                        mm = e.matmul(ps_mm, band[:, f * 128:(f + 1) * 128],
                                 xn[:, f * 1024 + T * 512: f * 1024 + T * 512 + 512],
                                 start=(f == 0), stop=(f == NF - 1))
                    s_mm.inc(mm)
                mt = s_mm.exp

                if do_norm:
                    @blk.scalar
                    def _(e):
                        e.wait_ge(s_mm.h, mt)
                        s_sc.inc(e.square(out=sq[0][:], in_=ps_mm))
                    st1 = s_sc.exp
                    @blk.tensor
                    def _(e):
                        e.wait_ge(s_sc.h, st1)
                        s_mm.inc(e.matmul(ps_n, ones1f[:], sq[0][:], start=True, stop=True))
                    mt2 = s_mm.exp
                    @blk.vector
                    def _(e):
                        e.wait_ge(s_mm.h, mt2)
                        s_ve.inc(norm_out_ap(e, T))

                stc = s_st.exp
                @blk.vector
                def _(e, stc=stc):
                    e.wait_ge(s_mm.h, mt)
                    e.wait_ge(s_st.h, stc)
                    e.tensor_copy(out=qf[:], in_=ps_mm)
                    if do_rope:
                        cs = cos_s[:, T * 512:(T + 1) * 512]
                        sn = sin_s[:, T * 512:(T + 1) * 512]
                        e.tensor_copy(out=tb[0:64, :], in_=qf[64:128, :])
                        e.tensor_copy(out=tb[64:128, :], in_=qf[0:64, :])
                        e.tensor_tensor(out=ta[:], in0=qf[:], in1=cs, op=ALU.mult)
                        e.tensor_tensor(out=tb[:], in0=tb[:], in1=sn, op=ALU.mult)
                        last = e.tensor_tensor(out=qbf[:], in0=ta[:], in1=tb[:], op=ALU.add)
                    else:
                        last = e.tensor_copy(out=vstage[:], in_=ps_mm)
                    s_ve.inc(last)
                vt = s_ve.exp
                store_fn(blk, T, vt)

    def q_norm(e, T, o=None):
        s_ve.fence(e, e.tensor_copy(out=qn[:], in_=ps_n))
        s_ve.fence(e, e.reduce_max(out=qn[:, 1:2], in_=qn[:], axis=AX.X))
        return e.tensor_tensor(out=qmx[:, q_norm.o:q_norm.o + 1],
                        in0=qmx[:, q_norm.o:q_norm.o + 1],
                        in1=qn[:, 1:2], op=ALU.max)

    for o in range(NH):
        q_norm.o = o

        def q_store(blk, T, vt, o=o):
            @blk.sync
            def _(e):
                e.wait_ge(s_ve.h, vt)
                e.dma_start(out=o_q[o * 128:(o + 1) * 128, T * 512:(T + 1) * 512],
                            in_=qbf[:]).then_inc(s_st.h, 16)
            s_st.exp += 16

        proj(Wq, o, do_rope=True, do_norm=True, norm_out_ap=q_norm, store_fn=q_store)

    if _A_STOP == "q":
        return _finish()

    for g in range(NKV):
        def k_norm(e, T, g=g):
            return e.tensor_copy(out=knrow[:], in_=ps_n)

        def k_store(blk, T, vt, g=g):
            @blk.sync
            def _(e):
                e.wait_ge(s_ve.h, vt)
                e.dma_start(out=o_k[g * 128:(g + 1) * 128, T * 512:(T + 1) * 512],
                            in_=qbf[:]).then_inc(s_st.h, 16)
                e.dma_start(out=o_kn[g:g + 1, T * 512:(T + 1) * 512],
                            in_=knrow[:]).then_inc(s_st.h, 16)
            s_st.exp += 32

        proj(Wk, g, do_rope=True, do_norm=True, norm_out_ap=k_norm, store_fn=k_store)

    if _A_STOP == "k":
        return _finish()

    for g in range(NKV):
        def v_store(blk, T, vt, g=g):
            @blk.tensor
            def _(e):
                e.wait_ge(s_ve.h, vt)
                for k4 in range(4):
                    mm = e.transpose(ps_tb_k4[k4],
                                vstage[:, k4 * 128:(k4 + 1) * 128], ident[:])
                s_mm.inc(mm)
            mt3 = s_mm.exp
            stc2 = s_st.exp
            @blk.vector
            def _(e, stc2=stc2):
                e.wait_ge(s_mm.h, mt3)
                e.wait_ge(s_st.h, stc2)
                s_ve.inc(e.tensor_copy(out=qbf[:], in_=ps_tb))
            vt3 = s_ve.exp
            @blk.sync
            def _(e):
                e.wait_ge(s_ve.h, vt3)
                e.dma_start(
                    out=o_v[T * 512:(T + 1) * 512, g * 128:(g + 1) * 128]
                    .rearrange("(k p) d -> p k d", p=128),
                    in_=qbf[:].rearrange("p (k d) -> p k d", k=4),
                ).then_inc(s_st.h, 16)
            s_st.exp += 16

        proj(Wv, g, do_rope=False, do_norm=False, norm_out_ap=None, store_fn=v_store)

    if _A_STOP == "v":
        return _finish()

    wr_full = p.sb([128, NF * 64], BF16)
    rb_row = p.sb([1, 64], F32)
    rbB = p.sb([128, 64], F32)
    probs = p.sb([128, 64], F32)
    work = p.sb([128, 64], F32)
    eqm = p.sb([128, 64], F32)
    mask8 = p.sb([128, 64], F32)
    wsv = p.sb([128, 1], F32)
    mrow1 = p.sb([128, 1], F32)
    zr = p.sb([128, 1], F32)
    wde = p.sb([128, 64], BF16)
    wdeT = p.sb([64, 128], BF16)
    ps_r = p.ps([128, 512])[:, 0:64]
    ps_wt = p.ps([128, 1024], BF16)[0:64, 0:128]

    with nc.Block() as blk:
        @blk.gpsimd
        def _(e):
            e.dma_start(out=wr_full[:].rearrange("p (f o) -> p f o", o=64),
                        in_=wchunks(WrT)).then_inc(s_wb.h, 16)
        @blk.sync
        def _(e):
            e.dma_start(out=rb_row[:], in_=rbias[:, :]).then_inc(s_wb.h, 16)
        s_wb.exp += 32
        wbr = s_wb.exp
        @blk.tensor
        def _(e):
            e.wait_ge(s_wb.h, wbr)
            s_mm.inc(e.matmul(ps_b_64, onesr[:], rb_row[:], start=True, stop=True))
        mt = s_mm.exp
        @blk.vector
        def _(e):
            e.wait_ge(s_mm.h, mt)
            s_ve.inc(e.tensor_copy(out=rbB[:], in_=ps_b_64))

    for ts in range(8):
        with nc.Block() as blk:
            @blk.tensor
            def _(e):
                for f in range(NF):
                    mm = e.matmul(ps_r,
                             xn[:, f * 1024 + ts * 128: f * 1024 + ts * 128 + 128],
                             wr_full[:, f * 64:(f + 1) * 64],
                             start=(f == 0), stop=(f == NF - 1))
                s_mm.inc(mm)
            mt = s_mm.exp
            @blk.vector
            def _(e):
                e.wait_ge(s_mm.h, mt)
                s_ve.fence(e, e.tensor_tensor(out=probs[:], in0=ps_r, in1=rbB[:], op=ALU.add))
                s_ve.inc(e.reduce_max(out=mrow1[:], in_=probs[:], axis=AX.X, negate=True))
            vt = s_ve.exp
            @blk.scalar
            def _(e):
                e.wait_ge(s_ve.h, vt)
                act = e.activation(out=probs[:], in_=probs[:], func=AF.Exp,
                             bias=mrow1[:])
                s_sc.inc(act)
            st2 = s_sc.exp
            @blk.vector
            def _(e):
                e.wait_ge(s_sc.h, st2)
                s_ve.fence(e, e.reduce_sum(out=zr[:], in_=probs[:], axis=AX.X))
                s_ve.fence(e, e.reciprocal(out=zr[:], in_=zr[:]))
                e.memset(mask8[:], 0.0)
                e.memset(wsv[:], 0.0)
                s_ve.fence(e, e.tensor_copy(out=work[:], in_=probs[:]))
                for it in range(8):
                    s_ve.fence(e, e.reduce_max(out=mrow1[:], in_=work[:], axis=AX.X))
                    s_ve.fence(e, e.scalar_tensor_tensor(out=eqm[:], in0=work[:], scalar=mrow1[:],
                                           in1=ones64[:], op0=ALU.is_ge, op1=ALU.mult))
                    e.tensor_tensor(out=mask8[:], in0=mask8[:], in1=eqm[:], op=ALU.max)
                    e.scalar_tensor_tensor(out=wsv[:], in0=mrow1[:], scalar=0.0,
                                           in1=wsv[:], op0=ALU.max, op1=ALU.add)
                    s_ve.fence(e, e.scalar_tensor_tensor(out=work[:], in0=eqm[:], scalar=-1e30,
                                           in1=work[:], op0=ALU.mult, op1=ALU.add))
                s_ve.fence(e, e.tensor_tensor(out=wsv[:], in0=wsv[:], in1=zr[:], op=ALU.mult))
                s_ve.inc(e.scalar_tensor_tensor(out=wde[:], in0=mask8[:], scalar=zr[:],
                                                in1=probs[:], op0=ALU.mult, op1=ALU.mult))
            vt2 = s_ve.exp
            @blk.tensor
            def _(e):
                e.wait_ge(s_ve.h, vt2)
                s_mm.inc(e.transpose(ps_wt, wde[:, 0:64], ident[:]))
            mt2 = s_mm.exp
            @blk.vector
            def _(e):
                e.wait_ge(s_mm.h, mt2)
                s_ve.inc(e.tensor_copy(out=wdeT[:], in_=ps_wt))
            vt3 = s_ve.exp
            @blk.sync
            def _(e):
                e.wait_ge(s_ve.h, vt3)
                e.dma_start(out=o_ws[ts * 128:(ts + 1) * 128, :],
                            in_=wsv[:]).then_inc(s_st.h, 16)
                e.dma_start(out=o_wd[0:64, ts * 128:(ts + 1) * 128],
                            in_=wdeT[:]).then_inc(s_st.h, 16)
            s_st.exp += 32

    with nc.Block() as blk:
        @blk.sync
        def _(e):
            e.dma_start(out=o_qm[:, :], in_=qmx[:]).then_inc(s_st.h, 16)
            s_st.exp += 16
            e.wait_ge(s_st.h, s_st.exp)

    p.close()
    return nc


def build_phase_B(p=None, io=None):
    standalone = p is None
    if standalone:
        p = P()
        io = {}
    nc = p.nc
    Qf = io.get("Q") or p.inp("Q", (H, TPC), BF16)
    Kf = io.get("Kfull") or p.inp("Kfull", (1024, S), BF16)
    Vf = io.get("Vtmfull") or p.inp("Vtmfull", (S, 1024), BF16)
    e8 = io.get("E8") or p.inp("E8", (NKV, NH), F32)
    maskw = io.get("masks") or p.inp("masks", (4 * 128, 128), BF16)
    if standalone:
        knf = p.inp("knormsq_full", (NKV, S), F32)
        qmf = p.inp("qmaxsq", (1, NH), F32)
        o_ao = p.out("attnout", (H, TPC), BF16)

    attnout = io.get("attnout_sb")
    if attnout is None:
        attnout = p.sb([128, NH * TPC], BF16)
    kg = p.sb([128, S], BF16)
    vg = p.sb([128, S], BF16)
    qh = p.sb([128, TPC], BF16)
    attnT = [p.sb([128, TPC], BF16) for _ in range(2)]
    maskt = p.sb([128, 4 * 128], BF16)
    knrows = p.sb([NKV, S], F32) if standalone else None
    kx8 = p.sb([NKV, 1], F32)
    qmr = io.get("qmx_sb")
    if qmr is None:
        qmr = p.sb([1, NH], F32)
    negm = p.sb([1, NH], F32)
    negMB = p.sb([128, NH], F32)
    e8s = p.sb([NKV, NH], F32)
    ones1 = p.sb([128, 1], BF16)
    onesr = p.sb([1, 128], F32)
    zrow = p.sb([1, TPC], F32)
    zB = p.sb([128, TPC], F32)

    s_ld = p.nsem("ld")
    s_qk = p.nsem("qk")
    s_zav = p.nsem("zav")
    s_sc = p.nsem("sc")
    s_ve = p.nsem("ve")
    s_mmx = p.nsem("mm")
    s_st = p.nsem("st")

    ps_s = [p.ps([128, 512]) for _ in range(4)]
    ps_av = p.ps([128, 1024])
    ps_z = p.ps([128, 1024])

    def s_half(par, j):
        return ps_s[2 * par + j]

    with nc.Block() as blk:
        @blk.sync
        def _(e):
            e.dma_start(out=maskt[:].rearrange("p (a q) -> p a q", a=4),
                        in_=maskw.rearrange("(a p) q -> p a q", p=128)).then_inc(s_ld.h, 16)
            e.dma_start(out=e8s[:], in_=e8[:, :]).then_inc(s_ld.h, 16)
            s_ld.exp += 32
            if standalone:
                e.dma_start(out=knrows[:], in_=knf[:, :]).then_inc(s_ld.h, 16)
                e.dma_start(out=qmr[:], in_=qmf[:, :]).then_inc(s_ld.h, 16)
                s_ld.exp += 32
        ldt = s_ld.exp
        @blk.vector
        def _(e):
            e.memset(ones1[:], 1.0)
            e.memset(onesr[:], 1.0)
            e.wait_ge(s_ld.h, ldt)
            if standalone:
                s_ve.fence(e, e.reduce_max(out=kx8[:], in_=knrows[:], axis=AX.X))
            s_ve.inc(e.nop())
        vt = s_ve.exp
        if not standalone:
            @blk.tensor
            def _(e):
                e.wait_ge(s_ve.h, vt)
                s_mmx.inc(e.matmul(ps_av[0:NKV, 0:1], io["kmax_sb"][:, :],
                                   onesr[0:1, 0:1], start=True, stop=True))
            mtx = s_mmx.exp
            @blk.vector
            def _(e, mtx=mtx):
                e.wait_ge(s_mmx.h, mtx)
                s_ve.inc(e.tensor_copy(out=kx8[:], in_=ps_av[0:NKV, 0:1]))
            vt = s_ve.exp
        @blk.tensor
        def _(e):
            e.wait_ge(s_ve.h, vt)
            s_mmx.inc(e.matmul(ps_z[0:1, 0:NH], kx8[:], e8s[:], start=True, stop=True))
        mt = s_mmx.exp
        @blk.vector
        def _(e):
            e.wait_ge(s_mmx.h, mt)
            s_ve.fence(e, e.tensor_copy(out=negm[:], in_=ps_z[0:1, 0:NH]))
            s_ve.inc(e.tensor_tensor(out=negm[:], in0=negm[:], in1=qmr[:], op=ALU.mult))
        vt2 = s_ve.exp
        @blk.scalar
        def _(e):
            e.wait_ge(s_ve.h, vt2)
            s_sc.fence(e, e.activation(out=negm[:], in_=negm[:], func=AF.Ln))
            s_sc.inc(e.activation(out=negm[:], in_=negm[:], func=AF.Exp, scale=0.5))
        st1 = s_sc.exp
        @blk.vector
        def _(e):
            e.wait_ge(s_sc.h, st1)
            s_ve.inc(e.tensor_scalar(out=negm[:], in0=negm[:], scalar1=-SCALE,
                                     scalar2=None, op0=ALU.mult))
        vt3 = s_ve.exp
        @blk.tensor
        def _(e):
            e.wait_ge(s_ve.h, vt3)
            s_mmx.inc(e.matmul(ps_z[:, 0:NH], onesr[:], negm[:], start=True, stop=True))
        mt2 = s_mmx.exp
        @blk.vector
        def _(e):
            e.wait_ge(s_mmx.h, mt2)
            s_ve.inc(e.tensor_copy(out=negMB[:], in_=ps_z[:, 0:NH]))

    for g in range(NKV):
        with nc.Block() as blk:
            @blk.sync
            def _(e):
                e.dma_start(out=kg[:], in_=Kf[g * 128:(g + 1) * 128, :]).then_inc(s_ld.h, 16)
                e.dma_start(
                    out=vg[:].rearrange("p (k d) -> p k d", d=128),
                    in_=Vf.rearrange("(k p) gd -> p k gd", p=128)[:, :, g * 128:(g + 1) * 128],
                ).then_inc(s_ld.h, 16)
            s_ld.exp += 32

        for hl in range(4):
            h = 4 * g + hl
            with nc.Block() as blk:
                @blk.scalar
                def _(e):
                    e.dma_start(
                        out=qh[:].rearrange("p (l r) -> p l r", l=8),
                        in_=Qf[h * 128:(h + 1) * 128, :]
                        .rearrange("p (l r) -> p l r", l=8)[:, ::-1, :],
                    ).then_inc(s_ld.h, 16)
                s_ld.exp += 16
                ldq = s_ld.exp

                base_sc = s_sc.exp
                base_ve = s_ve.exp
                base_qk = s_qk.exp
                base_zav = s_zav.exp

                def emit_zav(e, kz):
                    parz = kz % 2
                    wz = (8 - kz // 4) * 128
                    for j in range((wz + 511) // 512):
                        w = min(512, wz - j * 512)
                        e.matmul(ps_z[0:1, j * 512:j * 512 + w], ones1[:],
                                 attnT[parz][:, j * 512:j * 512 + w],
                                 start=(kz == 0),
                                 stop=(kz == (31 if j == 0 else 15)))
                        mm = e.matmul(ps_av[:, j * 512:j * 512 + w],
                                      vg[:, kz * 128:(kz + 1) * 128],
                                      attnT[parz][:, j * 512:j * 512 + w],
                                      start=(kz == 0),
                                      stop=(kz == (31 if j == 0 else 15)))
                    s_zav.inc(mm)

                @blk.tensor
                def _(e):
                    e.wait_ge(s_ld.h, ldq)
                    for kap in range(32):
                        par = kap % 2
                        wq = (8 - kap // 4) * 128
                        if kap >= 2:
                            e.wait_ge(s_sc.h, base_sc + kap - 1)
                        for j in range((wq + 511) // 512):
                            w = min(512, wq - j * 512)
                            mm = e.matmul(s_half(par, j)[:, 0:w],
                                          kg[:, kap * 128:(kap + 1) * 128],
                                          qh[:, j * 512:j * 512 + w],
                                          start=True, stop=True)
                        s_qk.inc(mm)
                        if kap >= 1:
                            kz = kap - 1
                            e.wait_ge(s_ve.h, base_ve + kz + 1)
                            emit_zav(e, kz)
                    e.wait_ge(s_ve.h, base_ve + 32)
                    emit_zav(e, 31)

                @blk.scalar
                def _(e):
                    for kap in range(32):
                        par = kap % 2
                        wq = (8 - kap // 4) * 128
                        e.wait_ge(s_qk.h, base_qk + kap + 1)
                        if kap >= 2:
                            e.wait_ge(s_zav.h, base_zav + kap - 1)
                        for j in range((wq + 511) // 512):
                            w = min(512, wq - j * 512)
                            act = e.activation(out=attnT[par][:, j * 512:j * 512 + w],
                                               in_=s_half(par, j)[:, 0:w],
                                               func=AF.Exp, scale=SCALE,
                                               bias=negMB[:, h:h + 1])
                        s_sc.inc(act)

                @blk.vector
                def _(e):
                    for kap in range(32):
                        par = kap % 2
                        wq = (8 - kap // 4) * 128
                        e.wait_ge(s_sc.h, base_sc + kap + 1)
                        s_ve.inc(e.tensor_tensor(
                            out=attnT[par][:, wq - 128:wq],
                            in0=attnT[par][:, wq - 128:wq],
                            in1=maskt[:, (kap % 4) * 128:(kap % 4 + 1) * 128],
                            op=ALU.mult))

            with nc.Block() as blk:
                @blk.vector
                def _(e):
                    s_ve.fence(e, e.tensor_copy(out=zrow[:], in_=ps_z[0:1, :]))
                    s_ve.inc(e.reciprocal(out=zrow[:], in_=zrow[:]))
                vt = s_ve.exp
                @blk.tensor
                def _(e):
                    e.wait_ge(s_ve.h, vt)
                    for j in range(2):
                        mm = e.matmul(ps_z[:, j * 512:(j + 1) * 512], onesr[:],
                                      zrow[:, j * 512:(j + 1) * 512], start=True, stop=True)
                    s_mmx.inc(mm)
                mt = s_mmx.exp
                @blk.vector
                def _(e):
                    e.wait_ge(s_mmx.h, mt)
                    s_ve.fence(e, e.tensor_copy(out=zB[:], in_=ps_z[:]))
                    for lb in range(8):
                        mmv = e.tensor_tensor(
                            out=attnout[:, h * 1024 + (7 - lb) * 128:
                                        h * 1024 + (8 - lb) * 128],
                            in0=ps_av[:, lb * 128:(lb + 1) * 128],
                            in1=zB[:, lb * 128:(lb + 1) * 128],
                            op=ALU.mult)
                    s_ve.inc(mmv)

    if standalone:
        with nc.Block() as blk:
            @blk.sync
            def _(e):
                e.dma_start(out=o_ao.rearrange("(hh p) t -> p hh t", p=128),
                            in_=attnout[:].rearrange("p (hh t) -> p hh t", hh=NH)
                            ).then_inc(s_st.h, 16)
                s_st.exp += 16
                e.wait_ge(s_st.h, s_st.exp)
        p.close()
        return nc
    return attnout


def build_phase_C(p=None, io=None):
    standalone = p is None
    if standalone:
        p = P()
        io = {}
    nc = p.nc
    Wo = io.get("Wo") or p.inp("Wo", (H, H), F32)
    x = io.get("x_fm") or p.inp("x_fm", (H, TPC), F32)
    ebias = io.get("ebias") or p.inp("ebias", (64, H), F32)
    wdownT = io.get("wdownT") or p.inp("wdownT", (H, 128), F32)
    bdown = io.get("bdown") or p.inp("bdown", (128, 1), F32)
    o_x1 = io.get("o_x1")
    if o_x1 is None:
        o_x1 = p.out("x1_fm", (H, TPC), F32)
    o_xn1 = io.get("o_xn1")
    if o_xn1 is None:
        o_xn1 = p.out("xn1", (H, TPC), BF16)
    o_xmid = io.get("o_xmid")
    if o_xmid is None:
        o_xmid = p.out("xmid", (128, TPC), BF16)
    if standalone:
        ao = p.inp("attnout", (H, TPC), BF16)
        xnw_in = p.inp("xn", (H, TPC), BF16)
        wsum = p.inp("wsum", (TPC, 1), F32)
        wdeT = p.inp("wdenseT", (64, TPC), BF16)
    else:
        xnw_in = io.get("xn_hbm")
    use_xnt = xnw_in is not None

    aor = io.get("attnout_sb")
    if aor is None:
        aor = p.sb([128, NH * TPC], BF16)
    band = p.sb([128, NF * 128], BF16)
    wdb = p.sb([128, NF * 128], BF16)
    eb_o = [p.sb([64, 128], BF16) for _ in range(2)]
    wde = io.get("wde_sb")
    if wde is None:
        wde = p.sb([64, TPC], BF16)
    wsrow = io.get("wsrow_sb")
    if wsrow is None:
        wsrow = p.sb([1, TPC], F32)
    wsB = p.sb([128, TPC], F32)
    bdn = p.sb([128, 1], F32)
    ones1f = p.sb([128, 1], F32)
    onesr = p.sb([1, 128], F32)
    xt = [p.sb([128, 512], F32) for _ in range(4)]
    xnt = [p.sb([128, 512], BF16) for _ in range(4)] if use_xnt else None
    tmp = [p.sb([128, 512], F32) for _ in range(2)]
    x1b = [p.sb([128, 512], F32) for _ in range(4)]
    sqc = [p.sb([128, 512], F32) for _ in range(2)]
    stat = p.sb([1, 512], F32)
    mrow = [p.sb([1, 512], F32) for _ in range(2)]
    rrow = [p.sb([1, 512], F32) for _ in range(2)]
    mB = [p.sb([128, 512], F32) for _ in range(2)]
    rB = [p.sb([128, 512], F32) for _ in range(2)]
    x1t2 = [p.sb([128, 512], F32) for _ in range(2)]
    xn1t = [p.sb([128, 512], BF16) for _ in range(2)]
    xmid_sb = p.sb([128, 1024], BF16)

    s_ld = p.nsem("ld")
    s_wb = p.nsem("wb")
    s_mm = p.nsem("mm")
    s_ve = p.nsem("ve")
    s_sc = p.nsem("sc")
    s_st = p.nsem("st")

    x1ps = [p.ps([128, 512]) for _ in range(2)]
    st_sum = [p.ps([128, 512])[0:1, :] for _ in range(2)]
    st_sq = [p.ps([128, 512])[0:1, :] for _ in range(2)]
    xmid_ps = p.ps([128, 1024])

    with nc.Block() as blk:
        @blk.sync
        def _(e):
            e.dma_start(out=bdn[:], in_=bdown[:, :]).then_inc(s_ld.h, 16)
            s_ld.exp += 16
            if standalone:
                e.dma_start(out=aor[:].rearrange("p (hh t) -> p hh t", hh=NH),
                            in_=ao.rearrange("(hh p) t -> p hh t", p=128)).then_inc(s_ld.h, 16)
                e.dma_start(out=wde[:], in_=wdeT[:, :]).then_inc(s_ld.h, 16)
                e.dma_start(out=wsrow[:], in_=wsum.rearrange("t o -> o t")).then_inc(s_ld.h, 16)
                s_ld.exp += 48
            else:
                e.dma_start(out=wsrow[:],
                            in_=io["wsum_hbm"].rearrange("t o -> o t")).then_inc(s_ld.h, 16)
                s_ld.exp += 16
        @blk.gpsimd
        def _(e):
            e.dma_start(out=wdb[:].rearrange("p (f o) -> p f o", o=128),
                        in_=wchunks(wdownT)).then_inc(s_ld.h, 16)
            s_ld.exp += 16
        ldt = s_ld.exp
        @blk.vector
        def _(e):
            e.memset(ones1f[:], 1.0)
            e.memset(onesr[:], 1.0)
            e.nop().then_inc(s_ve.h, 1)
        s_ve.exp += 1
        vt0 = s_ve.exp
        @blk.tensor
        def _(e):
            e.wait_ge(s_ld.h, ldt)
            e.wait_ge(s_ve.h, vt0)
            for j in range(2):
                mm = e.matmul(x1ps[j][:], onesr[:], wsrow[:, j * 512:(j + 1) * 512],
                              start=True, stop=True)
            s_mm.inc(mm)
        mt0 = s_mm.exp
        @blk.vector
        def _(e):
            e.wait_ge(s_mm.h, mt0)
            for j in range(2):
                mmv = e.tensor_copy(out=wsB[:, j * 512:(j + 1) * 512], in_=x1ps[j][:])
            s_ve.inc(mmv)

    band2 = [band, wdb]
    b_wb = s_wb.exp
    b_ld = s_ld.exp
    b_mm = s_mm.exp
    b_ve = s_ve.exp
    b_sc = s_sc.exp
    b_st = s_st.exp
    nld = 32 + (32 if use_xnt else 0)

    with nc.Block() as blk:
        @blk.gpsimd
        def _(e):
            for o in range(NF):
                if o >= 2:
                    e.wait_ge(s_mm.h, b_mm + 4 * (o - 2) + 2)
                e.dma_start(out=band2[o % 2][:].rearrange("p (f oo) -> p f oo", oo=128),
                            in_=wchunks(Wo)[:, :, o * 128:(o + 1) * 128]).then_inc(s_wb.h, 16)
                e.dma_start(out=eb_o[o % 2][:],
                            in_=ebias[:, o * 128:(o + 1) * 128]).then_inc(s_wb.h, 16)
        s_wb.exp += 32 * NF

        @blk.scalar
        def _(e):
            for o in range(NF):
                if o >= 2:
                    e.wait_ge(s_ve.h, b_ve + 2 * (o - 1))
                for T in range(2):
                    e.dma_start(out=xt[(o % 2) * 2 + T][:],
                                in_=x.rearrange("(f pp) t -> pp f t", pp=128)
                                [:, o, T * 512:(T + 1) * 512]).then_inc(s_ld.h, 16)
                    if use_xnt:
                        e.dma_start(out=xnt[(o % 2) * 2 + T][:],
                                    in_=xnw_in.rearrange("(f pp) t -> pp f t", pp=128)
                                    [:, o, T * 512:(T + 1) * 512]).then_inc(s_ld.h, 16)
                if o >= 1:
                    om = o - 1
                    for T in range(2):
                        e.wait_ge(s_ve.h, b_ve + 2 * om + T + 1)
                        if om >= 1:
                            e.wait_ge(s_mm.h, b_mm + 4 * om + T + 1)
                        s_sc.inc(e.square(out=sqc[T][:], in_=x1b[(om % 2) * 2 + T][:]))
            for T in range(2):
                e.wait_ge(s_ve.h, b_ve + 2 * (NF - 1) + T + 1)
                e.wait_ge(s_mm.h, b_mm + 4 * (NF - 1) + T + 1)
                s_sc.inc(e.square(out=sqc[T][:], in_=x1b[((NF - 1) % 2) * 2 + T][:]))
        s_ld.exp += nld * NF

        @blk.tensor
        def _(e):
            e.wait_ge(s_ld.h, b_ld)
            for o in range(NF):
                e.wait_ge(s_wb.h, b_wb + 32 * (o + 1))
                for T in range(2):
                    if o >= 1:
                        e.wait_ge(s_ve.h, b_ve + 2 * (o - 1) + T + 1)
                    for f in range(NH):
                        e.matmul(x1ps[T][:], band2[o % 2][:, f * 128:(f + 1) * 128],
                                 aor[:, f * 1024 + T * 512: f * 1024 + T * 512 + 512],
                                 start=(f == 0), stop=False)
                    s_mm.inc(e.matmul(x1ps[T][:], eb_o[o % 2][:],
                                      wde[:, T * 512:(T + 1) * 512],
                                      start=False, stop=True))
                for T in range(2):
                    om = o - 1
                    if om >= 0:
                        e.wait_ge(s_ve.h, b_ve + 2 * om + T + 1)
                        e.matmul(st_sum[T], ones1f[:], x1b[(om % 2) * 2 + T][:],
                                 start=(om == 0), stop=False)
                        e.wait_ge(s_sc.h, b_sc + 2 * om + T + 1)
                        s_mm.inc(e.matmul(st_sq[T], ones1f[:], sqc[T][:],
                                          start=(om == 0), stop=False))
            om = NF - 1
            for T in range(2):
                e.wait_ge(s_ve.h, b_ve + 2 * om + T + 1)
                e.matmul(st_sum[T], ones1f[:], x1b[(om % 2) * 2 + T][:],
                         start=False, stop=True)
                e.wait_ge(s_sc.h, b_sc + 2 * (NF - 1) + T + 1)
                s_mm.inc(e.matmul(st_sq[T], ones1f[:], sqc[T][:],
                                  start=False, stop=True))

        @blk.vector
        def _(e):
            for o in range(NF):
                for T in range(2):
                    e.wait_ge(s_mm.h, b_mm + (T + 1 if o == 0 else 4 * o + T - 1))
                    e.wait_ge(s_ld.h, b_ld + nld * (o + 1))
                    if o >= 2:
                        e.wait_ge(s_st.h, b_st + 32 * (o - 1))
                        e.wait_ge(s_sc.h, b_sc + 2 * (o - 2) + T + 1)
                    xnsrc = (xnt[(o % 2) * 2 + T][:] if use_xnt else
                             io["xn_own"][:, o * 1024 + T * 512: o * 1024 + T * 512 + 512])
                    e.tensor_tensor(out=tmp[T][:], in0=xnsrc,
                                    in1=wsB[:, T * 512:(T + 1) * 512], op=ALU.mult)
                    e.tensor_tensor(out=tmp[T][:], in0=tmp[T][:], in1=x1ps[T][:], op=ALU.add)
                    s_ve.inc(e.tensor_tensor(out=x1b[(o % 2) * 2 + T][:], in0=tmp[T][:],
                                             in1=xt[(o % 2) * 2 + T][:], op=ALU.add))

        @blk.sync
        def _(e):
            for o in range(NF):
                for T in range(2):
                    e.wait_ge(s_ve.h, b_ve + 2 * o + T + 1)
                    e.dma_start(out=o_x1.rearrange("(f pp) t -> pp f t", pp=128)
                                [:, o, T * 512:(T + 1) * 512],
                                in_=x1b[(o % 2) * 2 + T][:]).then_inc(s_st.h, 16)
                    s_st.exp += 16

    with nc.Block() as blk:
        @blk.sync
        def _(e):
            e.wait_ge(s_st.h, s_st.exp)
        @blk.gpsimd
        def _(e):
            e.dma_start(out=wdb[:].rearrange("p (f o) -> p f o", o=128),
                        in_=wchunks(wdownT)).then_inc(s_wb.h, 16)
        s_wb.exp += 16
        wdbt = s_wb.exp
        for T in range(2):
            @blk.vector
            def _(e, T=T):
                s_ve.fence(e, e.tensor_scalar(out=mrow[T][:], in0=st_sum[T], scalar1=1.0 / H,
                                              scalar2=None, op0=ALU.mult))
                s_ve.fence(e, e.tensor_scalar(out=stat[:], in0=st_sq[T], scalar1=1.0 / H,
                                              scalar2=None, op0=ALU.mult))
                s_ve.fence(e, e.tensor_tensor(out=rrow[T][:], in0=mrow[T][:], in1=mrow[T][:],
                                              op=ALU.mult))
                s_ve.fence(e, e.tensor_tensor(out=stat[:], in0=stat[:], in1=rrow[T][:],
                                              op=ALU.subtract))
                s_ve.inc(e.tensor_scalar(out=stat[:], in0=stat[:], scalar1=1e-5,
                                         scalar2=None, op0=ALU.add))
            vt = s_ve.exp
            @blk.scalar
            def _(e, T=T):
                e.wait_ge(s_ve.h, vt)
                s_sc.fence(e, e.activation(out=stat[:], in_=stat[:], func=AF.Ln))
                s_sc.inc(e.activation(out=rrow[T][:], in_=stat[:], func=AF.Exp, scale=-0.5))
            st = s_sc.exp
            @blk.tensor
            def _(e, T=T):
                e.wait_ge(s_sc.h, st)
                e.matmul(x1ps[0][:], onesr[:], mrow[T][:], start=True, stop=True)
                s_mm.inc(e.matmul(x1ps[1][:], onesr[:], rrow[T][:], start=True, stop=True))
            mt = s_mm.exp
            @blk.vector
            def _(e, T=T):
                e.wait_ge(s_mm.h, mt)
                e.tensor_copy(out=mB[T][:], in_=x1ps[0][:])
                s_ve.inc(e.tensor_copy(out=rB[T][:], in_=x1ps[1][:]))

    for o in range(NF):
        with nc.Block() as blk:
            @blk.sync
            def _(e):
                for T in range(2):
                    e.dma_start(out=x1t2[T][:],
                                in_=o_x1.rearrange("(f pp) t -> pp f t", pp=128)
                                [:, o, T * 512:(T + 1) * 512]).then_inc(s_ld.h, 16)
            s_ld.exp += 32
            ldo = s_ld.exp
            for T in range(2):
                @blk.vector
                def _(e, T=T):
                    e.wait_ge(s_ld.h, ldo)
                    e.tensor_tensor(out=x1t2[T][:], in0=x1t2[T][:], in1=mB[T][:],
                                    op=ALU.subtract)
                    s_ve.inc(e.tensor_tensor(out=xn1t[T][:], in0=x1t2[T][:], in1=rB[T][:],
                                             op=ALU.mult))
                vt = s_ve.exp
                @blk.tensor
                def _(e, T=T, o=o):
                    e.wait_ge(s_ve.h, vt)
                    e.wait_ge(s_wb.h, wdbt)
                    s_mm.inc(e.matmul(xmid_ps[:, T * 512:(T + 1) * 512],
                                      wdb[:, o * 128:(o + 1) * 128], xn1t[T][:],
                                      start=(o == 0), stop=(o == NF - 1)))
                @blk.sync
                def _(e, T=T, o=o):
                    e.wait_ge(s_ve.h, vt)
                    e.dma_start(out=o_xn1.rearrange("(f pp) t -> pp f t", pp=128)
                                [:, o, T * 512:(T + 1) * 512],
                                in_=xn1t[T][:]).then_inc(s_st.h, 16)
                s_st.exp += 16

    with nc.Block() as blk:
        @blk.scalar
        def _(e):
            s_sc.inc(e.activation(out=xmid_sb[:], in_=xmid_ps[:],
                                  func=AF.Gelu, bias=bdn[:]))
        st = s_sc.exp
        @blk.sync
        def _(e):
            e.wait_ge(s_sc.h, st)
            e.dma_start(out=o_xmid[:, :], in_=xmid_sb[:]).then_inc(s_st.h, 16)
            s_st.exp += 16
            e.wait_ge(s_st.h, s_st.exp)

    if standalone:
        p.close()
        return nc
    return None



_D_DEBUG = False


def build_phase_D(p=None, io=None):
    standalone = p is None
    if standalone:
        p = P()
        io = {}
    nc = p.nc
    x1 = io.get("x1") or p.inp("x1_fm", (H, TPC), F32)
    xn1 = io.get("xn1") or p.inp("xn1", (H, TPC), BF16)
    xm0 = io.get("xm0") or p.inp("xmid_l0", (128, TPC), BF16)
    xm1 = io.get("xm1") or p.inp("xmid_l1", (128, TPC), BF16)
    gate1 = io.get("gate1") or p.inp("gate1", (1, 1), F32)
    indsel = io.get("indsel") or p.inp("indsel", (128, 2), F32)
    inwT = io.get("inwT") or p.inp("inwT", (128, 552), F32)
    convw_x = io.get("convw_x") or p.inp("convw_x", (128, 14), F32)
    convw_bc = io.get("convw_bc") or p.inp("convw_bc", (16, 14), F32)
    convb_x = io.get("convb_x") or p.inp("convb_x", (128, 2), F32)
    convb_bc = io.get("convb_bc") or p.inp("convb_bc", (16, 2), F32)
    dtb = io.get("dtb") or p.inp("dtb", (4, 1), F32)
    fcdT = io.get("fcdT") or p.inp("fcdT", (128, 8), F32)
    Dv = io.get("Dv") or p.inp("Dv", (4, 1), F32)
    normw = io.get("normw") or p.inp("normw", (128, 2), F32)
    outwT = io.get("outwT") or p.inp("outwT", (128, 256), F32)
    ind4 = io.get("ind4") or p.inp("ind4", (4, 256), F32)
    WbT = io.get("WbT") or p.inp("WbT", (H, H), F32)
    bb32 = io.get("bb32") or p.inp("bb32", (128, NF), F32)
    bu32 = io.get("bu32") or p.inp("bu32", (128, NF), F32)
    wr2T = io.get("wr2T") or p.inp("wr2T", (H, 8), F32)
    A2w = io.get("A2") or p.inp("A2", (H, 128), F32)
    Bmw = io.get("Bm") or p.inp("Bm", (128, H), F32)
    wupT = io.get("wupT") or p.inp("wupT", (128, H), F32)
    ind16 = io.get("ind16") or p.inp("ind16", (8, 128), BF16)
    identw = io.get("ident") or p.inp("ident", (128, 128), BF16)
    o_out = io.get("o_out")
    if o_out is None:
        o_out = p.out("out_fm", (H, TPC), F32)
    if _D_DEBUG:
        o_h1 = p.out("dbg_h1", (128, 2048), BF16)
        o_h2 = p.out("dbg_h2", (128, 2048), BF16)
        o_hfm = p.out("dbg_hfm", (128, TPC), BF16)
        o_cwT = p.out("dbg_cwT", (8, TPC), BF16)
        o_hw = p.out("dbg_hw", (128, TPC), BF16)

    xn1r = p.sb([128, NF * TPC], BF16)
    u = p.sb([128, 2048], BF16)
    h1 = p.sb([128, 2048], BF16)
    inw_b = p.sb([128, 552], BF16)
    fcd_b = p.sb([128, 8], BF16)
    outw_b = p.sb([128, 256], BF16)
    ind4_s = p.sb([4, 256], F32)
    ind16_s = p.sb([8, 128], BF16)
    ident = p.sb([128, 128], BF16)
    cwx = p.sb([128, 14], F32)
    cwbc = p.sb([16, 14], F32)
    cbx = p.sb([128, 2], F32)
    cbbc = p.sb([16, 2], F32)
    dtb_f = p.sb([128, 1], F32)
    dtb_s = dtb_f[32:36, :]
    dv_s = p.sb([4, 1], F32)
    nw_s = p.sb([128, 2], F32)
    sel_s = p.sb([128, 2], F32)
    onesr = p.sb([1, 128], F32)
    ones1f = p.sb([128, 1], F32)
    ones8 = p.sb([128, 8], F32)
    onesbig = p.sb([128, 2048], F32)
    zs = p.sb([128, 4096], F32)
    hf = zs[:, 0:2048]
    xpre = p.sb([128, 4096], F32)
    xs_b = p.sb([128, 4096], BF16)
    yn = p.sb([128, 4096], BF16)
    _arA = p.sb([128, 4096], F32)
    Fl = _arA[0:4, 0:2048]
    g4 = _arA[0:4, 2048:4096]
    dts = _arA[32:36, 0:4096]
    _arB = p.sb([128, 4096], F32)
    Bpre = _arB[0:16, 0:2048]
    Cpre = _arB[0:16, 2048:4096]
    spf = _arB[32:36, 0:4096]
    _arC = p.sb([128, 4096], F32)
    bc_s = _arC[0:1, 0:2048]
    rr = _arC[0:1, 2048:4096]
    bc4 = _arC[32:36, 0:2048]
    tmpB = p.sb([128, 1024], F32)
    small = tmpB[0:16, 0:1024]
    a1B = p.sb([128, 1], F32)
    a2B = p.sb([128, 1], F32)
    gsc = p.sb([1, 4], F32)
    hfm = p.sb([128, TPC], BF16)
    wr2_b = p.sb([128, NF * 8], BF16)
    Bm_o = [p.sb([128, 128], BF16) for _ in range(2)]
    wup_o = [p.sb([128, 128], BF16) for _ in range(2)]
    band = p.sb([128, NF * 128], BF16)
    A2_b = band
    probs = p.sb([128, 8], F32)
    work8 = p.sb([128, 8], F32)
    eq1 = p.sb([128, 8], F32)
    eq2 = p.sb([128, 8], F32)
    m1 = p.sb([128, 1], F32)
    m2 = p.sb([128, 1], F32)
    cw = p.sb([128, 8], F32)
    cwb = p.sb([128, 8], BF16)
    cwT = p.sb([8, TPC], BF16)
    cwBt = tmpB[:, 0:512]
    hw = p.sb([128, TPC], BF16)
    bsum = p.sb([128, NF], F32)
    bsum2 = p.sb([128, NF], F32)
    x1t = [xpre[:, 0:512], xpre[:, 512:1024]]
    x2t = x1t

    s_ld = p.nsem("ld")
    s_wb = p.nsem("wb")
    s_mm = p.nsem("mm")
    s_ve = p.nsem("ve")
    s_sc = p.nsem("sc")
    s_st = p.nsem("st")

    psA = p.ps([128, 1024])
    psB = p.ps([128, 1024])
    psC = p.ps([128, 512])
    psD = p.ps([128, 512])
    psE = p.ps([128, 512])
    psT = p.ps([128, 1024], BF16)

    def ts_ap(e, out, in0, scal, op0, in1=None, op1=None, base=0):
        if in1 is None:
            in1 = onesbig[base:base + in0.shape[0], 0:in0.shape[-1]]
            op1 = ALU.mult
        return e.scalar_tensor_tensor(out=out, in0=in0, scalar=scal, in1=in1,
                                      op0=op0, op1=op1)

    with nc.Block() as blk:
        @blk.sync
        def _(e):
            e.dma_start(out=xn1r[:].rearrange("p (f t) -> p f t", f=NF),
                        in_=xn1.rearrange("(f p) t -> p f t", p=128)).then_inc(s_ld.h, 16)
            e.dma_start(out=u[:, 0:1024], in_=xm0[:, :]).then_inc(s_ld.h, 16)
            e.dma_start(out=u[:, 1024:2048], in_=xm1[:, :]).then_inc(s_ld.h, 16)
            for ap_in, ap_sb in [(convw_x, cwx[:]), (convw_bc, cwbc[:]), (convb_x, cbx[:]),
                                 (convb_bc, cbbc[:]), (dtb, dtb_s), (Dv, dv_s[:]),
                                 (normw, nw_s[:]), (indsel, sel_s[:]), (gate1, gsc[0:1, 0:1]),
                                 (ind4, ind4_s[:]), (ind16, ind16_s[:]), (identw, ident[:]),
                                 (bb32, bsum[:]), (bu32, bsum2[:])]:
                e.dma_start(out=ap_sb, in_=ap_in[:, :]).then_inc(s_ld.h, 16)
        @blk.gpsimd
        def _(e):
            e.dma_start(out=inw_b[:], in_=inwT[:, :]).then_inc(s_ld.h, 16)
            e.dma_start(out=fcd_b[:], in_=fcdT[:, :]).then_inc(s_ld.h, 16)
            e.dma_start(out=outw_b[:], in_=outwT[:, :]).then_inc(s_ld.h, 16)
            e.dma_start(out=wr2_b[:].rearrange("p (f o) -> p f o", o=8),
                        in_=wchunks(wr2T)).then_inc(s_ld.h, 16)
            e.dma_start(out=A2_b[:].rearrange("p (f o) -> p f o", o=128),
                        in_=wchunks(A2w)).then_inc(s_ld.h, 16)
        s_ld.exp += 16 * 22
        ldt = s_ld.exp
        @blk.vector
        def _(e):
            e.memset(onesr[:], 1.0)
            e.memset(ones1f[:], 1.0)
            e.memset(ones8[:], 1.0)
            e.memset(onesbig[:], 1.0)
            e.wait_ge(s_ld.h, ldt)
            s_ve.inc(e.tensor_tensor(out=bsum[:], in0=bsum[:], in1=bsum2[:], op=ALU.add))
        @blk.scalar
        def _(e):
            e.wait_ge(s_ld.h, ldt)
            s_sc.inc(e.activation(out=gsc[0:1, 1:2], in_=gsc[0:1, 0:1],
                                  func=AF.Exp, scale=-1.0))
        st = s_sc.exp
        @blk.vector
        def _(e):
            e.wait_ge(s_sc.h, st)
            s_ve.fence(e, e.tensor_scalar(out=gsc[0:1, 1:2], in0=gsc[0:1, 1:2],
                                          scalar1=1.0, scalar2=None, op0=ALU.add))
            s_ve.fence(e, e.reciprocal(out=gsc[0:1, 2:3], in_=gsc[0:1, 1:2]))
            s_ve.fence(e, e.tensor_scalar(out=gsc[0:1, 0:1], in0=gsc[0:1, 2:3],
                                          scalar1=1.0, scalar2=None, op0=ALU.add))
            s_ve.inc(e.tensor_scalar(out=gsc[0:1, 1:2], in0=gsc[0:1, 2:3],
                                     scalar1=-1.0, scalar2=1.0, op0=ALU.mult, op1=ALU.add))
        vt2 = s_ve.exp
        @blk.tensor
        def _(e):
            e.wait_ge(s_ve.h, vt2)
            e.matmul(psC[:, 0:1], onesr[:], gsc[0:1, 0:1], start=True, stop=True)
            s_mm.inc(e.matmul(psC[:, 1:2], onesr[:], gsc[0:1, 1:2], start=True, stop=True))
        mt = s_mm.exp
        @blk.vector
        def _(e):
            e.wait_ge(s_mm.h, mt)
            e.tensor_copy(out=a1B[:], in_=psC[:, 0:1])
            s_ve.inc(e.tensor_copy(out=a2B[:], in_=psC[:, 1:2]))

    def hydra(uin, uout):
        with nc.Block() as blk:
            base_mm = s_mm.exp
            base_ve = s_ve.exp
            @blk.tensor
            def _(e):
                for nt in range(4):
                    if nt:
                        e.wait_ge(s_ve.h, base_ve + nt)
                    rhs = uin[:, nt * 512:(nt + 1) * 512]
                    e.matmul(psA[:, 0:512], inw_b[:, 0:128], rhs, start=True, stop=True)
                    e.matmul(psA[:, 512:1024], inw_b[:, 128:256], rhs, start=True, stop=True)
                    e.matmul(psB[:, 0:512], inw_b[:, 256:384], rhs, start=True, stop=True)
                    e.matmul(psB[:, 512:1024], inw_b[:, 384:512], rhs, start=True, stop=True)
                    e.matmul(psC[0:16, :], inw_b[:, 512:528], rhs, start=True, stop=True)
                    e.matmul(psC[32:48, :], inw_b[:, 528:544], rhs, start=True, stop=True)
                    e.matmul(psD[0:4, :], inw_b[:, 544:548], rhs, start=True, stop=True)
                    s_mm.inc(e.matmul(psD[32:36, :], inw_b[:, 548:552], rhs,
                                      start=True, stop=True))
            s_mm.exp = base_mm + 4
            @blk.vector
            def _(e):
                for nt in range(4):
                    e.wait_ge(s_mm.h, base_mm + nt + 1)
                    sl = slice(nt * 512, (nt + 1) * 512)
                    e.tensor_copy(out=zs[:, sl], in_=psA[:, 0:512])
                    e.tensor_copy(out=zs[:, 2048 + nt * 512:2048 + (nt + 1) * 512],
                                  in_=psA[:, 512:1024])
                    e.tensor_copy(out=xpre[:, sl], in_=psB[:, 0:512])
                    e.tensor_copy(out=xpre[:, 2048 + nt * 512:2048 + (nt + 1) * 512],
                                  in_=psB[:, 512:1024])
                    e.tensor_copy(out=Bpre[:, sl], in_=psC[0:16, :])
                    e.tensor_copy(out=Cpre[:, sl], in_=psC[32:48, :])
                    e.tensor_copy(out=dts[:, sl], in_=psD[0:4, :])
                    s_ve.inc(e.tensor_copy(
                        out=dts[:, 2048 + nt * 512:2048 + (nt + 1) * 512],
                        in_=psD[32:36, :]))
            s_ve.exp = base_ve + 4

        with nc.Block() as blk:
            @blk.vector
            def _(e):
                for c in range(2):
                    base = c * 2048
                    w5 = cwx[:, 7 * c + 5:7 * c + 6]
                    w6 = cwx[:, 7 * c + 6:7 * c + 7]
                    s_ve.fence(e, ts_ap(e, tmpB[:, 0:1024], xpre[:, base:base + 1024],
                                        w5, ALU.mult))
                    s_ve.fence(e, ts_ap(e, xpre[:, base:base + 1024],
                                        xpre[:, base:base + 1024], w6, ALU.mult))
                    s_ve.fence(e, ts_ap(e, xpre[:, base + 1024:base + 2048],
                                        xpre[:, base + 1024:base + 2048], w6, ALU.mult,
                                        in1=tmpB[:, 0:1024], op1=ALU.add))
                for (pre, w5_, w6_) in [(Bpre, cwbc[:, 5:6], cwbc[:, 6:7]),
                                        (Cpre, cwbc[:, 12:13], cwbc[:, 13:14])]:
                    s_ve.fence(e, ts_ap(e, small, pre[:, 0:1024], w5_, ALU.mult))
                    s_ve.fence(e, ts_ap(e, pre[:, 0:1024], pre[:, 0:1024], w6_, ALU.mult))
                    s_ve.fence(e, ts_ap(e, pre[:, 1024:2048], pre[:, 1024:2048], w6_, ALU.mult,
                                        in1=small, op1=ALU.add))
                s_ve.fence(e, ts_ap(e, dts[:, 0:2048], dts[:, 0:2048], dtb_s, ALU.add, base=32))
                s_ve.inc(ts_ap(e, dts[:, 2048:4096], dts[:, 2048:4096], dtb_s, ALU.add, base=32))
            vt = s_ve.exp
            @blk.scalar
            def _(e):
                e.wait_ge(s_ve.h, vt)
                for c in range(2):
                    base = c * 2048
                    e.activation(out=xs_b[:, base:base + 1024], in_=xpre[:, base:base + 1024],
                                 func=AF.Silu, bias=cbx[:, c:c + 1])
                    e.activation(out=xs_b[:, base + 1024:base + 2048],
                                 in_=xpre[:, base + 1024:base + 2048],
                                 func=AF.Silu, bias=cbx[:, c:c + 1])
                e.activation(out=Bpre[:, :], in_=Bpre[:, :], func=AF.Silu, bias=cbbc[:, 0:1])
                e.activation(out=Cpre[:, :], in_=Cpre[:, :], func=AF.Silu, bias=cbbc[:, 1:2])
                s_sc.fence(e, e.activation(out=spf[:, :], in_=dts[:, :], func=AF.Abs))
                s_sc.inc(e.activation(out=spf[:, :], in_=spf[:, :], func=AF.Exp, scale=-1.0))
            st = s_sc.exp
            @blk.vector
            def _(e):
                e.wait_ge(s_sc.h, st)
                s_ve.inc(e.tensor_scalar(out=spf[:, :], in0=spf[:, :], scalar1=1.0,
                                         scalar2=None, op0=ALU.add))
            vt3 = s_ve.exp
            @blk.scalar
            def _(e):
                e.wait_ge(s_ve.h, vt3)
                s_sc.inc(e.activation(out=spf[:, :], in_=spf[:, :], func=AF.Ln))
            st2 = s_sc.exp
            @blk.vector
            def _(e):
                e.wait_ge(s_sc.h, st2)
                s_ve.fence(e, e.tensor_scalar(out=dts[:, :], in0=dts[:, :], scalar1=0.0,
                                              scalar2=None, op0=ALU.max))
                s_ve.fence(e, e.tensor_tensor(out=dts[:, :], in0=dts[:, :], in1=spf[:, :], op=ALU.add))
                s_ve.inc(e.tensor_tensor(out=Bpre[:, :], in0=Bpre[:, :], in1=Cpre[:, :],
                                          op=ALU.mult))
            vt4 = s_ve.exp
            base_mm = s_mm.exp
            base_ve = s_ve.exp
            @blk.tensor
            def _(e):
                e.wait_ge(s_ve.h, vt4)
                for nt in range(4):
                    if nt:
                        e.wait_ge(s_ve.h, base_ve + nt)
                    e.matmul(psE[0:1, :], ones1f[0:16, :], Bpre[:, nt * 512:(nt + 1) * 512],
                             start=True, stop=True)
                    for c in range(2):
                        mm = e.matmul(psD[0:4, :], fcd_b[:, 4 * c:4 * c + 4],
                                      xs_b[:, c * 2048 + nt * 512:c * 2048 + (nt + 1) * 512],
                                      start=(c == 0), stop=(c == 1))
                    s_mm.inc(mm)
            s_mm.exp = base_mm + 4
            @blk.vector
            def _(e):
                for nt in range(4):
                    e.wait_ge(s_mm.h, base_mm + nt + 1)
                    e.tensor_copy(out=bc_s[:, nt * 512:(nt + 1) * 512], in_=psE[0:1, :])
                    s_ve.inc(e.tensor_copy(out=g4[:, nt * 512:(nt + 1) * 512],
                                           in_=psD[0:4, :]))
            s_ve.exp = base_ve + 4

        with nc.Block() as blk:
            base_mm = s_mm.exp
            base_ve = s_ve.exp
            @blk.tensor
            def _(e):
                for j in range(4):
                    if j:
                        e.wait_ge(s_ve.h, base_ve + j)
                    s_mm.inc(e.matmul(psD[0:4, :], onesr[0:1, 0:4],
                                      bc_s[:, j * 512:(j + 1) * 512], start=True, stop=True))
            s_mm.exp = base_mm + 4
            @blk.vector
            def _(e):
                for j in range(4):
                    e.wait_ge(s_mm.h, base_mm + j + 1)
                    s_ve.inc(e.tensor_copy(out=bc4[:, j * 512:(j + 1) * 512],
                                           in_=psD[0:4, :]))
            s_ve.exp = base_ve + 4
            @blk.vector
            def _(e):
                s_ve.fence(e, e.tensor_tensor(out=Fl[:, 0:1024], in0=dts[:, 3072:4096],
                                              in1=bc4[:, 1024:2048], op=ALU.mult))
                s_ve.fence(e, e.tensor_tensor(out=Fl[:, 1024:2048], in0=dts[:, 0:1024],
                                              in1=bc4[:, 0:1024], op=ALU.mult))
                s_ve.inc(ts_ap(e, g4[:, :], g4[:, :], dv_s[:], ALU.add))
            vt6 = s_ve.exp
            base_mm = s_mm.exp
            base_ve = s_ve.exp
            @blk.tensor
            def _(e):
                e.wait_ge(s_ve.h, vt6)
                k = 0
                for c in range(2):
                    for j in range(4):
                        if k:
                            e.wait_ge(s_ve.h, base_ve + k)
                        e.matmul(psA[:, 0:512], ind4_s[:, c * 128:(c + 1) * 128],
                                 Fl[:, (j % 2) * 512 + (j // 2) * 1024:
                                    (j % 2) * 512 + (j // 2) * 1024 + 512],
                                 start=True, stop=True)
                        s_mm.inc(e.matmul(psB[:, 0:512], ind4_s[:, c * 128:(c + 1) * 128],
                                          g4[:, j * 512:(j + 1) * 512],
                                          start=True, stop=True))
                        k += 1
            s_mm.exp = base_mm + 8
            @blk.vector
            def _(e):
                k = 0
                for c in range(2):
                    for j in range(4):
                        e.wait_ge(s_mm.h, base_mm + k + 1)
                        base = c * 2048
                        jj = j * 512
                        swapped = (jj + 1024) % 2048
                        e.tensor_tensor(out=tmpB[:, 0:512], in0=psA[:, 0:512],
                                        in1=xs_b[:, base + swapped:base + swapped + 512],
                                        op=ALU.mult)
                        e.tensor_tensor(out=tmpB[:, 512:1024], in0=psB[:, 0:512],
                                        in1=xs_b[:, base + jj:base + jj + 512],
                                        op=ALU.mult)
                        s_ve.inc(e.tensor_tensor(out=xpre[:, base + jj:base + jj + 512],
                                                 in0=tmpB[:, 0:512], in1=tmpB[:, 512:1024],
                                                 op=ALU.add))
                        k += 1
            s_ve.exp = base_ve + 8
            vt7 = s_ve.exp
            @blk.scalar
            def _(e):
                e.wait_ge(s_ve.h, vt7)
                s_sc.inc(e.activation(out=zs[:], in_=zs[:], func=AF.Silu))
            st3 = s_sc.exp
            @blk.vector
            def _(e):
                e.wait_ge(s_sc.h, st3)
                s_ve.inc(e.tensor_tensor(out=xpre[:], in0=xpre[:], in1=zs[:], op=ALU.mult))
            vt8 = s_ve.exp
            @blk.scalar
            def _(e):
                e.wait_ge(s_ve.h, vt8)
                s_sc.inc(e.square(out=zs[:], in_=xpre[:]))
            st4 = s_sc.exp
            base_mm = s_mm.exp
            base_ve = s_ve.exp
            @blk.tensor
            def _(e):
                e.wait_ge(s_sc.h, st4)
                for nt in range(4):
                    if nt:
                        e.wait_ge(s_ve.h, base_ve + nt)
                    for c in range(2):
                        mm = e.matmul(psE[0:1, :], ones1f[:],
                                      zs[:, c * 2048 + nt * 512:c * 2048 + (nt + 1) * 512],
                                      start=(c == 0), stop=(c == 1))
                    s_mm.inc(mm)
            s_mm.exp = base_mm + 4
            @blk.vector
            def _(e):
                for nt in range(4):
                    e.wait_ge(s_mm.h, base_mm + nt + 1)
                    s_ve.inc(e.tensor_copy(out=rr[:, nt * 512:(nt + 1) * 512],
                                           in_=psE[0:1, :]))
            s_ve.exp = base_ve + 4
            @blk.vector
            def _(e):
                s_ve.inc(e.tensor_scalar(out=rr[:, :], in0=rr[:, :], scalar1=1.0 / 256.0,
                                         scalar2=1e-5, op0=ALU.mult, op1=ALU.add))
            vt9 = s_ve.exp
            @blk.scalar
            def _(e):
                e.wait_ge(s_ve.h, vt9)
                s_sc.fence(e, e.activation(out=rr[:, :], in_=rr[:, :], func=AF.Ln))
                s_sc.inc(e.activation(out=rr[:, :], in_=rr[:, :], func=AF.Exp, scale=-0.5))
            st5 = s_sc.exp
            base_mm = s_mm.exp
            base_ve = s_ve.exp
            @blk.tensor
            def _(e):
                e.wait_ge(s_sc.h, st5)
                for j in range(4):
                    if j:
                        e.wait_ge(s_ve.h, base_ve + j)
                    s_mm.inc(e.matmul(psB[:, 0:512], onesr[:], rr[:, j * 512:(j + 1) * 512],
                                      start=True, stop=True))
            s_mm.exp = base_mm + 4
            @blk.vector
            def _(e):
                for j in range(4):
                    e.wait_ge(s_mm.h, base_mm + j + 1)
                    for c in range(2):
                        mmv = ts_ap(e, yn[:, c * 2048 + j * 512:c * 2048 + (j + 1) * 512],
                                    xpre[:, c * 2048 + j * 512:c * 2048 + (j + 1) * 512],
                                    nw_s[:, c:c + 1], ALU.mult,
                                    in1=psB[:, 0:512], op1=ALU.mult)
                    s_ve.inc(mmv)
            s_ve.exp = base_ve + 4
        with nc.Block() as blk:
            base_mm = s_mm.exp
            base_ve = s_ve.exp
            @blk.tensor
            def _(e):
                for nt in range(4):
                    if nt:
                        e.wait_ge(s_ve.h, base_ve + nt)
                    for c in range(2):
                        mm = e.matmul(psA[:, 0:512], outw_b[:, c * 128:(c + 1) * 128],
                                      yn[:, c * 2048 + nt * 512:c * 2048 + (nt + 1) * 512],
                                      start=(c == 0), stop=(c == 1))
                    s_mm.inc(mm)
            s_mm.exp = base_mm + 4
            @blk.vector
            def _(e):
                for nt in range(4):
                    e.wait_ge(s_mm.h, base_mm + nt + 1)
                    s_ve.inc(e.tensor_copy(out=uout[:, nt * 512:(nt + 1) * 512],
                                           in_=psA[:, 0:512]))
            s_ve.exp = base_ve + 4

    hydra(u, h1)
    if _D_DEBUG:
        with nc.Block() as blk:
            @blk.sync
            def _(e):
                e.dma_start(out=o_h1[:, :], in_=h1[:]).then_inc(s_st.h, 16)
                s_st.exp += 16
                e.wait_ge(s_st.h, s_st.exp)
    hydra(h1, h1)
    if _D_DEBUG:
        with nc.Block() as blk:
            @blk.sync
            def _(e):
                e.dma_start(out=o_h2[:, :], in_=h1[:]).then_inc(s_st.h, 16)
                s_st.exp += 16
                e.wait_ge(s_st.h, s_st.exp)

    with nc.Block() as blk:
        @blk.vector
        def _(e):
            s_ve.fence(e, ts_ap(e, hf[:, :], h1[:], a2B[:], ALU.mult))
            s_ve.fence(e, ts_ap(e, hf[:, :], u[:], a1B[:], ALU.mult, in1=hf[:, :], op1=ALU.add))
            s_ve.fence(e, ts_ap(e, tmpB[:, 0:1024], hf[:, 1024:2048], sel_s[:, 1:2], ALU.mult))
            s_ve.inc(ts_ap(e, hfm[:], hf[:, 0:1024], sel_s[:, 0:1], ALU.mult,
                           in1=tmpB[:, 0:1024], op1=ALU.add))

    for ts in range(8):
        with nc.Block() as blk:
            @blk.tensor
            def _(e):
                for f in range(NF):
                    mm = e.matmul(psC[:, 0:8],
                                  xn1r[:, f * 1024 + ts * 128: f * 1024 + ts * 128 + 128],
                                  wr2_b[:, f * 8:(f + 1) * 8],
                                  start=(f == 0), stop=(f == NF - 1))
                s_mm.inc(mm)
            mt = s_mm.exp
            @blk.vector
            def _(e):
                e.wait_ge(s_mm.h, mt)
                s_ve.fence(e, e.tensor_copy(out=probs[:], in_=psC[:, 0:8]))
                s_ve.inc(e.reduce_max(out=m1[:], in_=probs[:], axis=AX.X, negate=True))
            vt = s_ve.exp
            @blk.scalar
            def _(e):
                e.wait_ge(s_ve.h, vt)
                s_sc.inc(e.activation(out=probs[:], in_=probs[:], func=AF.Exp, bias=m1[:]))
            st = s_sc.exp
            @blk.vector
            def _(e):
                e.wait_ge(s_sc.h, st)
                s_ve.fence(e, e.reduce_sum(out=m2[:], in_=probs[:], axis=AX.X))
                s_ve.fence(e, e.reciprocal(out=m2[:], in_=m2[:]))
                s_ve.fence(e, ts_ap(e, probs[:], probs[:], m2[:], ALU.mult,
                                    in1=ones8[:], op1=ALU.mult))
                s_ve.fence(e, e.reduce_max(out=m1[:], in_=probs[:], axis=AX.X))
                s_ve.fence(e, ts_ap(e, eq1[:], probs[:], m1[:], ALU.is_ge,
                                    in1=ones8[:], op1=ALU.mult))
                s_ve.fence(e, ts_ap(e, work8[:], eq1[:], -1e30, ALU.mult,
                                    in1=probs[:], op1=ALU.add))
                s_ve.fence(e, e.reduce_max(out=m2[:], in_=work8[:], axis=AX.X))
                s_ve.fence(e, ts_ap(e, eq2[:], work8[:], m2[:], ALU.is_ge,
                                    in1=ones8[:], op1=ALU.mult))
                s_ve.fence(e, e.tensor_tensor(out=eq1[:], in0=eq1[:], in1=eq2[:], op=ALU.add))
                s_ve.fence(e, e.tensor_tensor(out=m1[:], in0=m1[:], in1=m2[:], op=ALU.add))
                s_ve.fence(e, e.reciprocal(out=m1[:], in_=m1[:]))
                s_ve.fence(e, e.tensor_scalar(out=m1[:], in0=m1[:], scalar1=2.0,
                                              scalar2=None, op0=ALU.mult))
                s_ve.fence(e, ts_ap(e, cw[:], eq1[:], m1[:], ALU.mult,
                                    in1=probs[:], op1=ALU.mult))
                s_ve.inc(e.tensor_copy(out=cwb[:], in_=cw[:]))
            vt2 = s_ve.exp
            @blk.tensor
            def _(e):
                e.wait_ge(s_ve.h, vt2)
                s_mm.inc(e.transpose(psT[0:8, 0:128], cwb[:], ident[:]))
            mt2 = s_mm.exp
            @blk.vector
            def _(e):
                e.wait_ge(s_mm.h, mt2)
                s_ve.inc(e.tensor_copy(out=cwT[:, ts * 128:(ts + 1) * 128],
                                       in_=psT[0:8, 0:128]))

    for T in range(2):
        with nc.Block() as blk:
            @blk.tensor
            def _(e, T=T):
                for f in range(NF):
                    mm = e.matmul(psA[:, 0:512],
                                  A2_b[:, f * 128:(f + 1) * 128],
                                  xn1r[:, f * 1024 + T * 512: f * 1024 + T * 512 + 512],
                                  start=(f == 0), stop=(f == NF - 1))
                s_mm.inc(mm)
                s_mm.inc(e.matmul(psB[:, 0:512], ind16_s[:],
                                  cwT[:, T * 512:(T + 1) * 512], start=True, stop=True))
            mt = s_mm.exp
            @blk.vector
            def _(e, T=T):
                e.wait_ge(s_mm.h, mt)
                s_ve.fence(e, e.tensor_copy(out=cwBt, in_=psB[:, 0:512]))
                s_ve.inc(e.tensor_tensor(out=hw[:, T * 512:(T + 1) * 512],
                                         in0=psA[:, 0:512], in1=cwBt, op=ALU.mult))

    if _D_DEBUG:
        with nc.Block() as blk:
            @blk.sync
            def _(e):
                e.dma_start(out=o_hfm[:, :], in_=hfm[:]).then_inc(s_st.h, 16)
                e.dma_start(out=o_cwT[:, :], in_=cwT[:]).then_inc(s_st.h, 16)
                e.dma_start(out=o_hw[:, :], in_=hw[:]).then_inc(s_st.h, 16)
                s_st.exp += 48
                e.wait_ge(s_st.h, s_st.exp)

    s_l2d = p.nsem("l2")
    wb32 = [_arA[:, 0:2048], _arA[:, 2048:4096]]
    bandD = [band, yn]

    b_mm, b_ve, b_st, b_wb = s_mm.exp, s_ve.exp, s_st.exp, s_wb.exp
    b_ld, b_l2 = s_ld.exp, s_l2d.exp
    EVT, CA, EVC, X1L, W0L, W1L, STOd = ({} for _ in range(7))
    mmc = b_mm
    for o in range(NF):
        for T in range(2):
            mmc += 1; EVT[(o, T)] = mmc
    vec = b_ve
    for o in range(NF + 1):
        if o >= 1:
            for T in range(2):
                vec += 1; EVC[(o - 1, T)] = vec
        if o < NF:
            vec += 1; CA[(o, 0)] = vec
            vec += 1; CA[(o, 1)] = vec
    lde = b_ld
    for o in range(NF):
        lde += 16; X1L[(o, 0)] = lde
        lde += 16; X1L[(o, 1)] = lde
        lde += 16; W0L[o] = lde
    ldo = b_l2
    for o in range(NF):
        ldo += 16; W1L[o] = ldo
    stc = b_st
    for o in range(NF):
        for T in range(2):
            stc += 16; STOd[(o, T)] = stc
    WBD = [b_wb + 32 * (o + 1) for o in range(NF)]

    with nc.Block() as blk:
        @blk.gpsimd
        def _(e):
            for o in range(NF + 1):
                if o < NF:
                    if o >= 1:
                        e.wait_ge(s_mm.h, EVT[(o - 1, 1)])
                    e.dma_start(out=Bm_o[o % 2][:],
                                in_=Bmw[:, o * 128:(o + 1) * 128]).then_inc(s_wb.h, 16)
                    e.dma_start(out=wup_o[o % 2][:],
                                in_=wupT[:, o * 128:(o + 1) * 128]).then_inc(s_wb.h, 16)
                if o >= 1:
                    for T in range(2):
                        e.wait_ge(s_ve.h, EVC[(o - 1, T)])
                        e.dma_start(out=o_out.rearrange("(f pp) t -> pp f t", pp=128)
                                    [:, o - 1, T * 512:(T + 1) * 512],
                                    in_=x2t[T][:]).then_inc(s_st.h, 16)
        s_wb.exp += 32 * NF
        s_st.exp += 32 * NF

        @blk.sync
        def _(e):
            for o in range(NF):
                if o >= 1:
                    e.wait_ge(s_st.h, STOd[(o - 1, 1)])
                for T in range(2):
                    e.dma_start(out=x1t[T][:],
                                in_=x1.rearrange("(f pp) t -> pp f t", pp=128)
                                [:, o, T * 512:(T + 1) * 512]).then_inc(s_ld.h, 16)
                if o >= 1:
                    e.wait_ge(s_ve.h, CA[(o - 1, 0)])
                e.dma_start(out=wb32[0].rearrange("p (f oo) -> p f oo", oo=128),
                            in_=wchunks(WbT)[:, 0:16, o * 128:(o + 1) * 128]
                            ).then_inc(s_ld.h, 16)
        s_ld.exp += 48 * NF

        @blk.scalar
        def _(e):
            for o in range(NF):
                if o >= 1:
                    e.wait_ge(s_ve.h, CA[(o - 1, 1)])
                e.dma_start(out=wb32[1].rearrange("p (f oo) -> p f oo", oo=128),
                            in_=wchunks(WbT)[:, 16:32, o * 128:(o + 1) * 128]
                            ).then_inc(s_l2d.h, 16)
        s_l2d.exp += 16 * NF

        @blk.vector
        def _(e):
            for o in range(NF + 1):
                if o >= 1:
                    for T in range(2):
                        e.wait_ge(s_mm.h, EVT[(o - 1, T)])
                        e.wait_ge(s_ld.h, X1L[(o - 1, T)])
                        ts_ap(e, x2t[T][:], x1t[T][:], bsum[:, o - 1:o], ALU.add,
                              in1=psA[:, T * 512:(T + 1) * 512], op1=ALU.add)
                        s_ve.inc(e.tensor_tensor(
                            out=x2t[T][:], in0=x2t[T][:],
                            in1=xn1r[:, (o - 1) * 1024 + T * 512:
                                     (o - 1) * 1024 + T * 512 + 512],
                            op=ALU.add))
                if o < NF:
                    e.wait_ge(s_ld.h, W0L[o])
                    if o >= 2:
                        e.wait_ge(s_mm.h, EVT[(o - 2, 1)])
                    s_ve.inc(e.tensor_copy(out=bandD[o % 2][:, 0:2048],
                                           in_=wb32[0]))
                    e.wait_ge(s_l2d.h, W1L[o])
                    s_ve.inc(e.tensor_copy(out=bandD[o % 2][:, 2048:4096],
                                           in_=wb32[1]))

        @blk.tensor
        def _(e):
            for o in range(NF):
                e.wait_ge(s_wb.h, WBD[o])
                e.wait_ge(s_ve.h, CA[(o, 1)])
                for T in range(2):
                    if o >= 1:
                        e.wait_ge(s_ve.h, EVC[(o - 1, T)])
                    for f in range(NF):
                        e.matmul(psA[:, T * 512:(T + 1) * 512],
                                 bandD[o % 2][:, f * 128:(f + 1) * 128],
                                 xn1r[:, f * 1024 + T * 512: f * 1024 + T * 512 + 512],
                                 start=(f == 0), stop=False)
                    e.matmul(psA[:, T * 512:(T + 1) * 512], Bm_o[o % 2][:],
                             hw[:, T * 512:(T + 1) * 512], start=False, stop=False)
                    s_mm.inc(e.matmul(psA[:, T * 512:(T + 1) * 512],
                                      wup_o[o % 2][:],
                                      hfm[:, T * 512:(T + 1) * 512],
                                      start=False, stop=True))

    assert s_mm.exp == mmc, (s_mm.exp, mmc)
    assert s_ve.exp == vec, (s_ve.exp, vec)
    assert s_st.exp == stc, (s_st.exp, stc)

    with nc.Block() as blk:
        @blk.sync
        def _(e):
            e.wait_ge(s_st.h, s_st.exp)

    if standalone:
        p.close()
        return nc
    return None


_NC_CACHE = {}


def _get_nc(name, builder):
    if name not in _NC_CACHE:
        _NC_CACHE[name] = builder()
    return _NC_CACHE[name]


def kernel(**inp):
    if FUSED:
        return kernel_fused(inp)
    x = np.asarray(inp["L0_attn_hidden_states"])
    ident = np.eye(128).astype(bf16)
    metas = [core_meta(c) for c in range(NCORES)]

    WrT = np.ascontiguousarray(np.asarray(inp["L0_ffn_router_weight"]).T)
    rbias = np.ascontiguousarray(np.asarray(inp["L0_ffn_router_bias"])[None, :])
    in_maps = []
    for c in range(NCORES):
        b, mq, rows = metas[c]
        cos, sin = rope_tables(rows)
        in_maps.append({
            "x_fm": np.ascontiguousarray(x[b, rows, :].T),
            "Wq": np.asarray(inp["L0_attn_Wq"]), "Wk": np.asarray(inp["L0_attn_Wk"]),
            "Wv": np.asarray(inp["L0_attn_Wv"]), "WrT": WrT, "rbias": rbias,
            "cosT": cos, "sinT": sin, "ident": ident,
        })
    resA = run_bass_kernel_spmd(_get_nc("A", build_phase_A), in_maps,
                                core_ids=list(range(NCORES))).results

    Kfull = {bb: np.zeros((1024, S), bf16) for bb in range(2)}
    Vfull = {bb: np.zeros((S, 1024), bf16) for bb in range(2)}
    knf = {bb: np.zeros((NKV, S), np.float32) for bb in range(2)}
    for c in range(NCORES):
        b, mq, rows = metas[c]
        Kfull[b][:, rows] = resA[c]["K"]
        Vfull[b][rows, :] = resA[c]["Vtm"]
        knf[b][:, rows] = resA[c]["knormsq"]
    E8 = np.repeat(np.eye(NKV, dtype=np.float32), 4, axis=1)
    in_maps = []
    for c in range(NCORES):
        b, mq, rows = metas[c]
        in_maps.append({
            "Q": resA[c]["Q"], "Kfull": Kfull[b], "Vtmfull": Vfull[b],
            "knormsq_full": knf[b], "qmaxsq": resA[c]["qmaxsq"],
            "E8": E8, "masks": attn_masks(mq),
        })
    resB = run_bass_kernel_spmd(_get_nc("B", build_phase_B), in_maps,
                                core_ids=list(range(NCORES))).results

    wdownT = np.ascontiguousarray(np.asarray(inp["L1_attn_w_down"]).T)
    bdown = np.ascontiguousarray(np.asarray(inp["L1_attn_b_down"])[:, None])
    in_maps = []
    for c in range(NCORES):
        b, mq, rows = metas[c]
        in_maps.append({
            "attnout": resB[c]["attnout"], "Wo": np.asarray(inp["L0_attn_Wo"]),
            "x_fm": np.ascontiguousarray(x[b, rows, :].T),
            "xn": resA[c]["xn"], "wsum": resA[c]["wsum"], "wdenseT": resA[c]["wdenseT"],
            "ebias": np.asarray(inp["L0_ffn_expert_bias"]),
            "wdownT": wdownT, "bdown": bdown,
        })
    resC = run_bass_kernel_spmd(_get_nc("C", build_phase_C), in_maps,
                                core_ids=list(range(NCORES))).results

    cw_ = np.asarray(inp["L1_attn_hy_conv_w"])[:, 0, :]
    cb_ = np.asarray(inp["L1_attn_hy_conv_b"])
    fcd = np.asarray(inp["L1_attn_hy_fcD_w"])
    outw = np.asarray(inp["L1_attn_hy_out_w"])
    Bten = np.asarray(inp["L1_ffn_B"])
    ind4 = np.zeros((4, 256), np.float32)
    for col in range(256):
        feat = (col // 128) * 128 + (col % 128)
        ind4[feat // 64, col] = 1.0
    ind16 = np.zeros((8, 128), np.float32)
    for pp in range(128):
        ind16[pp // 16, pp] = 1.0
    common = {
        "gate1": np.asarray(inp["L1_attn_gate1"], np.float32).reshape(1, 1),
        "inwT": np.ascontiguousarray(np.asarray(inp["L1_attn_hy_in_w"]).T),
        "convw_x": np.ascontiguousarray(np.concatenate([cw_[0:128], cw_[128:256]], 1)),
        "convw_bc": np.ascontiguousarray(np.concatenate([cw_[256:272], cw_[272:288]], 1)),
        "convb_x": np.ascontiguousarray(np.stack([cb_[0:128], cb_[128:256]], 1)),
        "convb_bc": np.ascontiguousarray(np.stack([cb_[256:272], cb_[272:288]], 1)),
        "dtb": np.ascontiguousarray(np.asarray(inp["L1_attn_hy_dt_bias"])[:, None]),
        "fcdT": np.ascontiguousarray(np.concatenate([fcd.T[0:128], fcd.T[128:256]], 1)),
        "Dv": np.ascontiguousarray(np.asarray(inp["L1_attn_hy_D"])[:, None]),
        "normw": np.ascontiguousarray(
            np.stack([np.asarray(inp["L1_attn_hy_norm_w"])[0:128],
                      np.asarray(inp["L1_attn_hy_norm_w"])[128:256]], 1)),
        "outwT": np.ascontiguousarray(np.concatenate([outw.T[0:128], outw.T[128:256]], 1)),
        "ind4": ind4,
        "WbT": np.ascontiguousarray(np.asarray(inp["L1_ffn_W_base"]).T),
        "bb32": np.ascontiguousarray(np.asarray(inp["L1_ffn_b_base"]).reshape(NF, 128).T),
        "bu32": np.ascontiguousarray(np.asarray(inp["L1_attn_b_up"]).reshape(NF, 128).T),
        "wr2T": np.ascontiguousarray(np.asarray(inp["L1_ffn_W_router"]).T),
        "A2": np.ascontiguousarray(np.asarray(inp["L1_ffn_A"]).reshape(128, H).T),
        "Bm": np.ascontiguousarray(Bten.transpose(0, 2, 1).reshape(128, H)),
        "wupT": np.ascontiguousarray(np.asarray(inp["L1_attn_w_up"]).T),
        "ind16": ind16.astype(bf16), "ident": ident,
    }
    in_maps = []
    for c in range(NCORES):
        b, mq, rows = metas[c]
        partner = c ^ 1
        xmid_l0 = resC[c]["xmid"] if b == 0 else resC[partner]["xmid"]
        xmid_l1 = resC[partner]["xmid"] if b == 0 else resC[c]["xmid"]
        indsel = np.zeros((128, 2), np.float32)
        indsel[:, 0] = 1.0 if b == 0 else 0.0
        indsel[:, 1] = 1.0 - indsel[:, 0]
        m = {"x1_fm": resC[c]["x1_fm"], "xn1": resC[c]["xn1"],
             "xmid_l0": xmid_l0, "xmid_l1": xmid_l1, "indsel": indsel}
        m.update(common)
        in_maps.append(m)
    resD = run_bass_kernel_spmd(_get_nc("D", build_phase_D), in_maps,
                                core_ids=list(range(NCORES))).results

    out = np.zeros((B, S, H), np.float32)
    for c in range(NCORES):
        b, mq, rows = metas[c]
        out[b, rows, :] = resD[c]["out_fm"].T
    return out


def emit_A_fused(p, io):
    nc = p.nc
    x = io["x_full"]
    Wq, Wk, Wv = io["Wq"], io["Wk"], io["Wv"]
    WrT, rbias = io["WrT"], io["rbias"]
    cosT, sinT, cosF, sinF = io["cosT"], io["sinT"], io["cosF"], io["sinF"]
    identw = io["ident"]
    xnfull, Qh, Kfull, Vtm = io["xnfull"], io["Q_hbm"], io["Kfull"], io["Vtmfull"]
    wsum_hbm = io["wsum_hbm"]
    kmaxr, qmx, wde = io["kmax_sb"], io["qmx_sb"], io["wde_sb"]

    s_ld = p.nsem("ld")
    s_wb = p.nsem("wb")
    s_mm = p.nsem("mm")
    s_ve = p.nsem("ve")
    s_sc = p.nsem("sc")
    s_st = p.nsem("st")

    with p.scope():
        ones1 = p.sb([128, 1], BF16)
        ones1f = p.sb([128, 1], F32)
        onesr = p.sb([1, 128], F32)
        ones64 = p.sb([128, 64], F32)
        ident = p.sb([128, 128], BF16)
        cos_s = p.sb([128, TPC], F32)
        sin_s = p.sb([128, TPC], F32)
        cosf_s = p.sb([128, S], F32)
        sinf_s = p.sb([128, S], F32)
        mrows = p.sb([1, 512], F32)
        rrows = p.sb([1, 512], F32)
        stat = p.sb([1, 512], F32)
        stat2 = p.sb([1, 512], F32)
        mB = p.sb([128, 512], F32)
        rB = p.sb([128, 512], F32)
        sq = [p.sb([128, 512], F32) for _ in range(2)]
        qf = p.sb([128, 512], F32)
        ta = p.sb([128, 512], F32)
        tb = p.sb([128, 512], F32)
        qbf = p.sb([128, 512], BF16)
        qn = p.sb([1, 512], F32)
        kmrow = p.sb([1, 512], F32)

        with nc.Block() as blk:
            @blk.sync
            def _(e):
                e.dma_start(out=cos_s[:], in_=cosT[:, :]).then_inc(s_ld.h, 16)
                e.dma_start(out=sin_s[:], in_=sinT[:, :]).then_inc(s_ld.h, 16)
                e.dma_start(out=cosf_s[:], in_=cosF[:, :]).then_inc(s_ld.h, 16)
                e.dma_start(out=sinf_s[:], in_=sinF[:, :]).then_inc(s_ld.h, 16)
            @blk.scalar
            def _(e):
                e.dma_start(out=ident[:], in_=identw[:, :]).then_inc(s_ld.h, 16)
            @blk.vector
            def _(e):
                e.memset(ones1[:], 1.0)
                e.memset(ones1f[:], 1.0)
                e.memset(onesr[:], 1.0)
                e.memset(ones64[:], 1.0)
                e.memset(qmx[:], 0.0)
                e.memset(kmaxr[:], 0.0)
            s_ld.exp += 80

        s_l2a = p.nsem("l2")
        with p.scope():
            xsb = [p.sb([128, 8 * 512], F32) for _ in range(2)]
            xap = [p.sb([128, 8 * 512], F32) for _ in range(4)]
            xnst2 = [p.sb([128, 8 * 512], BF16) for _ in range(2)]
            mB2 = [p.sb([128, 512], F32) for _ in range(2)]
            rB2 = [p.sb([128, 512], F32) for _ in range(2)]
            mrows2 = [p.sb([1, 512], F32) for _ in range(2)]
            rrows2 = [p.sb([1, 512], F32) for _ in range(2)]
            statb = [p.sb([1, 512], F32) for _ in range(2)]
            aps = p.sb([128, 512], F32)
            ps_sum = p.ps([128, 512])[0:1, :]
            ps_sq = p.ps([128, 512])[0:1, :]
            ps_bm = p.ps([128, 512])[:]
            ps_br = p.ps([128, 512])[:]

            NTf = 8
            b_mm, b_ve, b_sc, b_st = s_mm.exp, s_ve.exp, s_sc.exp, s_st.exp
            b_ld, b_l2 = s_ld.exp, s_l2a.exp
            G1, BM, SQ1, LE, F1, CP, AP, SLD, ALD, STO1 = ({} for _ in range(10))
            K1 = {}
            mmc = b_mm
            for i in range(NTf + 1):
                if i >= 1:
                    mmc += 1; BM[i - 1] = mmc
                if i < NTf:
                    for fc in range(4):
                        for f in range(8):
                            mmc += 1; K1[(i, fc * 8 + f)] = mmc
                        G1[(i, fc)] = mmc
            vec = b_ve
            for i in range(NTf + 1):
                if i >= 1:
                    vec += 1; CP[i - 1] = vec
                    for fc in range(4):
                        vec += 1; AP[(i - 1, fc)] = vec
                if i < NTf:
                    vec += 5; F1[i] = vec
            scc = b_sc
            for i in range(NTf):
                for k in range(32):
                    scc += 1; SQ1[(i, k)] = scc
                scc += 2; LE[i] = scc
            lde = b_ld
            for i in range(NTf):
                for fc in range(4):
                    lde += 16; SLD[(i, fc)] = lde
            ldo = b_l2
            for i in range(NTf):
                for fc in range(4):
                    ldo += 16; ALD[(i, fc)] = ldo
            stc = b_st
            for i in range(1, NTf + 1):
                for fc in range(4):
                    stc += 16; STO1[(i - 1, fc)] = stc

            def x_src(i, fc):
                return (x.rearrange("(f p) t -> p f t", p=128)
                        [:, fc * 8:(fc + 1) * 8, i * 512:(i + 1) * 512])

            with nc.Block() as blk:
                @blk.sync
                def _(e):
                    for i in range(NTf):
                        for fc in range(4):
                            c = 4 * i + fc
                            if c >= 2:
                                cp = c - 2
                                e.wait_ge(s_mm.h, G1[(cp // 4, cp % 4)])
                                e.wait_ge(s_sc.h, SQ1[(cp // 4, (cp % 4) * 8 + 7)])
                            e.dma_start(out=xsb[c % 2][:].rearrange(
                                "p (f t) -> p f t", f=8),
                                in_=x_src(i, fc)).then_inc(s_ld.h, 16)
                s_ld.exp += 16 * 4 * NTf

                @blk.scalar
                def _(e):
                    for i in range(NTf):
                        for fc in range(4):
                            if i >= 1:
                                e.wait_ge(s_ve.h, AP[(i - 1, fc)])
                            e.dma_start(out=xap[(4 * i + fc) % 4][:].rearrange(
                                "p (f t) -> p f t", f=8),
                                in_=x_src(i, fc)).then_inc(s_l2a.h, 16)
                        for fc in range(4):
                            e.wait_ge(s_ld.h, SLD[(i, fc)])
                            for f in range(8):
                                k = fc * 8 + f
                                if k >= 2:
                                    e.wait_ge(s_mm.h, K1[(i, k - 2)])
                                s_sc.inc(e.square(
                                    out=sq[k % 2][:],
                                    in_=xsb[(4 * i + fc) % 2][:, f * 512:(f + 1) * 512]))
                        e.wait_ge(s_ve.h, F1[i])
                        s_sc.fence(e, e.activation(out=statb[i % 2][:],
                                                   in_=statb[i % 2][:], func=AF.Ln))
                        s_sc.inc(e.activation(out=rrows2[i % 2][:], in_=statb[i % 2][:],
                                              func=AF.Exp, scale=-0.5))
                s_l2a.exp += 16 * 4 * NTf

                @blk.tensor
                def _(e):
                    for i in range(NTf + 1):
                        if i >= 1:
                            jj = i - 1
                            e.wait_ge(s_sc.h, LE[jj])
                            if jj >= 1:
                                e.wait_ge(s_ve.h, CP[jj - 1])
                            e.matmul(ps_bm, onesr[:], mrows2[jj % 2][:],
                                     start=True, stop=True)
                            s_mm.inc(e.matmul(ps_br, onesr[:], rrows2[jj % 2][:],
                                              start=True, stop=True))
                        if i < NTf:
                            for fc in range(4):
                                e.wait_ge(s_ld.h, SLD[(i, fc)])
                                if fc == 0 and i >= 1:
                                    e.wait_ge(s_ve.h, F1[i - 1])
                                for f in range(8):
                                    k = fc * 8 + f
                                    e.matmul(ps_sum, ones1f[:],
                                             xsb[(4 * i + fc) % 2][:, f * 512:(f + 1) * 512],
                                             start=(k == 0), stop=(k == NF - 1))
                                    e.wait_ge(s_sc.h, SQ1[(i, k)])
                                    s_mm.inc(e.matmul(ps_sq, ones1f[:], sq[k % 2][:],
                                                      start=(k == 0), stop=(k == NF - 1)))

                @blk.vector
                def _(e):
                    for i in range(NTf + 1):
                        if i >= 1:
                            jj = i - 1
                            e.wait_ge(s_mm.h, BM[jj])
                            e.tensor_copy(out=mB2[jj % 2][:], in_=ps_bm[:, 0:512])
                            s_ve.inc(e.tensor_copy(out=rB2[jj % 2][:],
                                                   in_=ps_br[:, 0:512]))
                            for fc in range(4):
                                ca = 4 * jj + fc
                                e.wait_ge(s_l2a.h, ALD[(jj, fc)])
                                if ca >= 2:
                                    cp = ca - 2
                                    e.wait_ge(s_st.h, STO1[(cp // 4, cp % 4)])
                                for f in range(8):
                                    e.tensor_tensor(out=aps[:],
                                                    in0=xap[ca % 4][:, f * 512:(f + 1) * 512],
                                                    in1=mB2[jj % 2][:], op=ALU.subtract)
                                    mmv = e.tensor_tensor(
                                        out=xnst2[fc % 2][:, f * 512:(f + 1) * 512],
                                        in0=aps[:], in1=rB2[jj % 2][:], op=ALU.mult)
                                s_ve.inc(mmv)
                        if i < NTf:
                            e.wait_ge(s_mm.h, G1[(i, 3)])
                            if i >= 1:
                                e.wait_ge(s_sc.h, LE[i - 1])
                            s_ve.fence(e, e.tensor_scalar(out=mrows2[i % 2][:],
                                                          in0=ps_sum, scalar1=1.0 / H,
                                                          scalar2=None, op0=ALU.mult))
                            s_ve.fence(e, e.tensor_scalar(out=statb[i % 2][:], in0=ps_sq,
                                                          scalar1=1.0 / H, scalar2=None,
                                                          op0=ALU.mult))
                            s_ve.fence(e, e.tensor_tensor(out=stat2[:], in0=mrows2[i % 2][:],
                                                          in1=mrows2[i % 2][:], op=ALU.mult))
                            s_ve.fence(e, e.tensor_tensor(out=statb[i % 2][:],
                                                          in0=statb[i % 2][:],
                                                          in1=stat2[:], op=ALU.subtract))
                            s_ve.inc(e.tensor_scalar(out=statb[i % 2][:],
                                                     in0=statb[i % 2][:], scalar1=1e-5,
                                                     scalar2=None, op0=ALU.add))

                @blk.gpsimd
                def _(e):
                    for i in range(1, NTf + 1):
                        jj = i - 1
                        for fc in range(4):
                            e.wait_ge(s_ve.h, AP[(jj, fc)])
                            e.dma_start(
                                out=xnfull.rearrange("(f p) t -> p f t", p=128)
                                [:, fc * 8:(fc + 1) * 8, jj * 512:(jj + 1) * 512],
                                in_=xnst2[fc % 2][:].rearrange("p (f t) -> p f t", f=8),
                            ).then_inc(s_st.h, 16)
                s_st.exp += 16 * 4 * NTf

            assert s_mm.exp == mmc, (s_mm.exp, mmc)
            assert s_ve.exp == vec, (s_ve.exp, vec)
            assert s_sc.exp == scc, (s_sc.exp, scc)
            assert s_st.exp == stc, (s_st.exp, stc)

        s_l2 = p.nsem("l2")
        with p.scope():
            kband2 = [p.sb([128, NF * 256], BF16) for _ in range(2)]
            vband2 = [p.sb([128, NF * 256], BF16) for _ in range(2)]
            xbf2 = [p.sb([128, 8 * 512], BF16) for _ in range(4)]
            kf = [p.sb([128, 512], F32) for _ in range(4)]
            vf = [p.sb([128, 512], BF16) for _ in range(4)]
            sq2 = [p.sb([128, 512], F32) for _ in range(4)]
            kst = [p.sb([128, 512], BF16) for _ in range(4)]
            vst = [p.sb([128, 512], BF16) for _ in range(4)]
            ps_kv = [p.ps([128, 512]) for _ in range(4)]
            ps_n2g = [p.ps([128, 512])[0:1, :] for _ in range(2)]
            ps_t2 = p.ps([128, 1024], BF16)

            NJ = 32
            b_mm, b_ve, b_sc, b_st, b_wb = (s_mm.exp, s_ve.exp, s_sc.exp,
                                            s_st.exp, s_wb.exp)
            b_ld, b_l2 = s_ld.exp, s_l2.exp
            G, STT, TR, EV, KM, RP, VC, SQ, LDm, STO = ({} for _ in range(10))
            mmc = b_mm
            for j in range(NJ):
                for fc in range(4):
                    mmc += 1; G[(j, fc)] = mmc
                if j >= 1:
                    jj = j - 1
                    for gl in range(2):
                        mmc += 1; STT[(jj, gl)] = mmc
                    mmc += 1; TR[jj] = mmc
            for gl in range(2):
                mmc += 1; STT[(NJ - 1, gl)] = mmc
            mmc += 1; TR[NJ - 1] = mmc
            vec = b_ve
            for j in range(NJ + 1):
                if j < NJ:
                    vec += 1; EV[j] = vec
                if j >= 1:
                    jj = j - 1
                    for gl in range(2):
                        vec += 3; KM[(jj, gl)] = vec
                        vec += 1; RP[(jj, gl)] = vec
                    vec += 1; VC[jj] = vec
            scc = b_sc
            for j in range(NJ + 1):
                if j >= 1:
                    for gl in range(2):
                        scc += 1; SQ[(j - 1, gl)] = scc
            stc = b_st
            for j in range(NJ + 1):
                if j >= 1:
                    jj = j - 1
                    for gl in range(2):
                        stc += 16; STO[(jj, 'k', gl)] = stc
                    for gl in range(2):
                        stc += 16; STO[(jj, 'v', gl)] = stc
            lde = b_ld
            ldo = b_l2
            for j in range(NJ):
                lde += 16; LDm[(j, 0)] = lde
                lde += 16; LDm[(j, 2)] = lde
                ldo += 16; LDm[(j, 1)] = ldo
                ldo += 16; LDm[(j, 3)] = ldo
            WBS = [b_wb + 32 * (sw + 1) for sw in range(4)]

            def xn_src(j, fc):
                Tf = j % 8
                return (xnfull.rearrange("(f p) t -> p f t", p=128)
                        [:, fc * 8:(fc + 1) * 8, Tf * 512:(Tf + 1) * 512])

            with nc.Block() as blk:
                @blk.sync
                def _(e):
                    for j in range(NJ):
                        for fc in (0, 2):
                            if j >= 1:
                                e.wait_ge(s_mm.h, G[(j - 1, fc)])
                            e.dma_start(out=xbf2[fc][:].rearrange(
                                "p (f t) -> p f t", f=8),
                                in_=xn_src(j, fc)).then_inc(s_ld.h, 16)
                s_ld.exp += 32 * NJ

                @blk.scalar
                def _(e):
                    e.wait_ge(s_st.h, b_st)
                    for j in range(NJ + 1):
                        if j < NJ:
                            for fc in (1, 3):
                                if j >= 1:
                                    e.wait_ge(s_mm.h, G[(j - 1, fc)])
                                e.dma_start(out=xbf2[fc][:].rearrange(
                                    "p (f t) -> p f t", f=8),
                                    in_=xn_src(j, fc)).then_inc(s_l2.h, 16)
                        if j >= 1:
                            jj = j - 1
                            for gl in range(2):
                                e.wait_ge(s_ve.h, EV[jj])
                                if jj >= 2:
                                    e.wait_ge(s_mm.h, STT[(jj - 2, gl)])
                                s_sc.inc(e.square(out=sq2[(jj % 2) * 2 + gl][:],
                                                  in_=kf[(jj % 2) * 2 + gl][:]))
                s_l2.exp += 32 * NJ

                @blk.gpsimd
                def _(e):
                    e.dma_start(out=kband2[0][:].rearrange("p (f o) -> p f o", o=256),
                                in_=wchunks(Wk)[:, :, 0:256]).then_inc(s_wb.h, 16)
                    e.dma_start(out=vband2[0][:].rearrange("p (f o) -> p f o", o=256),
                                in_=wchunks(Wv)[:, :, 0:256]).then_inc(s_wb.h, 16)
                    for j in range(NJ + 1):
                        if j < NJ and j % 8 == 4 and j // 8 < 3:
                            swn = j // 8 + 1
                            if swn >= 2:
                                e.wait_ge(s_mm.h, G[(8 * (swn - 1) - 1, 3)])
                            e.dma_start(out=kband2[swn % 2][:].rearrange(
                                "p (f o) -> p f o", o=256),
                                in_=wchunks(Wk)[:, :, swn * 256:(swn + 1) * 256]
                                ).then_inc(s_wb.h, 16)
                            e.dma_start(out=vband2[swn % 2][:].rearrange(
                                "p (f o) -> p f o", o=256),
                                in_=wchunks(Wv)[:, :, swn * 256:(swn + 1) * 256]
                                ).then_inc(s_wb.h, 16)
                        if j >= 1:
                            jj = j - 1
                            sw, Tf = jj // 8, jj % 8
                            for gl in range(2):
                                g = sw * 2 + gl
                                e.wait_ge(s_ve.h, RP[(jj, gl)])
                                e.dma_start(out=Kfull[g * 128:(g + 1) * 128,
                                                      Tf * 512:(Tf + 1) * 512],
                                            in_=kst[(jj % 2) * 2 + gl][:]
                                            ).then_inc(s_st.h, 16)
                            e.wait_ge(s_ve.h, VC[jj])
                            for gl in range(2):
                                g = sw * 2 + gl
                                e.dma_start(
                                    out=Vtm[Tf * 512:(Tf + 1) * 512,
                                            g * 128:(g + 1) * 128]
                                    .rearrange("(k p) d -> p k d", p=128),
                                    in_=vst[(jj % 2) * 2 + gl][:]
                                    .rearrange("p (k d) -> p k d", k=4),
                                ).then_inc(s_st.h, 16)
                s_wb.exp += 128
                s_st.exp += 64 * NJ

                @blk.tensor
                def _(e):
                    for j in range(NJ + 1):
                        if j < NJ:
                            sw = j // 8
                            for fc in range(4):
                                e.wait_ge((s_ld if fc in (0, 2) else s_l2).h,
                                          LDm[(j, fc)])
                                if fc == 0:
                                    e.wait_ge(s_wb.h, WBS[sw])
                                    if j >= 1:
                                        e.wait_ge(s_ve.h, EV[j - 1])
                                for f in range(8):
                                    k = fc * 8 + f
                                    rhs = xbf2[fc][:, f * 512:(f + 1) * 512]
                                    for gl in range(2):
                                        e.matmul(ps_kv[gl][:],
                                                 kband2[sw % 2][:, k * 256 + gl * 128:
                                                                k * 256 + (gl + 1) * 128],
                                                 rhs, start=(k == 0), stop=(k == NF - 1))
                                        mm = e.matmul(ps_kv[2 + gl][:],
                                                      vband2[sw % 2][:, k * 256 + gl * 128:
                                                                     k * 256 + (gl + 1) * 128],
                                                      rhs, start=(k == 0), stop=(k == NF - 1))
                                s_mm.inc(mm)
                        if j >= 1:
                            jj = j - 1
                            for gl in range(2):
                                e.wait_ge(s_sc.h, SQ[(jj, gl)])
                                if jj >= 1:
                                    e.wait_ge(s_ve.h, KM[(jj - 1, gl)])
                                s_mm.inc(e.matmul(ps_n2g[gl], ones1f[:],
                                                  sq2[(jj % 2) * 2 + gl][:],
                                                  start=True, stop=True))
                            if jj >= 1:
                                e.wait_ge(s_ve.h, VC[jj - 1])
                            for gl in range(2):
                                for k4 in range(4):
                                    mm = e.transpose(
                                        ps_t2[:, gl * 512 + k4 * 128:
                                              gl * 512 + (k4 + 1) * 128],
                                        vf[(jj % 2) * 2 + gl][:, k4 * 128:(k4 + 1) * 128],
                                        ident[:])
                            s_mm.inc(mm)

                @blk.vector
                def _(e):
                    for j in range(NJ + 1):
                        if j < NJ:
                            e.wait_ge(s_mm.h, G[(j, 3)])
                            if j >= 2:
                                e.wait_ge(s_mm.h, TR[j - 2])
                                e.wait_ge(s_sc.h, SQ[(j - 2, 1)])
                            for gl in range(2):
                                e.tensor_copy(out=kf[(j % 2) * 2 + gl][:],
                                              in_=ps_kv[gl][:])
                                mmv = e.tensor_copy(out=vf[(j % 2) * 2 + gl][:],
                                                    in_=ps_kv[2 + gl][:])
                            s_ve.inc(mmv)
                        if j >= 1:
                            jj = j - 1
                            sw, Tf = jj // 8, jj % 8
                            for gl in range(2):
                                g = sw * 2 + gl
                                e.wait_ge(s_mm.h, STT[(jj, gl)])
                                s_ve.fence(e, e.tensor_copy(out=kmrow[:],
                                                            in_=ps_n2g[gl]))
                                s_ve.fence(e, e.reduce_max(out=qn[:, 0:1],
                                                           in_=kmrow[:], axis=AX.X))
                                s_ve.fence(e, e.tensor_tensor(
                                    out=kmaxr[:, g:g + 1], in0=kmaxr[:, g:g + 1],
                                    in1=qn[:, 0:1], op=ALU.max))
                                if jj >= 2:
                                    e.wait_ge(s_st.h, STO[(jj - 2, 'k', gl)])
                                cs = cosf_s[:, Tf * 512:(Tf + 1) * 512]
                                sn = sinf_s[:, Tf * 512:(Tf + 1) * 512]
                                kfb = kf[(jj % 2) * 2 + gl]
                                e.tensor_copy(out=tb[0:64, :], in_=kfb[64:128, :])
                                e.tensor_copy(out=tb[64:128, :], in_=kfb[0:64, :])
                                e.tensor_tensor(out=ta[:], in0=kfb[:], in1=cs,
                                                op=ALU.mult)
                                e.tensor_tensor(out=tb[:], in0=tb[:], in1=sn,
                                                op=ALU.mult)
                                s_ve.inc(e.tensor_tensor(
                                    out=kst[(jj % 2) * 2 + gl][:], in0=ta[:],
                                    in1=tb[:], op=ALU.add))
                            e.wait_ge(s_mm.h, TR[jj])
                            if jj >= 2:
                                e.wait_ge(s_st.h, STO[(jj - 2, 'v', 1)])
                            e.tensor_copy(out=vst[(jj % 2) * 2][:], in_=ps_t2[:, 0:512])
                            s_ve.inc(e.tensor_copy(out=vst[(jj % 2) * 2 + 1][:],
                                                   in_=ps_t2[:, 512:1024]))

            assert s_mm.exp == mmc, (s_mm.exp, mmc)
            assert s_ve.exp == vec, (s_ve.exp, vec)
            assert s_sc.exp == scc, (s_sc.exp, scc)
            assert s_st.exp == stc, (s_st.exp, stc)

        xn_own = p.sb([128, NF * TPC], BF16)
        with nc.Block() as blk:
            @blk.sync
            def _(e):
                e.wait_ge(s_st.h, s_st.exp)
                for arm in e.switch_core_id(n=128):
                    mqc = (arm.logical % 8) >> 1
                    for l in range(8):
                        e.dma_start(
                            out=xn_own[:].rearrange("p (f t) -> p f t", f=NF)
                            [:, :, l * 128:(l + 1) * 128],
                            in_=xnfull.rearrange("(f p) t -> p f t", p=128)
                            [:, :, (4 * l + mqc) * 128:(4 * l + mqc) * 128 + 128],
                        ).then_inc(s_ld.h, 16)
                s_ld.exp += 128
                e.wait_ge(s_ld.h, s_ld.exp)

        with p.scope():
            band2q = [p.sb([128, NF * 128], BF16) for _ in range(2)]
            qst = [qbf, p.sb([128, 512], BF16)]
            ps_q = [p.ps([128, 512]) for _ in range(2)]
            ps_n3g = [p.ps([128, 512])[0:1, :] for _ in range(2)]
            ps_mm2 = ps_q[0]

            NI = NH * NT
            b_mm, b_ve, b_sc, b_st, b_wb = (s_mm.exp, s_ve.exp, s_sc.exp,
                                            s_st.exp, s_wb.exp)
            G3, STT3, SQ3, RP3, STO3 = ({} for _ in range(5))
            mmc = b_mm
            for i in range(NI):
                mmc += 1; G3[i] = mmc
                if i >= 1:
                    mmc += 1; STT3[i - 1] = mmc
            mmc += 1; STT3[NI - 1] = mmc
            scc = b_sc
            for i in range(NI):
                scc += 1; SQ3[i] = scc
            vec = b_ve
            for i in range(1, NI + 1):
                vec += 3; vec += 1; RP3[i - 1] = vec
            stc = b_st
            for i in range(1, NI + 1):
                stc += 16; STO3[i - 1] = stc
            WBQ = [b_wb + 16 * (o + 1) for o in range(NH)]

            with nc.Block() as blk:
                @blk.gpsimd
                def _(e):
                    for o in range(NH):
                        if o >= 2:
                            e.wait_ge(s_mm.h, G3[2 * (o - 2) + 1])
                        e.dma_start(out=band2q[o % 2][:].rearrange(
                            "p (f oo) -> p f oo", oo=128),
                            in_=wchunks(Wq)[:, :, o * 128:(o + 1) * 128]
                            ).then_inc(s_wb.h, 16)
                s_wb.exp += 16 * NH

                @blk.tensor
                def _(e):
                    for i in range(NI + 1):
                        if i < NI:
                            o, T = i // 2, i % 2
                            if T == 0:
                                e.wait_ge(s_wb.h, WBQ[o])
                            if i >= 2:
                                e.wait_ge(s_sc.h, SQ3[i - 2])
                                e.wait_ge(s_ve.h, RP3[i - 2])
                            for f in range(NF):
                                mm = e.matmul(ps_q[i % 2][:],
                                              band2q[o % 2][:, f * 128:(f + 1) * 128],
                                              xn_own[:, f * TPC + T * 512:
                                                     f * TPC + T * 512 + 512],
                                              start=(f == 0), stop=(f == NF - 1))
                            s_mm.inc(mm)
                        if i >= 1:
                            jj = i - 1
                            e.wait_ge(s_sc.h, SQ3[jj])
                            s_mm.inc(e.matmul(ps_n3g[jj % 2], ones1f[:], sq[jj % 2][:],
                                              start=True, stop=True))

                @blk.scalar
                def _(e):
                    for i in range(NI):
                        e.wait_ge(s_mm.h, G3[i])
                        if i >= 2:
                            e.wait_ge(s_mm.h, STT3[i - 2])
                        s_sc.inc(e.square(out=sq[i % 2][:], in_=ps_q[i % 2][:]))

                @blk.vector
                def _(e):
                    for i in range(1, NI + 1):
                        jj = i - 1
                        o, T = jj // 2, jj % 2
                        e.wait_ge(s_mm.h, STT3[jj])
                        s_ve.fence(e, e.tensor_copy(out=qn[:], in_=ps_n3g[jj % 2]))
                        s_ve.fence(e, e.reduce_max(out=qn[:, 1:2], in_=qn[:],
                                                   axis=AX.X))
                        s_ve.fence(e, e.tensor_tensor(out=qmx[:, o:o + 1],
                                                      in0=qmx[:, o:o + 1],
                                                      in1=qn[:, 1:2], op=ALU.max))
                        if jj >= 2:
                            e.wait_ge(s_st.h, STO3[jj - 2])
                        e.tensor_copy(out=qf[:], in_=ps_q[jj % 2][:])
                        cs = cos_s[:, T * 512:(T + 1) * 512]
                        sn = sin_s[:, T * 512:(T + 1) * 512]
                        e.tensor_copy(out=tb[0:64, :], in_=qf[64:128, :])
                        e.tensor_copy(out=tb[64:128, :], in_=qf[0:64, :])
                        e.tensor_tensor(out=ta[:], in0=qf[:], in1=cs, op=ALU.mult)
                        e.tensor_tensor(out=tb[:], in0=tb[:], in1=sn, op=ALU.mult)
                        s_ve.inc(e.tensor_tensor(out=qst[jj % 2][:], in0=ta[:],
                                                 in1=tb[:], op=ALU.add))

                @blk.sync
                def _(e):
                    for i in range(1, NI + 1):
                        jj = i - 1
                        o, T = jj // 2, jj % 2
                        e.wait_ge(s_ve.h, RP3[jj])
                        e.dma_start(out=Qh[o * 128:(o + 1) * 128,
                                           T * 512:(T + 1) * 512],
                                    in_=qst[jj % 2][:]).then_inc(s_st.h, 16)
                s_st.exp += 16 * NI

            assert s_mm.exp == mmc, (s_mm.exp, mmc)
            assert s_ve.exp == vec, (s_ve.exp, vec)
            assert s_sc.exp == scc, (s_sc.exp, scc)
            assert s_st.exp == stc, (s_st.exp, stc)

            wr_full = p.sb([128, NF * 64], BF16)
            rb_row = p.sb([1, 64], F32)
            rbB = p.sb([128, 64], F32)
            probs = p.sb([128, 64], F32)
            work = p.sb([128, 64], F32)
            eqm = p.sb([128, 64], F32)
            mask8 = p.sb([128, 64], F32)
            wsv = p.sb([128, 1], F32)
            mrow1 = p.sb([128, 1], F32)
            zr = p.sb([128, 1], F32)
            wdet = p.sb([128, 64], BF16)
            ps_r = ps_mm2[:, 0:64]
            ps_wt = p.ps([128, 1024], BF16)[0:64, 0:128]

            with nc.Block() as blk:
                @blk.gpsimd
                def _(e):
                    e.dma_start(out=wr_full[:].rearrange("p (f o) -> p f o", o=64),
                                in_=wchunks(WrT)).then_inc(s_wb.h, 16)
                @blk.sync
                def _(e):
                    e.dma_start(out=rb_row[:], in_=rbias[:, :]).then_inc(s_wb.h, 16)
                s_wb.exp += 32
                wbr = s_wb.exp
                @blk.tensor
                def _(e, wbr=wbr):
                    e.wait_ge(s_wb.h, wbr)
                    s_mm.inc(e.matmul(ps_mm2[:, 0:64], onesr[:], rb_row[:],
                                      start=True, stop=True))
                mt = s_mm.exp
                @blk.vector
                def _(e, mt=mt):
                    e.wait_ge(s_mm.h, mt)
                    s_ve.inc(e.tensor_copy(out=rbB[:], in_=ps_mm2[:, 0:64]))

            for tsl in range(8):
                with nc.Block() as blk:
                    @blk.tensor
                    def _(e, tsl=tsl):
                        for f in range(NF):
                            mm = e.matmul(ps_r,
                                          xn_own[:, f * TPC + tsl * 128:
                                                 f * TPC + tsl * 128 + 128],
                                          wr_full[:, f * 64:(f + 1) * 64],
                                          start=(f == 0), stop=(f == NF - 1))
                        s_mm.inc(mm)
                    mt = s_mm.exp
                    @blk.vector
                    def _(e, mt=mt):
                        e.wait_ge(s_mm.h, mt)
                        s_ve.fence(e, e.tensor_tensor(out=probs[:], in0=ps_r, in1=rbB[:],
                                                      op=ALU.add))
                        s_ve.inc(e.reduce_max(out=mrow1[:], in_=probs[:], axis=AX.X,
                                              negate=True))
                    vt = s_ve.exp
                    @blk.scalar
                    def _(e, vt=vt):
                        e.wait_ge(s_ve.h, vt)
                        s_sc.inc(e.activation(out=probs[:], in_=probs[:], func=AF.Exp,
                                              bias=mrow1[:]))
                    st2 = s_sc.exp
                    @blk.vector
                    def _(e, st2=st2):
                        e.wait_ge(s_sc.h, st2)
                        s_ve.fence(e, e.reduce_sum(out=zr[:], in_=probs[:], axis=AX.X))
                        s_ve.fence(e, e.reciprocal(out=zr[:], in_=zr[:]))
                        e.memset(mask8[:], 0.0)
                        e.memset(wsv[:], 0.0)
                        s_ve.fence(e, e.tensor_copy(out=work[:], in_=probs[:]))
                        for it in range(8):
                            s_ve.fence(e, e.reduce_max(out=mrow1[:], in_=work[:], axis=AX.X))
                            s_ve.fence(e, e.scalar_tensor_tensor(
                                out=eqm[:], in0=work[:], scalar=mrow1[:], in1=ones64[:],
                                op0=ALU.is_ge, op1=ALU.mult))
                            e.tensor_tensor(out=mask8[:], in0=mask8[:], in1=eqm[:],
                                            op=ALU.max)
                            e.scalar_tensor_tensor(out=wsv[:], in0=mrow1[:], scalar=0.0,
                                                   in1=wsv[:], op0=ALU.max, op1=ALU.add)
                            s_ve.fence(e, e.scalar_tensor_tensor(
                                out=work[:], in0=eqm[:], scalar=-1e30, in1=work[:],
                                op0=ALU.mult, op1=ALU.add))
                        s_ve.fence(e, e.tensor_tensor(out=wsv[:], in0=wsv[:], in1=zr[:],
                                                      op=ALU.mult))
                        s_ve.inc(e.scalar_tensor_tensor(
                            out=wdet[:], in0=mask8[:], scalar=zr[:], in1=probs[:],
                            op0=ALU.mult, op1=ALU.mult))
                    vt2 = s_ve.exp
                    @blk.tensor
                    def _(e, vt2=vt2):
                        e.wait_ge(s_ve.h, vt2)
                        s_mm.inc(e.transpose(ps_wt, wdet[:, 0:64], ident[:]))
                    mt2 = s_mm.exp
                    @blk.vector
                    def _(e, tsl=tsl, mt2=mt2):
                        e.wait_ge(s_mm.h, mt2)
                        s_ve.inc(e.tensor_copy(out=wde[:, tsl * 128:(tsl + 1) * 128],
                                               in_=ps_wt))
                    vt3 = s_ve.exp
                    @blk.sync
                    def _(e, tsl=tsl, vt3=vt3):
                        e.wait_ge(s_ve.h, vt3)
                        e.dma_start(out=wsum_hbm[tsl * 128:(tsl + 1) * 128, :],
                                    in_=wsv[:]).then_inc(s_st.h, 16)
                    s_st.exp += 16

        with nc.Block() as blk:
            @blk.sync
            def _(e):
                e.dma_start(out=io["xn_hbm"].rearrange("(f p) t -> p f t", p=128),
                            in_=xn_own[:].rearrange("p (f t) -> p f t", f=NF)
                            ).then_inc(s_st.h, 16)
                s_st.exp += 16
                e.wait_ge(s_st.h, s_st.exp)


I32 = mybir.dt.int32


def emit_exchange(p, xmid_loc, xsh, fsh, xmid_lay):
    nc = p.nc
    s_st = p.nsem("st")
    s_ve = p.nsem("ve")
    with p.scope():
        onei = p.sb([1, 4], I32)
        with nc.Block() as blk:
            @blk.vector
            def _(e):
                s_ve.inc(e.memset(onei[:], 1))
            vt = s_ve.exp
            @blk.gpsimd
            def _(e):
                e.wait_ge(s_ve.h, vt)
                with (e.register("rf") as rf, e.register("rc") as rc,
                      e.register("rm") as rm):
                    base = s_st.exp
                    for arm in e.switch_core_id(n=128):
                        b = (arm.logical % 8) & 1
                        e.dma_start(out=xsh[b], in_=xmid_loc[:, :]).then_inc(s_st.h, 16)
                        e.wait_ge(s_st.h, base + 16)
                        e.dma_start(out=fsh[b:b + 1, :], in_=onei[:]).then_inc(s_st.h, 16)
                        e.wait_ge(s_st.h, base + 32)
                        e.load(rm, onei[0:1, 0:1])
                        e.reg_mov(rc, 1)
                        with e.While(rc):
                            e.load(rf, fsh[1 - b:2 - b, 0:1])
                            e.reg_sub(rc, rm, rf)
                        e.dma_start(out=xmid_lay[1 - b], in_=xsh[1 - b]).then_inc(s_st.h, 16)
                        e.dma_start(out=xmid_lay[b], in_=xmid_loc[:, :]).then_inc(s_st.h, 16)
                        e.wait_ge(s_st.h, base + 64)
                    s_st.exp = base + 64


def build_fused():
    p = P()
    nc = p.nc
    ext = {}
    for nm, shape, dt in [
        ("x_full", (H, S), F32), ("x_fm", (H, TPC), F32),
        ("Wq", (H, H), F32), ("Wk", (H, 1024), F32), ("Wv", (H, 1024), F32),
        ("WrT", (H, 64), F32), ("rbias", (1, 64), F32),
        ("cosT", (HD, TPC), F32), ("sinT", (HD, TPC), F32),
        ("cosF", (HD, S), F32), ("sinF", (HD, S), F32),
        ("ident", (128, 128), BF16), ("E8", (NKV, NH), F32),
        ("masks", (4 * 128, 128), BF16),
        ("Wo", (H, H), F32), ("ebias", (64, H), F32),
        ("wdownT", (H, 128), F32), ("bdown", (128, 1), F32),
        ("gate1", (1, 1), F32), ("indsel", (128, 2), F32),
        ("inwT", (128, 552), F32), ("convw_x", (128, 14), F32),
        ("convw_bc", (16, 14), F32), ("convb_x", (128, 2), F32),
        ("convb_bc", (16, 2), F32), ("dtb", (4, 1), F32),
        ("fcdT", (128, 8), F32), ("Dv", (4, 1), F32), ("normw", (128, 2), F32),
        ("outwT", (128, 256), F32), ("ind4", (4, 256), F32),
        ("WbT", (H, H), F32), ("bb32", (128, NF), F32), ("bu32", (128, NF), F32),
        ("wr2T", (H, 8), F32), ("A2", (H, 128), F32), ("Bm", (128, H), F32),
        ("wupT", (128, H), F32), ("ind16", (8, 128), BF16),
    ]:
        ext[nm] = p.inp(nm, shape, dt)
    o_out = p.out("out_fm", (H, TPC), F32)

    def scratch(nm, shape, dt, **kw):
        return nc.dram_tensor(nm, list(shape), dt, **kw).ap()

    xnfull = scratch("s_xnfull", (H, S), BF16)
    Qh = scratch("s_Q", (H, TPC), BF16)
    Kfull = scratch("s_K", (1024, S), BF16)
    Vtm = scratch("s_V", (S, 1024), BF16)
    wsum_hbm = scratch("s_wsum", (TPC, 1), F32)
    x1s = scratch("s_x1", (H, TPC), F32)
    xn1s = scratch("s_xn1", (H, TPC), BF16)
    xmid_loc = scratch("s_xmid", (128, TPC), BF16)
    xmid_lay = scratch("s_xmlay", (2, 128, TPC), BF16)
    xsh = scratch("sh_xmid", (2, 128, TPC), BF16, addr_space="Shared")
    fsh = scratch("sh_flag", (2, 4), I32, addr_space="Shared")

    xnown_hbm = scratch("s_xnown", (H, TPC), BF16)
    with p.scope():
        wde = p.sb([64, TPC], BF16)
        qmx = p.sb([1, NH], F32)
        kmax = p.sb([1, NKV], F32)

        with p.scope():
            ioA = dict(ext)
            ioA.update(xnfull=xnfull, Q_hbm=Qh, Kfull=Kfull, Vtmfull=Vtm,
                       wsum_hbm=wsum_hbm, kmax_sb=kmax,
                       qmx_sb=qmx, wde_sb=wde, xn_hbm=xnown_hbm)
            emit_A_fused(p, ioA)

        with p.scope():
            attnout = p.sb([128, NH * TPC], BF16)
            with p.scope():
                ioB = dict(Q=Qh, Kfull=Kfull, Vtmfull=Vtm, E8=ext["E8"],
                           masks=ext["masks"], attnout_sb=attnout,
                           qmx_sb=qmx, kmax_sb=kmax)
                build_phase_B(p, ioB)
            with p.scope():
                ioC = dict(Wo=ext["Wo"], x_fm=ext["x_fm"], ebias=ext["ebias"],
                           wdownT=ext["wdownT"], bdown=ext["bdown"],
                           o_x1=x1s, o_xn1=xn1s, o_xmid=xmid_loc,
                           attnout_sb=attnout, xn_hbm=xnown_hbm, wde_sb=wde,
                           wsum_hbm=wsum_hbm)
                build_phase_C(p, ioC)

    emit_exchange(p, xmid_loc, xsh, fsh, xmid_lay)

    with p.scope():
        ioD = dict(x1=x1s, xn1=xn1s, xm0=xmid_lay[0], xm1=xmid_lay[1],
                   gate1=ext["gate1"], indsel=ext["indsel"], inwT=ext["inwT"],
                   convw_x=ext["convw_x"], convw_bc=ext["convw_bc"],
                   convb_x=ext["convb_x"], convb_bc=ext["convb_bc"],
                   dtb=ext["dtb"], fcdT=ext["fcdT"], Dv=ext["Dv"],
                   normw=ext["normw"], outwT=ext["outwT"], ind4=ext["ind4"],
                   WbT=ext["WbT"], bb32=ext["bb32"], bu32=ext["bu32"],
                   wr2T=ext["wr2T"], A2=ext["A2"], Bm=ext["Bm"],
                   wupT=ext["wupT"], ind16=ext["ind16"], ident=ext["ident"],
                   o_out=o_out)
        build_phase_D(p, ioD)

    p.close()
    return nc


FUSED = True


def kernel_fused(inp):
    x = np.asarray(inp["L0_attn_hidden_states"])
    ident = np.eye(128).astype(bf16)
    metas = [core_meta(c) for c in range(NCORES)]
    cosF, sinF = rope_tables(np.arange(S))
    x_full_fm = [np.ascontiguousarray(x[bb].T) for bb in range(2)]
    cw_ = np.asarray(inp["L1_attn_hy_conv_w"])[:, 0, :]
    cb_ = np.asarray(inp["L1_attn_hy_conv_b"])
    fcd = np.asarray(inp["L1_attn_hy_fcD_w"])
    outw = np.asarray(inp["L1_attn_hy_out_w"])
    Bten = np.asarray(inp["L1_ffn_B"])
    ind4 = np.zeros((4, 256), np.float32)
    for col in range(256):
        feat = (col // 128) * 128 + (col % 128)
        ind4[feat // 64, col] = 1.0
    ind16 = np.zeros((8, 128), np.float32)
    for pp in range(128):
        ind16[pp // 16, pp] = 1.0
    E8 = np.repeat(np.eye(NKV, dtype=np.float32), 4, axis=1)
    nwv = np.asarray(inp["L1_attn_hy_norm_w"])
    common = {
        "Wq": np.asarray(inp["L0_attn_Wq"]), "Wk": np.asarray(inp["L0_attn_Wk"]),
        "Wv": np.asarray(inp["L0_attn_Wv"]),
        "WrT": np.ascontiguousarray(np.asarray(inp["L0_ffn_router_weight"]).T),
        "rbias": np.ascontiguousarray(np.asarray(inp["L0_ffn_router_bias"])[None, :]),
        "cosF": cosF, "sinF": sinF, "ident": ident, "E8": E8,
        "Wo": np.asarray(inp["L0_attn_Wo"]),
        "ebias": np.asarray(inp["L0_ffn_expert_bias"]),
        "wdownT": np.ascontiguousarray(np.asarray(inp["L1_attn_w_down"]).T),
        "bdown": np.ascontiguousarray(np.asarray(inp["L1_attn_b_down"])[:, None]),
        "gate1": np.asarray(inp["L1_attn_gate1"], np.float32).reshape(1, 1),
        "inwT": np.ascontiguousarray(np.asarray(inp["L1_attn_hy_in_w"]).T),
        "convw_x": np.ascontiguousarray(np.concatenate([cw_[0:128], cw_[128:256]], 1)),
        "convw_bc": np.ascontiguousarray(np.concatenate([cw_[256:272], cw_[272:288]], 1)),
        "convb_x": np.ascontiguousarray(np.stack([cb_[0:128], cb_[128:256]], 1)),
        "convb_bc": np.ascontiguousarray(np.stack([cb_[256:272], cb_[272:288]], 1)),
        "dtb": np.ascontiguousarray(np.asarray(inp["L1_attn_hy_dt_bias"])[:, None]),
        "fcdT": np.ascontiguousarray(np.concatenate([fcd.T[0:128], fcd.T[128:256]], 1)),
        "Dv": np.ascontiguousarray(np.asarray(inp["L1_attn_hy_D"])[:, None]),
        "normw": np.ascontiguousarray(np.stack([nwv[0:128], nwv[128:256]], 1)),
        "outwT": np.ascontiguousarray(np.concatenate([outw.T[0:128], outw.T[128:256]], 1)),
        "ind4": ind4,
        "WbT": np.ascontiguousarray(np.asarray(inp["L1_ffn_W_base"]).T),
        "bb32": np.ascontiguousarray(np.asarray(inp["L1_ffn_b_base"]).reshape(NF, 128).T),
        "bu32": np.ascontiguousarray(np.asarray(inp["L1_attn_b_up"]).reshape(NF, 128).T),
        "wr2T": np.ascontiguousarray(np.asarray(inp["L1_ffn_W_router"]).T),
        "A2": np.ascontiguousarray(np.asarray(inp["L1_ffn_A"]).reshape(128, H).T),
        "Bm": np.ascontiguousarray(Bten.transpose(0, 2, 1).reshape(128, H)),
        "wupT": np.ascontiguousarray(np.asarray(inp["L1_attn_w_up"]).T),
        "ind16": ind16.astype(bf16),
    }
    _ = np.asarray(inp["L1_attn_hy_A_log"])
    in_maps = []
    for c in range(NCORES):
        b, mq, rows = metas[c]
        cos, sin = rope_tables(rows)
        indsel = np.zeros((128, 2), np.float32)
        indsel[:, 0] = 1.0 if b == 0 else 0.0
        indsel[:, 1] = 1.0 - indsel[:, 0]
        m = {"x_full": x_full_fm[b],
             "x_fm": np.ascontiguousarray(x[b, rows, :].T),
             "cosT": cos, "sinT": sin,
             "masks": attn_masks(mq), "indsel": indsel}
        m.update(common)
        in_maps.append(m)
    res = run_bass_kernel_spmd(_get_nc("F", build_fused), in_maps,
                               core_ids=list(range(NCORES))).results
    out = np.zeros((B, S, H), np.float32)
    for c in range(NCORES):
        b, mq, rows = metas[c]
        out[b, rows, :] = res[c]["out_fm"].T
    return out


# version 22 (final)
# speedup vs baseline: 1.0730x; 1.0730x over previous
import numpy as np
import ml_dtypes
from contextlib import ExitStack

import concourse.bass as bass
import concourse.mybir as mybir
from concourse.bass_utils import run_bass_kernel_spmd

F32 = mybir.dt.float32
BF16 = mybir.dt.bfloat16
AF = mybir.ActivationFunctionType
ALU = mybir.AluOpType
AX = mybir.AxisListType
bf16 = ml_dtypes.bfloat16

NCORES = 8
H = 4096
S = 4096
B = 2
TPC = 1024
NH = 32
NKV = 8
HD = 128
NF = H // 128
NT = TPC // 512
SCALE = float(1.0 / np.sqrt(128.0))


def core_meta(c):
    b = c & 1
    mq = c >> 1
    rows = np.array([128 * (4 * li + mq) + r for li in range(8) for r in range(128)])
    return b, mq, rows


def rope_tables(rows):
    inv_freq = 1.0 / (10000.0 ** (np.arange(0, 128, 2, dtype=np.float64) / 128.0))
    ang = rows[None, :].astype(np.float64) * np.concatenate([inv_freq, inv_freq])[:, None]
    cos = np.cos(ang).astype(np.float32)
    sin = np.sin(ang).astype(np.float32)
    sin[0:64] *= -1.0
    return np.ascontiguousarray(cos), np.ascontiguousarray(sin)


def attn_masks(mq):
    m = np.zeros((4, 128, 128), np.float32)
    for kloc in range(4):
        if kloc < mq:
            m[kloc] = 1.0
        elif kloc == mq:
            kr = np.arange(128)[:, None]
            r = np.arange(128)[None, :]
            m[kloc] = (kr <= r).astype(np.float32)
    return np.ascontiguousarray(m.reshape(4 * 128, 128)).astype(bf16)


class Sem:
    __slots__ = ("h", "exp")

    def __init__(self, h):
        self.h = h
        self.exp = 0

    def inc(self, instr, n=1):
        instr.then_inc(self.h, n)
        self.exp += n
        return self.exp

    def fence(self, e, instr):
        self.inc(instr)
        e.wait_ge(self.h, self.exp)


class P:
    def __init__(self):
        self.nc = bass.Bass("TRN2", target_bir_lowering=False)
        self.es = ExitStack()
        self._scopes = [self.es]
        self._nsems = {}
        self._n = 0

    def scope(self):
        from contextlib import contextmanager

        @contextmanager
        def _s():
            st = ExitStack()
            self._scopes.append(st)
            try:
                yield
            finally:
                self._scopes.pop()
                st.close()
        return _s()

    def nsem(self, name):
        if name not in self._nsems:
            self._nsems[name] = self.sem()
        return self._nsems[name]

    def name(self, p):
        self._n += 1
        return f"{p}{self._n}"

    def sb(self, shape, dtype):
        return self._scopes[-1].enter_context(
            self.nc.sbuf_tensor(self.name("sb"), shape, dtype))

    def ps(self, shape, dtype=F32):
        return self._scopes[-1].enter_context(
            self.nc.psum_tensor(self.name("ps"), shape, dtype))

    def sem(self):
        return Sem(self.es.enter_context(self.nc.semaphore(self.name("sem"))))

    def inp(self, name, shape, dtype):
        return self.nc.dram_tensor(name, list(shape), dtype, kind="ExternalInput").ap()

    def out(self, name, shape, dtype):
        return self.nc.dram_tensor(name, list(shape), dtype, kind="ExternalOutput").ap()

    def close(self):
        self.es.close()


def wchunks(W):
    return W.rearrange("(f p) o -> p f o", p=128)


_A_STOP = None


def build_phase_A():
    p = P()
    nc = p.nc
    x = p.inp("x_fm", (H, TPC), F32)
    Wq = p.inp("Wq", (H, H), F32)
    Wk = p.inp("Wk", (H, 1024), F32)
    Wv = p.inp("Wv", (H, 1024), F32)
    WrT = p.inp("WrT", (H, 64), F32)
    rbias = p.inp("rbias", (1, 64), F32)
    cosT = p.inp("cosT", (HD, TPC), F32)
    sinT = p.inp("sinT", (HD, TPC), F32)
    identw = p.inp("ident", (128, 128), BF16)
    o_xn = p.out("xn", (H, TPC), BF16)
    o_q = p.out("Q", (H, TPC), BF16)
    o_k = p.out("K", (1024, TPC), BF16)
    o_v = p.out("Vtm", (TPC, 1024), BF16)
    o_kn = p.out("knormsq", (NKV, TPC), F32)
    o_qm = p.out("qmaxsq", (1, NH), F32)
    o_ws = p.out("wsum", (TPC, 1), F32)
    o_wd = p.out("wdenseT", (64, TPC), BF16)

    xn = p.sb([128, NF * TPC], BF16)
    xf32 = p.sb([128, NF * 512], F32)
    ones1 = p.sb([128, 1], BF16)
    ones1f = p.sb([128, 1], F32)
    ones64 = p.sb([128, 64], F32)
    onesr = p.sb([1, 128], F32)
    ident = p.sb([128, 128], BF16)
    cos_s = p.sb([128, TPC], F32)
    sin_s = p.sb([128, TPC], F32)
    stat = p.sb([1, 512], F32)
    mrow = p.sb([1, 512], F32)
    rrow = p.sb([1, 512], F32)
    mB = p.sb([128, 512], F32)
    rB = p.sb([128, 512], F32)
    sq = [p.sb([128, 512], F32) for _ in range(2)]
    band = p.sb([128, NF * 128], BF16)
    qf = p.sb([128, 512], F32)
    ta = p.sb([128, 512], F32)
    tb = p.sb([128, 512], F32)
    qbf = p.sb([128, 512], BF16)
    vstage = p.sb([128, 512], BF16)
    qn = p.sb([1, 512], F32)
    qmx = p.sb([1, NH], F32)
    knrow = p.sb([1, 512], F32)

    s_ld = p.sem()
    s_wb = p.sem()
    s_mm = p.sem()
    s_ve = p.sem()
    s_sc = p.sem()
    s_st = p.sem()

    ps_sum = p.ps([128, 512])[0:1, :]
    ps_sq = p.ps([128, 512])[0:1, :]

    def _finish():
        with nc.Block() as blk:
            @blk.sync
            def _(e):
                e.wait_ge(s_st.h, s_st.exp)
        p.close()
        return nc
    _ps_b_t = p.ps([128, 512])
    ps_b = _ps_b_t[:]
    ps_b_64 = _ps_b_t[:, 0:64]
    ps_mm = p.ps([128, 512])[:]
    ps_n = p.ps([128, 512])[0:1, :]
    _ps_tb_t = p.ps([128, 1024], BF16)
    ps_tb = _ps_tb_t[:, 0:512]
    ps_tb_k4 = [_ps_tb_t[:, k * 128:(k + 1) * 128] for k in range(4)]

    with nc.Block() as blk:
        @blk.sync
        def _(e):
            e.dma_start(out=cos_s[:], in_=cosT[:, :]).then_inc(s_ld.h, 16)
            e.dma_start(out=sin_s[:], in_=sinT[:, :]).then_inc(s_ld.h, 16)
        @blk.scalar
        def _(e):
            e.dma_start(out=ident[:], in_=identw[:, :]).then_inc(s_ld.h, 16)
        @blk.vector
        def _(e):
            e.memset(ones1[:], 1.0)
            e.memset(ones1f[:], 1.0)
            e.memset(ones64[:], 1.0)
            e.memset(onesr[:], 1.0)
            e.memset(qmx[:], 0.0)
        s_ld.exp += 48

    for T in range(NT):
        with nc.Block() as blk:
            @blk.sync
            def _(e):
                e.dma_start(
                    out=xf32[:].rearrange("p (f t) -> p f t", f=NF),
                    in_=x.rearrange("(f p) t -> p f t", p=128)[:, :, T * 512:(T + 1) * 512],
                ).then_inc(s_ld.h, 16)
            s_ld.exp += 16
            ldt = s_ld.exp

            @blk.scalar
            def _(e):
                e.wait_ge(s_ld.h, ldt)
                base_mm = s_mm.exp
                for f in range(NF):
                    if f >= 2:
                        e.wait_ge(s_mm.h, base_mm + f - 1)
                    s_sc.inc(e.square(out=sq[f % 2][:], in_=xf32[:, f * 512:(f + 1) * 512]))

            @blk.tensor
            def _(e):
                e.wait_ge(s_ld.h, ldt)
                base_sc = s_sc.exp - NF
                for f in range(NF):
                    e.matmul(ps_sum, ones1f[:], xf32[:, f * 512:(f + 1) * 512],
                             start=(f == 0), stop=(f == NF - 1))
                    e.wait_ge(s_sc.h, base_sc + f + 1)
                    s_mm.inc(e.matmul(ps_sq, ones1f[:], sq[f % 2][:],
                             start=(f == 0), stop=(f == NF - 1)))

        with nc.Block() as blk:
            @blk.vector
            def _(e):
                e.tensor_scalar(out=mrow[:], in0=ps_sum[0:1, :], scalar1=1.0 / H,
                                scalar2=None, op0=ALU.mult)
                e.tensor_scalar(out=stat[:], in0=ps_sq, scalar1=1.0 / H,
                                scalar2=None, op0=ALU.mult)
                e.tensor_tensor(out=rrow[:], in0=mrow[:], in1=mrow[:], op=ALU.mult)
                e.tensor_tensor(out=stat[:], in0=stat[:], in1=rrow[:], op=ALU.subtract)
                s_ve.inc(e.tensor_scalar(out=stat[:], in0=stat[:], scalar1=1e-5,
                                scalar2=None, op0=ALU.add))
            vt = s_ve.exp
            @blk.scalar
            def _(e):
                e.wait_ge(s_ve.h, vt)
                s_sc.fence(e, e.activation(out=stat[:], in_=stat[:], func=AF.Ln))
                s_sc.inc(e.activation(out=rrow[:], in_=stat[:], func=AF.Exp, scale=-0.5))
            st_ = s_sc.exp
            @blk.tensor
            def _(e):
                e.wait_ge(s_sc.h, st_)
                s_mm.inc(e.matmul(ps_b, onesr[:], mrow[:], start=True, stop=True))
            mt = s_mm.exp
            @blk.vector
            def _(e):
                e.wait_ge(s_mm.h, mt)
                s_ve.inc(e.tensor_copy(out=mB[:], in_=ps_b))
            vt2 = s_ve.exp
            @blk.tensor
            def _(e):
                e.wait_ge(s_ve.h, vt2)
                s_mm.inc(e.matmul(ps_b, onesr[:], rrow[:], start=True, stop=True))
            mt2 = s_mm.exp
            @blk.vector
            def _(e):
                e.wait_ge(s_mm.h, mt2)
                e.tensor_copy(out=rB[:], in_=ps_b)
                for f in range(NF):
                    e.tensor_tensor(out=sq[0][:], in0=xf32[:, f * 512:(f + 1) * 512],
                                    in1=mB[:], op=ALU.subtract)
                    last = e.tensor_tensor(
                        out=xn[:, f * 1024 + T * 512: f * 1024 + T * 512 + 512],
                        in0=sq[0][:], in1=rB[:], op=ALU.mult)
                s_ve.inc(last)

    with nc.Block() as blk:
        @blk.sync
        def _(e):
            e.dma_start(out=o_xn.rearrange("(f p) t -> p f t", p=128),
                        in_=xn[:].rearrange("p (f t) -> p f t", f=NF)).then_inc(s_st.h, 16)
            s_st.exp += 16

    if _A_STOP == "ln":
        return _finish()

    def proj(W, o, do_rope, do_norm, norm_out_ap, store_fn):
        with nc.Block() as blk:
            @blk.gpsimd
            def _(e):
                e.dma_start(out=band[:].rearrange("p (f o) -> p f o", o=128),
                            in_=wchunks(W)[:, :, o * 128:(o + 1) * 128]
                            ).then_inc(s_wb.h, 16)
            s_wb.exp += 16
            wbt = s_wb.exp
            for T in range(NT):
                @blk.tensor
                def _(e):
                    e.wait_ge(s_wb.h, wbt)
                    if T == 1:
                        e.wait_ge(s_ve.h, s_ve.exp)
                        e.wait_ge(s_sc.h, s_sc.exp)
                    for f in range(NF):
                        mm = e.matmul(ps_mm, band[:, f * 128:(f + 1) * 128],
                                 xn[:, f * 1024 + T * 512: f * 1024 + T * 512 + 512],
                                 start=(f == 0), stop=(f == NF - 1))
                    s_mm.inc(mm)
                mt = s_mm.exp

                if do_norm:
                    @blk.scalar
                    def _(e):
                        e.wait_ge(s_mm.h, mt)
                        s_sc.inc(e.square(out=sq[0][:], in_=ps_mm))
                    st1 = s_sc.exp
                    @blk.tensor
                    def _(e):
                        e.wait_ge(s_sc.h, st1)
                        s_mm.inc(e.matmul(ps_n, ones1f[:], sq[0][:], start=True, stop=True))
                    mt2 = s_mm.exp
                    @blk.vector
                    def _(e):
                        e.wait_ge(s_mm.h, mt2)
                        s_ve.inc(norm_out_ap(e, T))

                stc = s_st.exp
                @blk.vector
                def _(e, stc=stc):
                    e.wait_ge(s_mm.h, mt)
                    e.wait_ge(s_st.h, stc)
                    e.tensor_copy(out=qf[:], in_=ps_mm)
                    if do_rope:
                        cs = cos_s[:, T * 512:(T + 1) * 512]
                        sn = sin_s[:, T * 512:(T + 1) * 512]
                        e.tensor_copy(out=tb[0:64, :], in_=qf[64:128, :])
                        e.tensor_copy(out=tb[64:128, :], in_=qf[0:64, :])
                        e.tensor_tensor(out=ta[:], in0=qf[:], in1=cs, op=ALU.mult)
                        e.tensor_tensor(out=tb[:], in0=tb[:], in1=sn, op=ALU.mult)
                        last = e.tensor_tensor(out=qbf[:], in0=ta[:], in1=tb[:], op=ALU.add)
                    else:
                        last = e.tensor_copy(out=vstage[:], in_=ps_mm)
                    s_ve.inc(last)
                vt = s_ve.exp
                store_fn(blk, T, vt)

    def q_norm(e, T, o=None):
        s_ve.fence(e, e.tensor_copy(out=qn[:], in_=ps_n))
        s_ve.fence(e, e.reduce_max(out=qn[:, 1:2], in_=qn[:], axis=AX.X))
        return e.tensor_tensor(out=qmx[:, q_norm.o:q_norm.o + 1],
                        in0=qmx[:, q_norm.o:q_norm.o + 1],
                        in1=qn[:, 1:2], op=ALU.max)

    for o in range(NH):
        q_norm.o = o

        def q_store(blk, T, vt, o=o):
            @blk.sync
            def _(e):
                e.wait_ge(s_ve.h, vt)
                e.dma_start(out=o_q[o * 128:(o + 1) * 128, T * 512:(T + 1) * 512],
                            in_=qbf[:]).then_inc(s_st.h, 16)
            s_st.exp += 16

        proj(Wq, o, do_rope=True, do_norm=True, norm_out_ap=q_norm, store_fn=q_store)

    if _A_STOP == "q":
        return _finish()

    for g in range(NKV):
        def k_norm(e, T, g=g):
            return e.tensor_copy(out=knrow[:], in_=ps_n)

        def k_store(blk, T, vt, g=g):
            @blk.sync
            def _(e):
                e.wait_ge(s_ve.h, vt)
                e.dma_start(out=o_k[g * 128:(g + 1) * 128, T * 512:(T + 1) * 512],
                            in_=qbf[:]).then_inc(s_st.h, 16)
                e.dma_start(out=o_kn[g:g + 1, T * 512:(T + 1) * 512],
                            in_=knrow[:]).then_inc(s_st.h, 16)
            s_st.exp += 32

        proj(Wk, g, do_rope=True, do_norm=True, norm_out_ap=k_norm, store_fn=k_store)

    if _A_STOP == "k":
        return _finish()

    for g in range(NKV):
        def v_store(blk, T, vt, g=g):
            @blk.tensor
            def _(e):
                e.wait_ge(s_ve.h, vt)
                for k4 in range(4):
                    mm = e.transpose(ps_tb_k4[k4],
                                vstage[:, k4 * 128:(k4 + 1) * 128], ident[:])
                s_mm.inc(mm)
            mt3 = s_mm.exp
            stc2 = s_st.exp
            @blk.vector
            def _(e, stc2=stc2):
                e.wait_ge(s_mm.h, mt3)
                e.wait_ge(s_st.h, stc2)
                s_ve.inc(e.tensor_copy(out=qbf[:], in_=ps_tb))
            vt3 = s_ve.exp
            @blk.sync
            def _(e):
                e.wait_ge(s_ve.h, vt3)
                e.dma_start(
                    out=o_v[T * 512:(T + 1) * 512, g * 128:(g + 1) * 128]
                    .rearrange("(k p) d -> p k d", p=128),
                    in_=qbf[:].rearrange("p (k d) -> p k d", k=4),
                ).then_inc(s_st.h, 16)
            s_st.exp += 16

        proj(Wv, g, do_rope=False, do_norm=False, norm_out_ap=None, store_fn=v_store)

    if _A_STOP == "v":
        return _finish()

    wr_full = p.sb([128, NF * 64], BF16)
    rb_row = p.sb([1, 64], F32)
    rbB = p.sb([128, 64], F32)
    probs = p.sb([128, 64], F32)
    work = p.sb([128, 64], F32)
    eqm = p.sb([128, 64], F32)
    mask8 = p.sb([128, 64], F32)
    wsv = p.sb([128, 1], F32)
    mrow1 = p.sb([128, 1], F32)
    zr = p.sb([128, 1], F32)
    wde = p.sb([128, 64], BF16)
    wdeT = p.sb([64, 128], BF16)
    ps_r = p.ps([128, 512])[:, 0:64]
    ps_wt = p.ps([128, 1024], BF16)[0:64, 0:128]

    with nc.Block() as blk:
        @blk.gpsimd
        def _(e):
            e.dma_start(out=wr_full[:].rearrange("p (f o) -> p f o", o=64),
                        in_=wchunks(WrT)).then_inc(s_wb.h, 16)
        @blk.sync
        def _(e):
            e.dma_start(out=rb_row[:], in_=rbias[:, :]).then_inc(s_wb.h, 16)
        s_wb.exp += 32
        wbr = s_wb.exp
        @blk.tensor
        def _(e):
            e.wait_ge(s_wb.h, wbr)
            s_mm.inc(e.matmul(ps_b_64, onesr[:], rb_row[:], start=True, stop=True))
        mt = s_mm.exp
        @blk.vector
        def _(e):
            e.wait_ge(s_mm.h, mt)
            s_ve.inc(e.tensor_copy(out=rbB[:], in_=ps_b_64))

    for ts in range(8):
        with nc.Block() as blk:
            @blk.tensor
            def _(e):
                for f in range(NF):
                    mm = e.matmul(ps_r,
                             xn[:, f * 1024 + ts * 128: f * 1024 + ts * 128 + 128],
                             wr_full[:, f * 64:(f + 1) * 64],
                             start=(f == 0), stop=(f == NF - 1))
                s_mm.inc(mm)
            mt = s_mm.exp
            @blk.vector
            def _(e):
                e.wait_ge(s_mm.h, mt)
                s_ve.fence(e, e.tensor_tensor(out=probs[:], in0=ps_r, in1=rbB[:], op=ALU.add))
                s_ve.inc(e.reduce_max(out=mrow1[:], in_=probs[:], axis=AX.X, negate=True))
            vt = s_ve.exp
            @blk.scalar
            def _(e):
                e.wait_ge(s_ve.h, vt)
                act = e.activation(out=probs[:], in_=probs[:], func=AF.Exp,
                             bias=mrow1[:])
                s_sc.inc(act)
            st2 = s_sc.exp
            @blk.vector
            def _(e):
                e.wait_ge(s_sc.h, st2)
                s_ve.fence(e, e.reduce_sum(out=zr[:], in_=probs[:], axis=AX.X))
                s_ve.fence(e, e.reciprocal(out=zr[:], in_=zr[:]))
                e.memset(mask8[:], 0.0)
                e.memset(wsv[:], 0.0)
                s_ve.fence(e, e.tensor_copy(out=work[:], in_=probs[:]))
                for it in range(8):
                    s_ve.fence(e, e.reduce_max(out=mrow1[:], in_=work[:], axis=AX.X))
                    s_ve.fence(e, e.scalar_tensor_tensor(out=eqm[:], in0=work[:], scalar=mrow1[:],
                                           in1=ones64[:], op0=ALU.is_ge, op1=ALU.mult))
                    e.tensor_tensor(out=mask8[:], in0=mask8[:], in1=eqm[:], op=ALU.max)
                    e.scalar_tensor_tensor(out=wsv[:], in0=mrow1[:], scalar=0.0,
                                           in1=wsv[:], op0=ALU.max, op1=ALU.add)
                    s_ve.fence(e, e.scalar_tensor_tensor(out=work[:], in0=eqm[:], scalar=-1e30,
                                           in1=work[:], op0=ALU.mult, op1=ALU.add))
                s_ve.fence(e, e.tensor_tensor(out=wsv[:], in0=wsv[:], in1=zr[:], op=ALU.mult))
                s_ve.inc(e.scalar_tensor_tensor(out=wde[:], in0=mask8[:], scalar=zr[:],
                                                in1=probs[:], op0=ALU.mult, op1=ALU.mult))
            vt2 = s_ve.exp
            @blk.tensor
            def _(e):
                e.wait_ge(s_ve.h, vt2)
                s_mm.inc(e.transpose(ps_wt, wde[:, 0:64], ident[:]))
            mt2 = s_mm.exp
            @blk.vector
            def _(e):
                e.wait_ge(s_mm.h, mt2)
                s_ve.inc(e.tensor_copy(out=wdeT[:], in_=ps_wt))
            vt3 = s_ve.exp
            @blk.sync
            def _(e):
                e.wait_ge(s_ve.h, vt3)
                e.dma_start(out=o_ws[ts * 128:(ts + 1) * 128, :],
                            in_=wsv[:]).then_inc(s_st.h, 16)
                e.dma_start(out=o_wd[0:64, ts * 128:(ts + 1) * 128],
                            in_=wdeT[:]).then_inc(s_st.h, 16)
            s_st.exp += 32

    with nc.Block() as blk:
        @blk.sync
        def _(e):
            e.dma_start(out=o_qm[:, :], in_=qmx[:]).then_inc(s_st.h, 16)
            s_st.exp += 16
            e.wait_ge(s_st.h, s_st.exp)

    p.close()
    return nc


def build_phase_B(p=None, io=None):
    standalone = p is None
    if standalone:
        p = P()
        io = {}
    nc = p.nc
    Qf = io.get("Q") or p.inp("Q", (H, TPC), BF16)
    Kf = io.get("Kfull") or p.inp("Kfull", (1024, S), BF16)
    Vf = io.get("Vtmfull") or p.inp("Vtmfull", (S, 1024), BF16)
    e8 = io.get("E8") or p.inp("E8", (NKV, NH), F32)
    maskw = io.get("masks") or p.inp("masks", (4 * 128, 128), BF16)
    if standalone:
        knf = p.inp("knormsq_full", (NKV, S), F32)
        qmf = p.inp("qmaxsq", (1, NH), F32)
        o_ao = p.out("attnout", (H, TPC), BF16)

    attnout = io.get("attnout_sb")
    if attnout is None:
        attnout = p.sb([128, NH * TPC], BF16)
    kg = p.sb([128, S], BF16)
    vg = p.sb([128, S], BF16)
    qh = p.sb([128, TPC], BF16)
    attnT = [p.sb([128, TPC], BF16) for _ in range(2)]
    maskt = p.sb([128, 4 * 128], BF16)
    knrows = p.sb([NKV, S], F32) if standalone else None
    kx8 = p.sb([NKV, 1], F32)
    qmr = io.get("qmx_sb")
    if qmr is None:
        qmr = p.sb([1, NH], F32)
    negm = p.sb([1, NH], F32)
    negMB = p.sb([128, NH], F32)
    e8s = p.sb([NKV, NH], F32)
    ones1 = p.sb([128, 1], BF16)
    onesr = p.sb([1, 128], F32)
    zrow = p.sb([1, TPC], F32)
    zB = p.sb([128, TPC], F32)

    s_ld = p.nsem("ld")
    s_qk = p.nsem("qk")
    s_zav = p.nsem("zav")
    s_sc = p.nsem("sc")
    s_ve = p.nsem("ve")
    s_mmx = p.nsem("mm")
    s_st = p.nsem("st")

    ps_s = [p.ps([128, 512]) for _ in range(4)]
    ps_av = p.ps([128, 1024])
    ps_z = p.ps([128, 1024])

    def s_half(par, j):
        return ps_s[2 * par + j]

    with nc.Block() as blk:
        @blk.sync
        def _(e):
            e.dma_start(out=maskt[:].rearrange("p (a q) -> p a q", a=4),
                        in_=maskw.rearrange("(a p) q -> p a q", p=128)).then_inc(s_ld.h, 16)
            e.dma_start(out=e8s[:], in_=e8[:, :]).then_inc(s_ld.h, 16)
            s_ld.exp += 32
            if standalone:
                e.dma_start(out=knrows[:], in_=knf[:, :]).then_inc(s_ld.h, 16)
                e.dma_start(out=qmr[:], in_=qmf[:, :]).then_inc(s_ld.h, 16)
                s_ld.exp += 32
        ldt = s_ld.exp
        @blk.vector
        def _(e):
            e.memset(ones1[:], 1.0)
            e.memset(onesr[:], 1.0)
            e.wait_ge(s_ld.h, ldt)
            if standalone:
                s_ve.fence(e, e.reduce_max(out=kx8[:], in_=knrows[:], axis=AX.X))
            s_ve.inc(e.nop())
        vt = s_ve.exp
        if not standalone:
            @blk.tensor
            def _(e):
                e.wait_ge(s_ve.h, vt)
                s_mmx.inc(e.matmul(ps_av[0:NKV, 0:1], io["kmax_sb"][:, :],
                                   onesr[0:1, 0:1], start=True, stop=True))
            mtx = s_mmx.exp
            @blk.vector
            def _(e, mtx=mtx):
                e.wait_ge(s_mmx.h, mtx)
                s_ve.inc(e.tensor_copy(out=kx8[:], in_=ps_av[0:NKV, 0:1]))
            vt = s_ve.exp
        @blk.tensor
        def _(e):
            e.wait_ge(s_ve.h, vt)
            s_mmx.inc(e.matmul(ps_z[0:1, 0:NH], kx8[:], e8s[:], start=True, stop=True))
        mt = s_mmx.exp
        @blk.vector
        def _(e):
            e.wait_ge(s_mmx.h, mt)
            s_ve.fence(e, e.tensor_copy(out=negm[:], in_=ps_z[0:1, 0:NH]))
            s_ve.inc(e.tensor_tensor(out=negm[:], in0=negm[:], in1=qmr[:], op=ALU.mult))
        vt2 = s_ve.exp
        @blk.scalar
        def _(e):
            e.wait_ge(s_ve.h, vt2)
            s_sc.fence(e, e.activation(out=negm[:], in_=negm[:], func=AF.Ln))
            s_sc.inc(e.activation(out=negm[:], in_=negm[:], func=AF.Exp, scale=0.5))
        st1 = s_sc.exp
        @blk.vector
        def _(e):
            e.wait_ge(s_sc.h, st1)
            s_ve.inc(e.tensor_scalar(out=negm[:], in0=negm[:], scalar1=-SCALE,
                                     scalar2=None, op0=ALU.mult))
        vt3 = s_ve.exp
        @blk.tensor
        def _(e):
            e.wait_ge(s_ve.h, vt3)
            s_mmx.inc(e.matmul(ps_z[:, 0:NH], onesr[:], negm[:], start=True, stop=True))
        mt2 = s_mmx.exp
        @blk.vector
        def _(e):
            e.wait_ge(s_mmx.h, mt2)
            s_ve.inc(e.tensor_copy(out=negMB[:], in_=ps_z[:, 0:NH]))

    for g in range(NKV):
        with nc.Block() as blk:
            @blk.sync
            def _(e):
                e.dma_start(out=kg[:], in_=Kf[g * 128:(g + 1) * 128, :]).then_inc(s_ld.h, 16)
                e.dma_start(
                    out=vg[:].rearrange("p (k d) -> p k d", d=128),
                    in_=Vf.rearrange("(k p) gd -> p k gd", p=128)[:, :, g * 128:(g + 1) * 128],
                ).then_inc(s_ld.h, 16)
            s_ld.exp += 32

        for hl in range(4):
            h = 4 * g + hl
            with nc.Block() as blk:
                @blk.scalar
                def _(e):
                    e.dma_start(
                        out=qh[:].rearrange("p (l r) -> p l r", l=8),
                        in_=Qf[h * 128:(h + 1) * 128, :]
                        .rearrange("p (l r) -> p l r", l=8)[:, ::-1, :],
                    ).then_inc(s_ld.h, 16)
                s_ld.exp += 16
                ldq = s_ld.exp

                base_sc = s_sc.exp
                base_ve = s_ve.exp
                base_qk = s_qk.exp
                base_zav = s_zav.exp

                def emit_zav(e, kz):
                    parz = kz % 2
                    wz = (8 - kz // 4) * 128
                    for j in range((wz + 511) // 512):
                        w = min(512, wz - j * 512)
                        e.matmul(ps_z[0:1, j * 512:j * 512 + w], ones1[:],
                                 attnT[parz][:, j * 512:j * 512 + w],
                                 start=(kz == 0),
                                 stop=(kz == (31 if j == 0 else 15)))
                        mm = e.matmul(ps_av[:, j * 512:j * 512 + w],
                                      vg[:, kz * 128:(kz + 1) * 128],
                                      attnT[parz][:, j * 512:j * 512 + w],
                                      start=(kz == 0),
                                      stop=(kz == (31 if j == 0 else 15)))
                    s_zav.inc(mm)

                @blk.tensor
                def _(e):
                    e.wait_ge(s_ld.h, ldq)
                    for kap in range(32):
                        par = kap % 2
                        wq = (8 - kap // 4) * 128
                        if kap >= 2:
                            e.wait_ge(s_sc.h, base_sc + kap - 1)
                        for j in range((wq + 511) // 512):
                            w = min(512, wq - j * 512)
                            mm = e.matmul(s_half(par, j)[:, 0:w],
                                          kg[:, kap * 128:(kap + 1) * 128],
                                          qh[:, j * 512:j * 512 + w],
                                          start=True, stop=True)
                        s_qk.inc(mm)
                        if kap >= 1:
                            kz = kap - 1
                            e.wait_ge(s_ve.h, base_ve + kz + 1)
                            emit_zav(e, kz)
                    e.wait_ge(s_ve.h, base_ve + 32)
                    emit_zav(e, 31)

                @blk.scalar
                def _(e):
                    for kap in range(32):
                        par = kap % 2
                        wq = (8 - kap // 4) * 128
                        e.wait_ge(s_qk.h, base_qk + kap + 1)
                        if kap >= 2:
                            e.wait_ge(s_zav.h, base_zav + kap - 1)
                        for j in range((wq + 511) // 512):
                            w = min(512, wq - j * 512)
                            act = e.activation(out=attnT[par][:, j * 512:j * 512 + w],
                                               in_=s_half(par, j)[:, 0:w],
                                               func=AF.Exp, scale=SCALE,
                                               bias=negMB[:, h:h + 1])
                        s_sc.inc(act)

                @blk.vector
                def _(e):
                    for kap in range(32):
                        par = kap % 2
                        wq = (8 - kap // 4) * 128
                        e.wait_ge(s_sc.h, base_sc + kap + 1)
                        s_ve.inc(e.tensor_tensor(
                            out=attnT[par][:, wq - 128:wq],
                            in0=attnT[par][:, wq - 128:wq],
                            in1=maskt[:, (kap % 4) * 128:(kap % 4 + 1) * 128],
                            op=ALU.mult))

            with nc.Block() as blk:
                @blk.vector
                def _(e):
                    s_ve.fence(e, e.tensor_copy(out=zrow[:], in_=ps_z[0:1, :]))
                    s_ve.inc(e.reciprocal(out=zrow[:], in_=zrow[:]))
                vt = s_ve.exp
                @blk.tensor
                def _(e):
                    e.wait_ge(s_ve.h, vt)
                    for j in range(2):
                        mm = e.matmul(ps_z[:, j * 512:(j + 1) * 512], onesr[:],
                                      zrow[:, j * 512:(j + 1) * 512], start=True, stop=True)
                    s_mmx.inc(mm)
                mt = s_mmx.exp
                @blk.vector
                def _(e):
                    e.wait_ge(s_mmx.h, mt)
                    s_ve.fence(e, e.tensor_copy(out=zB[:], in_=ps_z[:]))
                    for lb in range(8):
                        mmv = e.tensor_tensor(
                            out=attnout[:, h * 1024 + (7 - lb) * 128:
                                        h * 1024 + (8 - lb) * 128],
                            in0=ps_av[:, lb * 128:(lb + 1) * 128],
                            in1=zB[:, lb * 128:(lb + 1) * 128],
                            op=ALU.mult)
                    s_ve.inc(mmv)

    if standalone:
        with nc.Block() as blk:
            @blk.sync
            def _(e):
                e.dma_start(out=o_ao.rearrange("(hh p) t -> p hh t", p=128),
                            in_=attnout[:].rearrange("p (hh t) -> p hh t", hh=NH)
                            ).then_inc(s_st.h, 16)
                s_st.exp += 16
                e.wait_ge(s_st.h, s_st.exp)
        p.close()
        return nc
    return attnout


def build_phase_C(p=None, io=None):
    standalone = p is None
    if standalone:
        p = P()
        io = {}
    nc = p.nc
    Wo = io.get("Wo") or p.inp("Wo", (H, H), F32)
    x = io.get("x_fm") or p.inp("x_fm", (H, TPC), F32)
    ebias = io.get("ebias") or p.inp("ebias", (64, H), F32)
    wdownT = io.get("wdownT") or p.inp("wdownT", (H, 128), F32)
    bdown = io.get("bdown") or p.inp("bdown", (128, 1), F32)
    o_x1 = io.get("o_x1")
    if o_x1 is None:
        o_x1 = p.out("x1_fm", (H, TPC), F32)
    o_xn1 = io.get("o_xn1")
    if o_xn1 is None:
        o_xn1 = p.out("xn1", (H, TPC), BF16)
    o_xmid = io.get("o_xmid")
    if o_xmid is None:
        o_xmid = p.out("xmid", (128, TPC), BF16)
    if standalone:
        ao = p.inp("attnout", (H, TPC), BF16)
        xnw_in = p.inp("xn", (H, TPC), BF16)
        wsum = p.inp("wsum", (TPC, 1), F32)
        wdeT = p.inp("wdenseT", (64, TPC), BF16)
    else:
        xnw_in = io.get("xn_hbm")
    use_xnt = xnw_in is not None

    aor = io.get("attnout_sb")
    if aor is None:
        aor = p.sb([128, NH * TPC], BF16)
    band = p.sb([128, NF * 128], BF16)
    wdb = p.sb([128, NF * 128], BF16)
    eb_o = [p.sb([64, 128], BF16) for _ in range(2)]
    wde = io.get("wde_sb")
    if wde is None:
        wde = p.sb([64, TPC], BF16)
    wsrow = io.get("wsrow_sb")
    if wsrow is None:
        wsrow = p.sb([1, TPC], F32)
    wsB = p.sb([128, TPC], F32)
    bdn = p.sb([128, 1], F32)
    ones1f = p.sb([128, 1], F32)
    onesr = p.sb([1, 128], F32)
    xt = [p.sb([128, 512], F32) for _ in range(4)]
    xnt = [p.sb([128, 512], BF16) for _ in range(4)] if use_xnt else None
    tmp = [p.sb([128, 512], F32) for _ in range(2)]
    x1b = [p.sb([128, 512], F32) for _ in range(4)]
    sqc = [p.sb([128, 512], F32) for _ in range(2)]
    stat = p.sb([1, 512], F32)
    mrow = [p.sb([1, 512], F32) for _ in range(2)]
    rrow = [p.sb([1, 512], F32) for _ in range(2)]
    mB = [p.sb([128, 512], F32) for _ in range(2)]
    rB = [p.sb([128, 512], F32) for _ in range(2)]
    x1t2 = [p.sb([128, 512], F32) for _ in range(2)]
    xn1t = [p.sb([128, 512], BF16) for _ in range(2)]
    xmid_sb = p.sb([128, 1024], BF16)

    s_ld = p.nsem("ld")
    s_wb = p.nsem("wb")
    s_mm = p.nsem("mm")
    s_ve = p.nsem("ve")
    s_sc = p.nsem("sc")
    s_st = p.nsem("st")

    x1ps = [p.ps([128, 512]) for _ in range(2)]
    st_sum = [p.ps([128, 512])[0:1, :] for _ in range(2)]
    st_sq = [p.ps([128, 512])[0:1, :] for _ in range(2)]
    xmid_ps = p.ps([128, 1024])

    with nc.Block() as blk:
        @blk.sync
        def _(e):
            e.dma_start(out=bdn[:], in_=bdown[:, :]).then_inc(s_ld.h, 16)
            s_ld.exp += 16
            if standalone:
                e.dma_start(out=aor[:].rearrange("p (hh t) -> p hh t", hh=NH),
                            in_=ao.rearrange("(hh p) t -> p hh t", p=128)).then_inc(s_ld.h, 16)
                e.dma_start(out=wde[:], in_=wdeT[:, :]).then_inc(s_ld.h, 16)
                e.dma_start(out=wsrow[:], in_=wsum.rearrange("t o -> o t")).then_inc(s_ld.h, 16)
                s_ld.exp += 48
            else:
                e.dma_start(out=wsrow[:],
                            in_=io["wsum_hbm"].rearrange("t o -> o t")).then_inc(s_ld.h, 16)
                s_ld.exp += 16
        @blk.gpsimd
        def _(e):
            e.dma_start(out=wdb[:].rearrange("p (f o) -> p f o", o=128),
                        in_=wchunks(wdownT)).then_inc(s_ld.h, 16)
            s_ld.exp += 16
        ldt = s_ld.exp
        @blk.vector
        def _(e):
            e.memset(ones1f[:], 1.0)
            e.memset(onesr[:], 1.0)
            e.nop().then_inc(s_ve.h, 1)
        s_ve.exp += 1
        vt0 = s_ve.exp
        @blk.tensor
        def _(e):
            e.wait_ge(s_ld.h, ldt)
            e.wait_ge(s_ve.h, vt0)
            for j in range(2):
                mm = e.matmul(x1ps[j][:], onesr[:], wsrow[:, j * 512:(j + 1) * 512],
                              start=True, stop=True)
            s_mm.inc(mm)
        mt0 = s_mm.exp
        @blk.vector
        def _(e):
            e.wait_ge(s_mm.h, mt0)
            for j in range(2):
                mmv = e.tensor_copy(out=wsB[:, j * 512:(j + 1) * 512], in_=x1ps[j][:])
            s_ve.inc(mmv)

    band2 = [band, wdb]
    b_wb = s_wb.exp
    b_ld = s_ld.exp
    b_mm = s_mm.exp
    b_ve = s_ve.exp
    b_sc = s_sc.exp
    b_st = s_st.exp
    nld = 32 + (32 if use_xnt else 0)

    with nc.Block() as blk:
        @blk.gpsimd
        def _(e):
            for o in range(NF):
                if o >= 2:
                    e.wait_ge(s_mm.h, b_mm + 4 * (o - 2) + 2)
                e.dma_start(out=band2[o % 2][:].rearrange("p (f oo) -> p f oo", oo=128),
                            in_=wchunks(Wo)[:, :, o * 128:(o + 1) * 128]).then_inc(s_wb.h, 16)
                e.dma_start(out=eb_o[o % 2][:],
                            in_=ebias[:, o * 128:(o + 1) * 128]).then_inc(s_wb.h, 16)
        s_wb.exp += 32 * NF

        @blk.scalar
        def _(e):
            for o in range(NF):
                if o >= 2:
                    e.wait_ge(s_ve.h, b_ve + 2 * (o - 1))
                for T in range(2):
                    e.dma_start(out=xt[(o % 2) * 2 + T][:],
                                in_=x.rearrange("(f pp) t -> pp f t", pp=128)
                                [:, o, T * 512:(T + 1) * 512]).then_inc(s_ld.h, 16)
                    if use_xnt:
                        e.dma_start(out=xnt[(o % 2) * 2 + T][:],
                                    in_=xnw_in.rearrange("(f pp) t -> pp f t", pp=128)
                                    [:, o, T * 512:(T + 1) * 512]).then_inc(s_ld.h, 16)
                if o >= 1:
                    om = o - 1
                    for T in range(2):
                        e.wait_ge(s_ve.h, b_ve + 2 * om + T + 1)
                        if om >= 1:
                            e.wait_ge(s_mm.h, b_mm + 4 * om + T + 1)
                        s_sc.inc(e.square(out=sqc[T][:], in_=x1b[(om % 2) * 2 + T][:]))
            for T in range(2):
                e.wait_ge(s_ve.h, b_ve + 2 * (NF - 1) + T + 1)
                e.wait_ge(s_mm.h, b_mm + 4 * (NF - 1) + T + 1)
                s_sc.inc(e.square(out=sqc[T][:], in_=x1b[((NF - 1) % 2) * 2 + T][:]))
        s_ld.exp += nld * NF

        @blk.tensor
        def _(e):
            e.wait_ge(s_ld.h, b_ld)
            for o in range(NF):
                e.wait_ge(s_wb.h, b_wb + 32 * (o + 1))
                for T in range(2):
                    if o >= 1:
                        e.wait_ge(s_ve.h, b_ve + 2 * (o - 1) + T + 1)
                    for f in range(NH):
                        e.matmul(x1ps[T][:], band2[o % 2][:, f * 128:(f + 1) * 128],
                                 aor[:, f * 1024 + T * 512: f * 1024 + T * 512 + 512],
                                 start=(f == 0), stop=False)
                    s_mm.inc(e.matmul(x1ps[T][:], eb_o[o % 2][:],
                                      wde[:, T * 512:(T + 1) * 512],
                                      start=False, stop=True))
                for T in range(2):
                    om = o - 1
                    if om >= 0:
                        e.wait_ge(s_ve.h, b_ve + 2 * om + T + 1)
                        e.matmul(st_sum[T], ones1f[:], x1b[(om % 2) * 2 + T][:],
                                 start=(om == 0), stop=False)
                        e.wait_ge(s_sc.h, b_sc + 2 * om + T + 1)
                        s_mm.inc(e.matmul(st_sq[T], ones1f[:], sqc[T][:],
                                          start=(om == 0), stop=False))
            om = NF - 1
            for T in range(2):
                e.wait_ge(s_ve.h, b_ve + 2 * om + T + 1)
                e.matmul(st_sum[T], ones1f[:], x1b[(om % 2) * 2 + T][:],
                         start=False, stop=True)
                e.wait_ge(s_sc.h, b_sc + 2 * (NF - 1) + T + 1)
                s_mm.inc(e.matmul(st_sq[T], ones1f[:], sqc[T][:],
                                  start=False, stop=True))

        @blk.vector
        def _(e):
            for o in range(NF):
                for T in range(2):
                    e.wait_ge(s_mm.h, b_mm + (T + 1 if o == 0 else 4 * o + T - 1))
                    e.wait_ge(s_ld.h, b_ld + nld * (o + 1))
                    if o >= 2:
                        e.wait_ge(s_st.h, b_st + 32 * (o - 1))
                        e.wait_ge(s_sc.h, b_sc + 2 * (o - 2) + T + 1)
                    xnsrc = (xnt[(o % 2) * 2 + T][:] if use_xnt else
                             io["xn_own"][:, o * 1024 + T * 512: o * 1024 + T * 512 + 512])
                    e.tensor_tensor(out=tmp[T][:], in0=xnsrc,
                                    in1=wsB[:, T * 512:(T + 1) * 512], op=ALU.mult)
                    e.tensor_tensor(out=tmp[T][:], in0=tmp[T][:], in1=x1ps[T][:], op=ALU.add)
                    s_ve.inc(e.tensor_tensor(out=x1b[(o % 2) * 2 + T][:], in0=tmp[T][:],
                                             in1=xt[(o % 2) * 2 + T][:], op=ALU.add))

        @blk.sync
        def _(e):
            for o in range(NF):
                for T in range(2):
                    e.wait_ge(s_ve.h, b_ve + 2 * o + T + 1)
                    e.dma_start(out=o_x1.rearrange("(f pp) t -> pp f t", pp=128)
                                [:, o, T * 512:(T + 1) * 512],
                                in_=x1b[(o % 2) * 2 + T][:]).then_inc(s_st.h, 16)
                    s_st.exp += 16

    with nc.Block() as blk:
        @blk.sync
        def _(e):
            e.wait_ge(s_st.h, s_st.exp)
        @blk.gpsimd
        def _(e):
            e.dma_start(out=wdb[:].rearrange("p (f o) -> p f o", o=128),
                        in_=wchunks(wdownT)).then_inc(s_wb.h, 16)
        s_wb.exp += 16
        wdbt = s_wb.exp
        for T in range(2):
            @blk.vector
            def _(e, T=T):
                s_ve.fence(e, e.tensor_scalar(out=mrow[T][:], in0=st_sum[T], scalar1=1.0 / H,
                                              scalar2=None, op0=ALU.mult))
                s_ve.fence(e, e.tensor_scalar(out=stat[:], in0=st_sq[T], scalar1=1.0 / H,
                                              scalar2=None, op0=ALU.mult))
                s_ve.fence(e, e.tensor_tensor(out=rrow[T][:], in0=mrow[T][:], in1=mrow[T][:],
                                              op=ALU.mult))
                s_ve.fence(e, e.tensor_tensor(out=stat[:], in0=stat[:], in1=rrow[T][:],
                                              op=ALU.subtract))
                s_ve.inc(e.tensor_scalar(out=stat[:], in0=stat[:], scalar1=1e-5,
                                         scalar2=None, op0=ALU.add))
            vt = s_ve.exp
            @blk.scalar
            def _(e, T=T):
                e.wait_ge(s_ve.h, vt)
                s_sc.fence(e, e.activation(out=stat[:], in_=stat[:], func=AF.Ln))
                s_sc.inc(e.activation(out=rrow[T][:], in_=stat[:], func=AF.Exp, scale=-0.5))
            st = s_sc.exp
            @blk.tensor
            def _(e, T=T):
                e.wait_ge(s_sc.h, st)
                e.matmul(x1ps[0][:], onesr[:], mrow[T][:], start=True, stop=True)
                s_mm.inc(e.matmul(x1ps[1][:], onesr[:], rrow[T][:], start=True, stop=True))
            mt = s_mm.exp
            @blk.vector
            def _(e, T=T):
                e.wait_ge(s_mm.h, mt)
                e.tensor_copy(out=mB[T][:], in_=x1ps[0][:])
                s_ve.inc(e.tensor_copy(out=rB[T][:], in_=x1ps[1][:]))

    for o in range(NF):
        with nc.Block() as blk:
            @blk.sync
            def _(e):
                for T in range(2):
                    e.dma_start(out=x1t2[T][:],
                                in_=o_x1.rearrange("(f pp) t -> pp f t", pp=128)
                                [:, o, T * 512:(T + 1) * 512]).then_inc(s_ld.h, 16)
            s_ld.exp += 32
            ldo = s_ld.exp
            for T in range(2):
                @blk.vector
                def _(e, T=T):
                    e.wait_ge(s_ld.h, ldo)
                    e.tensor_tensor(out=x1t2[T][:], in0=x1t2[T][:], in1=mB[T][:],
                                    op=ALU.subtract)
                    s_ve.inc(e.tensor_tensor(out=xn1t[T][:], in0=x1t2[T][:], in1=rB[T][:],
                                             op=ALU.mult))
                vt = s_ve.exp
                @blk.tensor
                def _(e, T=T, o=o):
                    e.wait_ge(s_ve.h, vt)
                    e.wait_ge(s_wb.h, wdbt)
                    s_mm.inc(e.matmul(xmid_ps[:, T * 512:(T + 1) * 512],
                                      wdb[:, o * 128:(o + 1) * 128], xn1t[T][:],
                                      start=(o == 0), stop=(o == NF - 1)))
                @blk.sync
                def _(e, T=T, o=o):
                    e.wait_ge(s_ve.h, vt)
                    e.dma_start(out=o_xn1.rearrange("(f pp) t -> pp f t", pp=128)
                                [:, o, T * 512:(T + 1) * 512],
                                in_=xn1t[T][:]).then_inc(s_st.h, 16)
                s_st.exp += 16

    with nc.Block() as blk:
        @blk.scalar
        def _(e):
            s_sc.inc(e.activation(out=xmid_sb[:], in_=xmid_ps[:],
                                  func=AF.Gelu, bias=bdn[:]))
        st = s_sc.exp
        @blk.sync
        def _(e):
            e.wait_ge(s_sc.h, st)
            e.dma_start(out=o_xmid[:, :], in_=xmid_sb[:]).then_inc(s_st.h, 16)
            s_st.exp += 16
            e.wait_ge(s_st.h, s_st.exp)

    if standalone:
        p.close()
        return nc
    return None



_D_DEBUG = False
_MIX_DEBUG = False


def build_phase_D(p=None, io=None):
    standalone = p is None
    if standalone:
        p = P()
        io = {}
    nc = p.nc
    x1 = io.get("x1") or p.inp("x1_fm", (H, TPC), F32)
    xn1 = io.get("xn1") or p.inp("xn1", (H, TPC), BF16)
    xm0 = io.get("xm0") or p.inp("xmid_l0", (128, TPC), BF16)
    xm1 = io.get("xm1") or p.inp("xmid_l1", (128, TPC), BF16)
    gate1 = io.get("gate1") or p.inp("gate1", (1, 1), F32)
    indsel = io.get("indsel") or p.inp("indsel", (128, 2), F32)
    inwT = io.get("inwT") or p.inp("inwT", (128, 552), F32)
    convw_x = io.get("convw_x") or p.inp("convw_x", (128, 14), F32)
    convw_bc = io.get("convw_bc") or p.inp("convw_bc", (16, 14), F32)
    convb_x = io.get("convb_x") or p.inp("convb_x", (128, 2), F32)
    convb_bc = io.get("convb_bc") or p.inp("convb_bc", (16, 2), F32)
    dtb = io.get("dtb") or p.inp("dtb", (4, 1), F32)
    fcdT = io.get("fcdT") or p.inp("fcdT", (128, 8), F32)
    Dv = io.get("Dv") or p.inp("Dv", (4, 1), F32)
    normw = io.get("normw") or p.inp("normw", (128, 2), F32)
    outwT = io.get("outwT") or p.inp("outwT", (128, 256), F32)
    ind4 = io.get("ind4") or p.inp("ind4", (4, 256), F32)
    WbT = io.get("WbT") or p.inp("WbT", (H, H), F32)
    bb32 = io.get("bb32") or p.inp("bb32", (128, NF), F32)
    bu32 = io.get("bu32") or p.inp("bu32", (128, NF), F32)
    wr2T = io.get("wr2T") or p.inp("wr2T", (H, 8), F32)
    A2w = io.get("A2") or p.inp("A2", (H, 128), F32)
    Bmw = io.get("Bm") or p.inp("Bm", (128, H), F32)
    wupT = io.get("wupT") or p.inp("wupT", (128, H), F32)
    ind16 = io.get("ind16") or p.inp("ind16", (8, 128), BF16)
    identw = io.get("ident") or p.inp("ident", (128, 128), BF16)
    o_out = io.get("o_out")
    if o_out is None:
        o_out = p.out("out_fm", (H, TPC), F32)
    if _D_DEBUG:
        o_h1 = p.out("dbg_h1", (128, 2048), BF16)
        o_h2 = p.out("dbg_h2", (128, 2048), BF16)
        o_hfm = p.out("dbg_hfm", (128, TPC), BF16)
        o_cwT = p.out("dbg_cwT", (8, TPC), BF16)
        o_hw = p.out("dbg_hw", (128, TPC), BF16)

    xn1r = p.sb([128, NF * TPC], BF16)
    u = p.sb([128, 2048], BF16)
    h1 = p.sb([128, 2048], BF16)
    inw_b = p.sb([128, 552], BF16)
    fcd_b = p.sb([128, 8], BF16)
    outw_b = p.sb([128, 256], BF16)
    ind4_s = p.sb([4, 256], F32)
    ind16_s = p.sb([8, 128], BF16)
    ident = p.sb([128, 128], BF16)
    cwx = p.sb([128, 14], F32)
    cwbc = p.sb([16, 14], F32)
    cbx = p.sb([128, 2], F32)
    cbbc = p.sb([16, 2], F32)
    dtb_f = p.sb([128, 1], F32)
    dtb_s = dtb_f[32:36, :]
    dv_s = p.sb([4, 1], F32)
    nw_s = p.sb([128, 2], F32)
    sel_s = p.sb([128, 2], F32)
    onesr = p.sb([1, 128], F32)
    ones1f = p.sb([128, 1], F32)
    ones8 = p.sb([128, 8], F32)
    onesbig = p.sb([128, 2048], F32)
    zs = p.sb([128, 4096], F32)
    hf = zs[:, 0:2048]
    xpre = p.sb([128, 4096], F32)
    xs_b = p.sb([128, 4096], BF16)
    yn = p.sb([128, 4096], BF16)
    _arA = p.sb([128, 4096], F32)
    Fl = _arA[0:4, 0:2048]
    g4 = _arA[0:4, 2048:4096]
    dts = _arA[32:36, 0:4096]
    _arB = p.sb([128, 4096], F32)
    Bpre = _arB[0:16, 0:2048]
    Cpre = _arB[0:16, 2048:4096]
    spf = _arB[32:36, 0:4096]
    _arC = p.sb([128, 4096], F32)
    bc_s = _arC[0:1, 0:2048]
    rr = _arC[0:1, 2048:4096]
    bc4 = _arC[32:36, 0:2048]
    tmpB = p.sb([128, 1024], F32)
    small = tmpB[0:16, 0:1024]
    a1B = p.sb([128, 1], F32)
    a2B = p.sb([128, 1], F32)
    gsc = p.sb([1, 4], F32)
    hfm = p.sb([128, TPC], BF16)
    wr2_b = p.sb([128, NF * 8], BF16)
    Bm_o = [p.sb([128, 128], BF16) for _ in range(2)]
    wup_o = [p.sb([128, 128], BF16) for _ in range(2)]
    band = p.sb([128, NF * 128], BF16)
    A2_b = band
    probs = p.sb([128, 8], F32)
    work8 = p.sb([128, 8], F32)
    eq1 = p.sb([128, 8], F32)
    eq2 = p.sb([128, 8], F32)
    m1 = p.sb([128, 1], F32)
    m2 = p.sb([128, 1], F32)
    cw = p.sb([128, 8], F32)
    cwb = p.sb([128, 8], BF16)
    cwT = p.sb([8, TPC], BF16)
    cwBt = tmpB[:, 0:512]
    hw = p.sb([128, TPC], BF16)
    bsum = p.sb([128, NF], F32)
    bsum2 = p.sb([128, NF], F32)
    x1t = [xpre[:, 0:512], xpre[:, 512:1024]]
    x2t = x1t

    s_ld = p.nsem("ld")
    s_wb = p.nsem("wb")
    s_mm = p.nsem("mm")
    s_ve = p.nsem("ve")
    s_sc = p.nsem("sc")
    s_st = p.nsem("st")

    psA = p.ps([128, 1024])
    psB = p.ps([128, 1024])
    psC = p.ps([128, 512])
    psD = p.ps([128, 512])
    psE = p.ps([128, 512])
    psT = p.ps([128, 1024], BF16)

    def ts_ap(e, out, in0, scal, op0, in1=None, op1=None, base=0):
        if in1 is None:
            in1 = onesbig[base:base + in0.shape[0], 0:in0.shape[-1]]
            op1 = ALU.mult
        return e.scalar_tensor_tensor(out=out, in0=in0, scalar=scal, in1=in1,
                                      op0=op0, op1=op1)

    emit_recv = io.get("emit_recv")
    split = emit_recv is not None
    o_base = io.get("base_hbm") if split else o_out
    with nc.Block() as blk:
        @blk.sync
        def _(e):
            e.dma_start(out=xn1r[:].rearrange("p (f t) -> p f t", f=NF),
                        in_=xn1.rearrange("(f p) t -> p f t", p=128)).then_inc(s_ld.h, 16)
            if not split:
                e.dma_start(out=u[:, 0:1024], in_=xm0[:, :]).then_inc(s_ld.h, 16)
                e.dma_start(out=u[:, 1024:2048], in_=xm1[:, :]).then_inc(s_ld.h, 16)
            for ap_in, ap_sb in [(convw_x, cwx[:]), (convw_bc, cwbc[:]), (convb_x, cbx[:]),
                                 (convb_bc, cbbc[:]), (dtb, dtb_s), (Dv, dv_s[:]),
                                 (normw, nw_s[:]), (indsel, sel_s[:]), (gate1, gsc[0:1, 0:1]),
                                 (ind4, ind4_s[:]), (ind16, ind16_s[:]), (identw, ident[:]),
                                 (bb32, bsum[:]), (bu32, bsum2[:])]:
                e.dma_start(out=ap_sb, in_=ap_in[:, :]).then_inc(s_ld.h, 16)
        @blk.gpsimd
        def _(e):
            e.dma_start(out=inw_b[:], in_=inwT[:, :]).then_inc(s_ld.h, 16)
            e.dma_start(out=fcd_b[:], in_=fcdT[:, :]).then_inc(s_ld.h, 16)
            e.dma_start(out=outw_b[:], in_=outwT[:, :]).then_inc(s_ld.h, 16)
            e.dma_start(out=wr2_b[:].rearrange("p (f o) -> p f o", o=8),
                        in_=wchunks(wr2T)).then_inc(s_ld.h, 16)
            e.dma_start(out=A2_b[:].rearrange("p (f o) -> p f o", o=128),
                        in_=wchunks(A2w)).then_inc(s_ld.h, 16)
        s_ld.exp += 16 * (20 if split else 22)
        ldt = s_ld.exp
        @blk.vector
        def _(e):
            e.memset(onesr[:], 1.0)
            e.memset(ones1f[:], 1.0)
            e.memset(ones8[:], 1.0)
            e.memset(onesbig[:], 1.0)
            e.wait_ge(s_ld.h, ldt)
            s_ve.inc(e.tensor_tensor(out=bsum[:], in0=bsum[:], in1=bsum2[:], op=ALU.add))
        @blk.scalar
        def _(e):
            e.wait_ge(s_ld.h, ldt)
            s_sc.inc(e.activation(out=gsc[0:1, 1:2], in_=gsc[0:1, 0:1],
                                  func=AF.Exp, scale=-1.0))
        st = s_sc.exp
        @blk.vector
        def _(e):
            e.wait_ge(s_sc.h, st)
            s_ve.fence(e, e.tensor_scalar(out=gsc[0:1, 1:2], in0=gsc[0:1, 1:2],
                                          scalar1=1.0, scalar2=None, op0=ALU.add))
            s_ve.fence(e, e.reciprocal(out=gsc[0:1, 2:3], in_=gsc[0:1, 1:2]))
            s_ve.fence(e, e.tensor_scalar(out=gsc[0:1, 0:1], in0=gsc[0:1, 2:3],
                                          scalar1=1.0, scalar2=None, op0=ALU.add))
            s_ve.inc(e.tensor_scalar(out=gsc[0:1, 1:2], in0=gsc[0:1, 2:3],
                                     scalar1=-1.0, scalar2=1.0, op0=ALU.mult, op1=ALU.add))
        vt2 = s_ve.exp
        @blk.tensor
        def _(e):
            e.wait_ge(s_ve.h, vt2)
            e.matmul(psC[:, 0:1], onesr[:], gsc[0:1, 0:1], start=True, stop=True)
            s_mm.inc(e.matmul(psC[:, 1:2], onesr[:], gsc[0:1, 1:2], start=True, stop=True))
        mt = s_mm.exp
        @blk.vector
        def _(e):
            e.wait_ge(s_mm.h, mt)
            e.tensor_copy(out=a1B[:], in_=psC[:, 0:1])
            s_ve.inc(e.tensor_copy(out=a2B[:], in_=psC[:, 1:2]))

    def hydra(uin, uout):
        with nc.Block() as blk:
            base_mm = s_mm.exp
            base_ve = s_ve.exp
            @blk.tensor
            def _(e):
                for nt in range(4):
                    if nt:
                        e.wait_ge(s_ve.h, base_ve + nt)
                    rhs = uin[:, nt * 512:(nt + 1) * 512]
                    e.matmul(psA[:, 0:512], inw_b[:, 0:128], rhs, start=True, stop=True)
                    e.matmul(psA[:, 512:1024], inw_b[:, 128:256], rhs, start=True, stop=True)
                    e.matmul(psB[:, 0:512], inw_b[:, 256:384], rhs, start=True, stop=True)
                    e.matmul(psB[:, 512:1024], inw_b[:, 384:512], rhs, start=True, stop=True)
                    e.matmul(psC[0:16, :], inw_b[:, 512:528], rhs, start=True, stop=True)
                    e.matmul(psC[32:48, :], inw_b[:, 528:544], rhs, start=True, stop=True)
                    e.matmul(psD[0:4, :], inw_b[:, 544:548], rhs, start=True, stop=True)
                    s_mm.inc(e.matmul(psD[32:36, :], inw_b[:, 548:552], rhs,
                                      start=True, stop=True))
            s_mm.exp = base_mm + 4
            @blk.vector
            def _(e):
                for nt in range(4):
                    e.wait_ge(s_mm.h, base_mm + nt + 1)
                    sl = slice(nt * 512, (nt + 1) * 512)
                    e.tensor_copy(out=zs[:, sl], in_=psA[:, 0:512])
                    e.tensor_copy(out=zs[:, 2048 + nt * 512:2048 + (nt + 1) * 512],
                                  in_=psA[:, 512:1024])
                    e.tensor_copy(out=xpre[:, sl], in_=psB[:, 0:512])
                    e.tensor_copy(out=xpre[:, 2048 + nt * 512:2048 + (nt + 1) * 512],
                                  in_=psB[:, 512:1024])
                    e.tensor_copy(out=Bpre[:, sl], in_=psC[0:16, :])
                    e.tensor_copy(out=Cpre[:, sl], in_=psC[32:48, :])
                    e.tensor_copy(out=dts[:, sl], in_=psD[0:4, :])
                    s_ve.inc(e.tensor_copy(
                        out=dts[:, 2048 + nt * 512:2048 + (nt + 1) * 512],
                        in_=psD[32:36, :]))
            s_ve.exp = base_ve + 4

        with nc.Block() as blk:
            @blk.vector
            def _(e):
                for c in range(2):
                    base = c * 2048
                    w5 = cwx[:, 7 * c + 5:7 * c + 6]
                    w6 = cwx[:, 7 * c + 6:7 * c + 7]
                    s_ve.fence(e, ts_ap(e, tmpB[:, 0:1024], xpre[:, base:base + 1024],
                                        w5, ALU.mult))
                    s_ve.fence(e, ts_ap(e, xpre[:, base:base + 1024],
                                        xpre[:, base:base + 1024], w6, ALU.mult))
                    s_ve.fence(e, ts_ap(e, xpre[:, base + 1024:base + 2048],
                                        xpre[:, base + 1024:base + 2048], w6, ALU.mult,
                                        in1=tmpB[:, 0:1024], op1=ALU.add))
                for (pre, w5_, w6_) in [(Bpre, cwbc[:, 5:6], cwbc[:, 6:7]),
                                        (Cpre, cwbc[:, 12:13], cwbc[:, 13:14])]:
                    s_ve.fence(e, ts_ap(e, small, pre[:, 0:1024], w5_, ALU.mult))
                    s_ve.fence(e, ts_ap(e, pre[:, 0:1024], pre[:, 0:1024], w6_, ALU.mult))
                    s_ve.fence(e, ts_ap(e, pre[:, 1024:2048], pre[:, 1024:2048], w6_, ALU.mult,
                                        in1=small, op1=ALU.add))
                s_ve.fence(e, ts_ap(e, dts[:, 0:2048], dts[:, 0:2048], dtb_s, ALU.add, base=32))
                s_ve.inc(ts_ap(e, dts[:, 2048:4096], dts[:, 2048:4096], dtb_s, ALU.add, base=32))
            vt = s_ve.exp
            @blk.scalar
            def _(e):
                e.wait_ge(s_ve.h, vt)
                for c in range(2):
                    base = c * 2048
                    e.activation(out=xs_b[:, base:base + 1024], in_=xpre[:, base:base + 1024],
                                 func=AF.Silu, bias=cbx[:, c:c + 1])
                    e.activation(out=xs_b[:, base + 1024:base + 2048],
                                 in_=xpre[:, base + 1024:base + 2048],
                                 func=AF.Silu, bias=cbx[:, c:c + 1])
                e.activation(out=Bpre[:, :], in_=Bpre[:, :], func=AF.Silu, bias=cbbc[:, 0:1])
                e.activation(out=Cpre[:, :], in_=Cpre[:, :], func=AF.Silu, bias=cbbc[:, 1:2])
                s_sc.fence(e, e.activation(out=spf[:, :], in_=dts[:, :], func=AF.Abs))
                s_sc.inc(e.activation(out=spf[:, :], in_=spf[:, :], func=AF.Exp, scale=-1.0))
            st = s_sc.exp
            @blk.vector
            def _(e):
                e.wait_ge(s_sc.h, st)
                s_ve.inc(e.tensor_scalar(out=spf[:, :], in0=spf[:, :], scalar1=1.0,
                                         scalar2=None, op0=ALU.add))
            vt3 = s_ve.exp
            @blk.scalar
            def _(e):
                e.wait_ge(s_ve.h, vt3)
                s_sc.inc(e.activation(out=spf[:, :], in_=spf[:, :], func=AF.Ln))
            st2 = s_sc.exp
            @blk.vector
            def _(e):
                e.wait_ge(s_sc.h, st2)
                s_ve.fence(e, e.tensor_scalar(out=dts[:, :], in0=dts[:, :], scalar1=0.0,
                                              scalar2=None, op0=ALU.max))
                s_ve.fence(e, e.tensor_tensor(out=dts[:, :], in0=dts[:, :], in1=spf[:, :], op=ALU.add))
                s_ve.inc(e.tensor_tensor(out=Bpre[:, :], in0=Bpre[:, :], in1=Cpre[:, :],
                                          op=ALU.mult))
            vt4 = s_ve.exp
            base_mm = s_mm.exp
            base_ve = s_ve.exp
            @blk.tensor
            def _(e):
                e.wait_ge(s_ve.h, vt4)
                for nt in range(4):
                    if nt:
                        e.wait_ge(s_ve.h, base_ve + nt)
                    e.matmul(psE[0:1, :], ones1f[0:16, :], Bpre[:, nt * 512:(nt + 1) * 512],
                             start=True, stop=True)
                    for c in range(2):
                        mm = e.matmul(psD[0:4, :], fcd_b[:, 4 * c:4 * c + 4],
                                      xs_b[:, c * 2048 + nt * 512:c * 2048 + (nt + 1) * 512],
                                      start=(c == 0), stop=(c == 1))
                    s_mm.inc(mm)
            s_mm.exp = base_mm + 4
            @blk.vector
            def _(e):
                for nt in range(4):
                    e.wait_ge(s_mm.h, base_mm + nt + 1)
                    e.tensor_copy(out=bc_s[:, nt * 512:(nt + 1) * 512], in_=psE[0:1, :])
                    s_ve.inc(e.tensor_copy(out=g4[:, nt * 512:(nt + 1) * 512],
                                           in_=psD[0:4, :]))
            s_ve.exp = base_ve + 4

        with nc.Block() as blk:
            base_mm = s_mm.exp
            base_ve = s_ve.exp
            @blk.tensor
            def _(e):
                for j in range(4):
                    if j:
                        e.wait_ge(s_ve.h, base_ve + j)
                    s_mm.inc(e.matmul(psD[0:4, :], onesr[0:1, 0:4],
                                      bc_s[:, j * 512:(j + 1) * 512], start=True, stop=True))
            s_mm.exp = base_mm + 4
            @blk.vector
            def _(e):
                for j in range(4):
                    e.wait_ge(s_mm.h, base_mm + j + 1)
                    s_ve.inc(e.tensor_copy(out=bc4[:, j * 512:(j + 1) * 512],
                                           in_=psD[0:4, :]))
            s_ve.exp = base_ve + 4
            @blk.vector
            def _(e):
                s_ve.fence(e, e.tensor_tensor(out=Fl[:, 0:1024], in0=dts[:, 3072:4096],
                                              in1=bc4[:, 1024:2048], op=ALU.mult))
                s_ve.fence(e, e.tensor_tensor(out=Fl[:, 1024:2048], in0=dts[:, 0:1024],
                                              in1=bc4[:, 0:1024], op=ALU.mult))
                s_ve.inc(ts_ap(e, g4[:, :], g4[:, :], dv_s[:], ALU.add))
            vt6 = s_ve.exp
            base_mm = s_mm.exp
            base_ve = s_ve.exp
            @blk.tensor
            def _(e):
                e.wait_ge(s_ve.h, vt6)
                k = 0
                for c in range(2):
                    for j in range(4):
                        if k:
                            e.wait_ge(s_ve.h, base_ve + k)
                        e.matmul(psA[:, 0:512], ind4_s[:, c * 128:(c + 1) * 128],
                                 Fl[:, (j % 2) * 512 + (j // 2) * 1024:
                                    (j % 2) * 512 + (j // 2) * 1024 + 512],
                                 start=True, stop=True)
                        s_mm.inc(e.matmul(psB[:, 0:512], ind4_s[:, c * 128:(c + 1) * 128],
                                          g4[:, j * 512:(j + 1) * 512],
                                          start=True, stop=True))
                        k += 1
            s_mm.exp = base_mm + 8
            @blk.vector
            def _(e):
                k = 0
                for c in range(2):
                    for j in range(4):
                        e.wait_ge(s_mm.h, base_mm + k + 1)
                        base = c * 2048
                        jj = j * 512
                        swapped = (jj + 1024) % 2048
                        e.tensor_tensor(out=tmpB[:, 0:512], in0=psA[:, 0:512],
                                        in1=xs_b[:, base + swapped:base + swapped + 512],
                                        op=ALU.mult)
                        e.tensor_tensor(out=tmpB[:, 512:1024], in0=psB[:, 0:512],
                                        in1=xs_b[:, base + jj:base + jj + 512],
                                        op=ALU.mult)
                        s_ve.inc(e.tensor_tensor(out=xpre[:, base + jj:base + jj + 512],
                                                 in0=tmpB[:, 0:512], in1=tmpB[:, 512:1024],
                                                 op=ALU.add))
                        k += 1
            s_ve.exp = base_ve + 8
            vt7 = s_ve.exp
            @blk.scalar
            def _(e):
                e.wait_ge(s_ve.h, vt7)
                s_sc.inc(e.activation(out=zs[:], in_=zs[:], func=AF.Silu))
            st3 = s_sc.exp
            @blk.vector
            def _(e):
                e.wait_ge(s_sc.h, st3)
                s_ve.inc(e.tensor_tensor(out=xpre[:], in0=xpre[:], in1=zs[:], op=ALU.mult))
            vt8 = s_ve.exp
            @blk.scalar
            def _(e):
                e.wait_ge(s_ve.h, vt8)
                s_sc.inc(e.square(out=zs[:], in_=xpre[:]))
            st4 = s_sc.exp
            base_mm = s_mm.exp
            base_ve = s_ve.exp
            @blk.tensor
            def _(e):
                e.wait_ge(s_sc.h, st4)
                for nt in range(4):
                    if nt:
                        e.wait_ge(s_ve.h, base_ve + nt)
                    for c in range(2):
                        mm = e.matmul(psE[0:1, :], ones1f[:],
                                      zs[:, c * 2048 + nt * 512:c * 2048 + (nt + 1) * 512],
                                      start=(c == 0), stop=(c == 1))
                    s_mm.inc(mm)
            s_mm.exp = base_mm + 4
            @blk.vector
            def _(e):
                for nt in range(4):
                    e.wait_ge(s_mm.h, base_mm + nt + 1)
                    s_ve.inc(e.tensor_copy(out=rr[:, nt * 512:(nt + 1) * 512],
                                           in_=psE[0:1, :]))
            s_ve.exp = base_ve + 4
            @blk.vector
            def _(e):
                s_ve.inc(e.tensor_scalar(out=rr[:, :], in0=rr[:, :], scalar1=1.0 / 256.0,
                                         scalar2=1e-5, op0=ALU.mult, op1=ALU.add))
            vt9 = s_ve.exp
            @blk.scalar
            def _(e):
                e.wait_ge(s_ve.h, vt9)
                s_sc.fence(e, e.activation(out=rr[:, :], in_=rr[:, :], func=AF.Ln))
                s_sc.inc(e.activation(out=rr[:, :], in_=rr[:, :], func=AF.Exp, scale=-0.5))
            st5 = s_sc.exp
            base_mm = s_mm.exp
            base_ve = s_ve.exp
            @blk.tensor
            def _(e):
                e.wait_ge(s_sc.h, st5)
                for j in range(4):
                    if j:
                        e.wait_ge(s_ve.h, base_ve + j)
                    s_mm.inc(e.matmul(psB[:, 0:512], onesr[:], rr[:, j * 512:(j + 1) * 512],
                                      start=True, stop=True))
            s_mm.exp = base_mm + 4
            @blk.vector
            def _(e):
                for j in range(4):
                    e.wait_ge(s_mm.h, base_mm + j + 1)
                    for c in range(2):
                        mmv = ts_ap(e, yn[:, c * 2048 + j * 512:c * 2048 + (j + 1) * 512],
                                    xpre[:, c * 2048 + j * 512:c * 2048 + (j + 1) * 512],
                                    nw_s[:, c:c + 1], ALU.mult,
                                    in1=psB[:, 0:512], op1=ALU.mult)
                    s_ve.inc(mmv)
            s_ve.exp = base_ve + 4
        with nc.Block() as blk:
            base_mm = s_mm.exp
            base_ve = s_ve.exp
            @blk.tensor
            def _(e):
                for nt in range(4):
                    if nt:
                        e.wait_ge(s_ve.h, base_ve + nt)
                    for c in range(2):
                        mm = e.matmul(psA[:, 0:512], outw_b[:, c * 128:(c + 1) * 128],
                                      yn[:, c * 2048 + nt * 512:c * 2048 + (nt + 1) * 512],
                                      start=(c == 0), stop=(c == 1))
                    s_mm.inc(mm)
            s_mm.exp = base_mm + 4
            @blk.vector
            def _(e):
                for nt in range(4):
                    e.wait_ge(s_mm.h, base_mm + nt + 1)
                    s_ve.inc(e.tensor_copy(out=uout[:, nt * 512:(nt + 1) * 512],
                                           in_=psA[:, 0:512]))
            s_ve.exp = base_ve + 4

    def emit_hydra_gate():
        hydra(u, h1)
        if _D_DEBUG:
            with nc.Block() as blk:
                @blk.sync
                def _(e):
                    e.dma_start(out=o_h1[:, :], in_=h1[:]).then_inc(s_st.h, 16)
                    s_st.exp += 16
                    e.wait_ge(s_st.h, s_st.exp)
        hydra(h1, h1)
        if _D_DEBUG:
            with nc.Block() as blk:
                @blk.sync
                def _(e):
                    e.dma_start(out=o_h2[:, :], in_=h1[:]).then_inc(s_st.h, 16)
                    s_st.exp += 16
                    e.wait_ge(s_st.h, s_st.exp)

        with nc.Block() as blk:
            @blk.vector
            def _(e):
                s_ve.fence(e, ts_ap(e, hf[:, :], h1[:], a2B[:], ALU.mult))
                s_ve.fence(e, ts_ap(e, hf[:, :], u[:], a1B[:], ALU.mult,
                                    in1=hf[:, :], op1=ALU.add))
                s_ve.fence(e, ts_ap(e, tmpB[:, 0:1024], hf[:, 1024:2048],
                                    sel_s[:, 1:2], ALU.mult))
                s_ve.inc(ts_ap(e, hfm[:], hf[:, 0:1024], sel_s[:, 0:1], ALU.mult,
                               in1=tmpB[:, 0:1024], op1=ALU.add))

    if not split:
        emit_hydra_gate()

    for ts in range(8):
        with nc.Block() as blk:
            @blk.tensor
            def _(e):
                for f in range(NF):
                    mm = e.matmul(psC[:, 0:8],
                                  xn1r[:, f * 1024 + ts * 128: f * 1024 + ts * 128 + 128],
                                  wr2_b[:, f * 8:(f + 1) * 8],
                                  start=(f == 0), stop=(f == NF - 1))
                s_mm.inc(mm)
            mt = s_mm.exp
            @blk.vector
            def _(e):
                e.wait_ge(s_mm.h, mt)
                s_ve.fence(e, e.tensor_copy(out=probs[:], in_=psC[:, 0:8]))
                s_ve.inc(e.reduce_max(out=m1[:], in_=probs[:], axis=AX.X, negate=True))
            vt = s_ve.exp
            @blk.scalar
            def _(e):
                e.wait_ge(s_ve.h, vt)
                s_sc.inc(e.activation(out=probs[:], in_=probs[:], func=AF.Exp, bias=m1[:]))
            st = s_sc.exp
            @blk.vector
            def _(e):
                e.wait_ge(s_sc.h, st)
                s_ve.fence(e, e.reduce_sum(out=m2[:], in_=probs[:], axis=AX.X))
                s_ve.fence(e, e.reciprocal(out=m2[:], in_=m2[:]))
                s_ve.fence(e, ts_ap(e, probs[:], probs[:], m2[:], ALU.mult,
                                    in1=ones8[:], op1=ALU.mult))
                s_ve.fence(e, e.reduce_max(out=m1[:], in_=probs[:], axis=AX.X))
                s_ve.fence(e, ts_ap(e, eq1[:], probs[:], m1[:], ALU.is_ge,
                                    in1=ones8[:], op1=ALU.mult))
                s_ve.fence(e, ts_ap(e, work8[:], eq1[:], -1e30, ALU.mult,
                                    in1=probs[:], op1=ALU.add))
                s_ve.fence(e, e.reduce_max(out=m2[:], in_=work8[:], axis=AX.X))
                s_ve.fence(e, ts_ap(e, eq2[:], work8[:], m2[:], ALU.is_ge,
                                    in1=ones8[:], op1=ALU.mult))
                s_ve.fence(e, e.tensor_tensor(out=eq1[:], in0=eq1[:], in1=eq2[:], op=ALU.add))
                s_ve.fence(e, e.tensor_tensor(out=m1[:], in0=m1[:], in1=m2[:], op=ALU.add))
                s_ve.fence(e, e.reciprocal(out=m1[:], in_=m1[:]))
                s_ve.fence(e, e.tensor_scalar(out=m1[:], in0=m1[:], scalar1=2.0,
                                              scalar2=None, op0=ALU.mult))
                s_ve.fence(e, ts_ap(e, cw[:], eq1[:], m1[:], ALU.mult,
                                    in1=probs[:], op1=ALU.mult))
                s_ve.inc(e.tensor_copy(out=cwb[:], in_=cw[:]))
            vt2 = s_ve.exp
            @blk.tensor
            def _(e):
                e.wait_ge(s_ve.h, vt2)
                s_mm.inc(e.transpose(psT[0:8, 0:128], cwb[:], ident[:]))
            mt2 = s_mm.exp
            @blk.vector
            def _(e):
                e.wait_ge(s_mm.h, mt2)
                s_ve.inc(e.tensor_copy(out=cwT[:, ts * 128:(ts + 1) * 128],
                                       in_=psT[0:8, 0:128]))

    for T in range(2):
        with nc.Block() as blk:
            @blk.tensor
            def _(e, T=T):
                for f in range(NF):
                    mm = e.matmul(psA[:, 0:512],
                                  A2_b[:, f * 128:(f + 1) * 128],
                                  xn1r[:, f * 1024 + T * 512: f * 1024 + T * 512 + 512],
                                  start=(f == 0), stop=(f == NF - 1))
                s_mm.inc(mm)
                s_mm.inc(e.matmul(psB[:, 0:512], ind16_s[:],
                                  cwT[:, T * 512:(T + 1) * 512], start=True, stop=True))
            mt = s_mm.exp
            @blk.vector
            def _(e, T=T):
                e.wait_ge(s_mm.h, mt)
                s_ve.fence(e, e.tensor_copy(out=cwBt, in_=psB[:, 0:512]))
                s_ve.inc(e.tensor_tensor(out=hw[:, T * 512:(T + 1) * 512],
                                         in0=psA[:, 0:512], in1=cwBt, op=ALU.mult))

    if _D_DEBUG:
        with nc.Block() as blk:
            @blk.sync
            def _(e):
                e.dma_start(out=o_hfm[:, :], in_=hfm[:]).then_inc(s_st.h, 16)
                e.dma_start(out=o_cwT[:, :], in_=cwT[:]).then_inc(s_st.h, 16)
                e.dma_start(out=o_hw[:, :], in_=hw[:]).then_inc(s_st.h, 16)
                s_st.exp += 48
                e.wait_ge(s_st.h, s_st.exp)

    s_l2d = p.nsem("l2")
    wb32 = [_arA[:, 0:2048], _arA[:, 2048:4096]]
    bandD = [band, yn]

    b_mm, b_ve, b_st, b_wb = s_mm.exp, s_ve.exp, s_st.exp, s_wb.exp
    b_ld, b_l2 = s_ld.exp, s_l2d.exp
    EVT, CA, EVC, X1L, W0L, W1L, STOd = ({} for _ in range(7))
    mmc = b_mm
    for o in range(NF):
        for T in range(2):
            mmc += 1; EVT[(o, T)] = mmc
    vec = b_ve
    for o in range(NF + 1):
        if o >= 1:
            for T in range(2):
                vec += 1; EVC[(o - 1, T)] = vec
        if o < NF:
            vec += 1; CA[(o, 0)] = vec
            vec += 1; CA[(o, 1)] = vec
    lde = b_ld
    for o in range(NF):
        lde += 16; X1L[(o, 0)] = lde
        lde += 16; X1L[(o, 1)] = lde
        lde += 16; W0L[o] = lde
    ldo = b_l2
    for o in range(NF):
        ldo += 16; W1L[o] = ldo
    stc = b_st
    for o in range(NF):
        for T in range(2):
            stc += 16; STOd[(o, T)] = stc
    nwl = 16 if split else 32
    WBD = [b_wb + nwl * (o + 1) for o in range(NF)]

    with nc.Block() as blk:
        @blk.gpsimd
        def _(e):
            for o in range(NF + 1):
                if o < NF:
                    if o >= 1:
                        e.wait_ge(s_mm.h, EVT[(o - 1, 1)])
                    e.dma_start(out=Bm_o[o % 2][:],
                                in_=Bmw[:, o * 128:(o + 1) * 128]).then_inc(s_wb.h, 16)
                    if not split:
                        e.dma_start(out=wup_o[o % 2][:],
                                    in_=wupT[:, o * 128:(o + 1) * 128]).then_inc(s_wb.h, 16)
                if o >= 1:
                    for T in range(2):
                        e.wait_ge(s_ve.h, EVC[(o - 1, T)])
                        e.dma_start(out=o_base.rearrange("(f pp) t -> pp f t", pp=128)
                                    [:, o - 1, T * 512:(T + 1) * 512],
                                    in_=x2t[T][:]).then_inc(s_st.h, 16)
        s_wb.exp += nwl * NF
        s_st.exp += 32 * NF

        @blk.sync
        def _(e):
            for o in range(NF):
                if o >= 1:
                    e.wait_ge(s_st.h, STOd[(o - 1, 1)])
                for T in range(2):
                    e.dma_start(out=x1t[T][:],
                                in_=x1.rearrange("(f pp) t -> pp f t", pp=128)
                                [:, o, T * 512:(T + 1) * 512]).then_inc(s_ld.h, 16)
                if o >= 1:
                    e.wait_ge(s_ve.h, CA[(o - 1, 0)])
                e.dma_start(out=wb32[0].rearrange("p (f oo) -> p f oo", oo=128),
                            in_=wchunks(WbT)[:, 0:16, o * 128:(o + 1) * 128]
                            ).then_inc(s_ld.h, 16)
        s_ld.exp += 48 * NF

        @blk.scalar
        def _(e):
            for o in range(NF):
                if o >= 1:
                    e.wait_ge(s_ve.h, CA[(o - 1, 1)])
                e.dma_start(out=wb32[1].rearrange("p (f oo) -> p f oo", oo=128),
                            in_=wchunks(WbT)[:, 16:32, o * 128:(o + 1) * 128]
                            ).then_inc(s_l2d.h, 16)
        s_l2d.exp += 16 * NF

        @blk.vector
        def _(e):
            for o in range(NF + 1):
                if o >= 1:
                    for T in range(2):
                        e.wait_ge(s_mm.h, EVT[(o - 1, T)])
                        e.wait_ge(s_ld.h, X1L[(o - 1, T)])
                        ts_ap(e, x2t[T][:], x1t[T][:], bsum[:, o - 1:o], ALU.add,
                              in1=psA[:, T * 512:(T + 1) * 512], op1=ALU.add)
                        s_ve.inc(e.tensor_tensor(
                            out=x2t[T][:], in0=x2t[T][:],
                            in1=xn1r[:, (o - 1) * 1024 + T * 512:
                                     (o - 1) * 1024 + T * 512 + 512],
                            op=ALU.add))
                if o < NF:
                    e.wait_ge(s_ld.h, W0L[o])
                    if o >= 2:
                        e.wait_ge(s_mm.h, EVT[(o - 2, 1)])
                    s_ve.inc(e.tensor_copy(out=bandD[o % 2][:, 0:2048],
                                           in_=wb32[0]))
                    e.wait_ge(s_l2d.h, W1L[o])
                    s_ve.inc(e.tensor_copy(out=bandD[o % 2][:, 2048:4096],
                                           in_=wb32[1]))

        @blk.tensor
        def _(e):
            for o in range(NF):
                e.wait_ge(s_wb.h, WBD[o])
                e.wait_ge(s_ve.h, CA[(o, 1)])
                for T in range(2):
                    if o >= 1:
                        e.wait_ge(s_ve.h, EVC[(o - 1, T)])
                    for f in range(NF):
                        e.matmul(psA[:, T * 512:(T + 1) * 512],
                                 bandD[o % 2][:, f * 128:(f + 1) * 128],
                                 xn1r[:, f * 1024 + T * 512: f * 1024 + T * 512 + 512],
                                 start=(f == 0), stop=False)
                    if split:
                        s_mm.inc(e.matmul(psA[:, T * 512:(T + 1) * 512], Bm_o[o % 2][:],
                                          hw[:, T * 512:(T + 1) * 512],
                                          start=False, stop=True))
                    else:
                        e.matmul(psA[:, T * 512:(T + 1) * 512], Bm_o[o % 2][:],
                                 hw[:, T * 512:(T + 1) * 512], start=False, stop=False)
                        s_mm.inc(e.matmul(psA[:, T * 512:(T + 1) * 512],
                                          wup_o[o % 2][:],
                                          hfm[:, T * 512:(T + 1) * 512],
                                          start=False, stop=True))

    assert s_mm.exp == mmc, (s_mm.exp, mmc)
    assert s_ve.exp == vec, (s_ve.exp, vec)
    assert s_st.exp == stc, (s_st.exp, stc)

    with nc.Block() as blk:
        @blk.sync
        def _(e):
            e.wait_ge(s_st.h, s_st.exp)

    if split:
        def u_after(e):
            e.dma_start(out=u[:, 0:1024], in_=xm0[:, :]).then_inc(s_ld.h, 16)
            e.dma_start(out=u[:, 1024:2048], in_=xm1[:, :]).then_inc(s_ld.h, 16)
            s_ld.exp += 32
            e.wait_ge(s_ld.h, s_ld.exp)
        emit_recv(u_after)

        with nc.Block() as blk:
            for eng in ("sync", "gpsimd", "scalar", "tensor", "vector"):
                getattr(blk, eng)(lambda e, v=s_ld.exp: e.wait_ge(s_ld.h, v))

        emit_hydra_gate()

        s_dbg = p.nsem("dbg")
        if _MIX_DEBUG:
            o_du = p.out("dbg_u", (128, 2048), BF16)
            o_dhfm = p.out("dbg_hfm", (128, TPC), BF16)
            o_dwup = p.out("dbg_wup", (128, 128), BF16)
            o_dmo = p.out("dbg_mo", (128, 1024), F32)
            with nc.Block() as blk:
                @blk.sync
                def _(e):
                    e.wait_ge(s_ve.h, s_ve.exp)
                    e.dma_start(out=o_du[:, :], in_=u[:]).then_inc(s_dbg.h, 16)
                    e.dma_start(out=o_dhfm[:, :], in_=hfm[:]).then_inc(s_dbg.h, 16)
                    s_dbg.exp += 32
                    e.wait_ge(s_dbg.h, s_dbg.exp)

        mo = [xpre[:, i * 512:(i + 1) * 512] for i in range(4)]
        ps_mix = [psA, psB]
        b2_mm, b2_ve = s_mm.exp, s_ve.exp
        b2_ld, b2_wb, b2_st = s_ld.exp, s_wb.exp, s_st.exp
        with nc.Block() as blk:
            @blk.gpsimd
            def _(e):
                for o in range(NF):
                    if o >= 2:
                        e.wait_ge(s_mm.h, b2_mm + 2 * (o - 1))
                    e.dma_start(out=wup_o[o % 2][:],
                                in_=wupT[:, o * 128:(o + 1) * 128]).then_inc(s_wb.h, 16)
                    if _MIX_DEBUG and o == 0:
                        e.wait_ge(s_wb.h, b2_wb + 16)
                        e.dma_start(out=o_dwup[:, :],
                                    in_=wup_o[0][:]).then_inc(s_dbg.h, 16)
                        s_dbg.exp += 16
                    if o == 0:
                        e.wait_ge(s_ve.h, b2_ve)
                    if o >= 2:
                        e.wait_ge(s_st.h, b2_st + 32 * (o - 1))
                    for T in range(2):
                        e.dma_start(out=mo[(o % 2) * 2 + T],
                                    in_=o_base.rearrange("(f pp) t -> pp f t", pp=128)
                                    [:, o, T * 512:(T + 1) * 512]).then_inc(s_ld.h, 16)
            s_wb.exp += 16 * NF
            s_ld.exp += 32 * NF

            @blk.tensor
            def _(e):
                for o in range(NF):
                    if o == 0:
                        e.wait_ge(s_ve.h, b2_ve)
                    e.wait_ge(s_wb.h, b2_wb + 16 * (o + 1))
                    for T in range(2):
                        if o >= 2:
                            e.wait_ge(s_ve.h, b2_ve + 2 * (o - 2) + T + 1)
                        s_mm.inc(e.matmul(ps_mix[o % 2][:, T * 512:(T + 1) * 512],
                                          wup_o[o % 2][:], hfm[:, T * 512:(T + 1) * 512],
                                          start=True, stop=True))

            @blk.vector
            def _(e):
                for o in range(NF):
                    e.wait_ge(s_ld.h, b2_ld + 32 * (o + 1))
                    if _MIX_DEBUG and o == 0:
                        e.tensor_copy(out=zs[:, 0:512], in_=mo[0])
                        e.tensor_copy(out=zs[:, 512:1024],
                                      in_=mo[1]).then_inc(s_dbg.h, 1)
                        s_dbg.exp += 1
                    for T in range(2):
                        e.wait_ge(s_mm.h, b2_mm + 2 * o + T + 1)
                        s_ve.inc(e.tensor_tensor(
                            out=mo[(o % 2) * 2 + T], in0=mo[(o % 2) * 2 + T],
                            in1=ps_mix[o % 2][:, T * 512:(T + 1) * 512], op=ALU.add))

            @blk.scalar
            def _(e):
                if _MIX_DEBUG:
                    e.wait_ge(s_dbg.h, s_dbg.exp)
                    e.dma_start(out=o_dmo[:, :], in_=zs[:, 0:1024]).then_inc(s_dbg.h, 16)
                for o in range(NF):
                    for T in range(2):
                        e.wait_ge(s_ve.h, b2_ve + 2 * o + T + 1)
                        e.dma_start(out=o_out.rearrange("(f pp) t -> pp f t", pp=128)
                                    [:, o, T * 512:(T + 1) * 512],
                                    in_=mo[(o % 2) * 2 + T]).then_inc(s_st.h, 16)
            s_st.exp += 32 * NF
            if _MIX_DEBUG:
                s_dbg.exp += 16

        with nc.Block() as blk:
            @blk.sync
            def _(e):
                e.wait_ge(s_st.h, s_st.exp)
                if _MIX_DEBUG:
                    e.wait_ge(s_dbg.h, s_dbg.exp)

    if standalone:
        p.close()
        return nc
    return None


_NC_CACHE = {}


def _get_nc(name, builder):
    if name not in _NC_CACHE:
        _NC_CACHE[name] = builder()
    return _NC_CACHE[name]


def kernel(**inp):
    if FUSED:
        return kernel_fused(inp)
    x = np.asarray(inp["L0_attn_hidden_states"])
    ident = np.eye(128).astype(bf16)
    metas = [core_meta(c) for c in range(NCORES)]

    WrT = np.ascontiguousarray(np.asarray(inp["L0_ffn_router_weight"]).T)
    rbias = np.ascontiguousarray(np.asarray(inp["L0_ffn_router_bias"])[None, :])
    in_maps = []
    for c in range(NCORES):
        b, mq, rows = metas[c]
        cos, sin = rope_tables(rows)
        in_maps.append({
            "x_fm": np.ascontiguousarray(x[b, rows, :].T),
            "Wq": np.asarray(inp["L0_attn_Wq"]), "Wk": np.asarray(inp["L0_attn_Wk"]),
            "Wv": np.asarray(inp["L0_attn_Wv"]), "WrT": WrT, "rbias": rbias,
            "cosT": cos, "sinT": sin, "ident": ident,
        })
    resA = run_bass_kernel_spmd(_get_nc("A", build_phase_A), in_maps,
                                core_ids=list(range(NCORES))).results

    Kfull = {bb: np.zeros((1024, S), bf16) for bb in range(2)}
    Vfull = {bb: np.zeros((S, 1024), bf16) for bb in range(2)}
    knf = {bb: np.zeros((NKV, S), np.float32) for bb in range(2)}
    for c in range(NCORES):
        b, mq, rows = metas[c]
        Kfull[b][:, rows] = resA[c]["K"]
        Vfull[b][rows, :] = resA[c]["Vtm"]
        knf[b][:, rows] = resA[c]["knormsq"]
    E8 = np.repeat(np.eye(NKV, dtype=np.float32), 4, axis=1)
    in_maps = []
    for c in range(NCORES):
        b, mq, rows = metas[c]
        in_maps.append({
            "Q": resA[c]["Q"], "Kfull": Kfull[b], "Vtmfull": Vfull[b],
            "knormsq_full": knf[b], "qmaxsq": resA[c]["qmaxsq"],
            "E8": E8, "masks": attn_masks(mq),
        })
    resB = run_bass_kernel_spmd(_get_nc("B", build_phase_B), in_maps,
                                core_ids=list(range(NCORES))).results

    wdownT = np.ascontiguousarray(np.asarray(inp["L1_attn_w_down"]).T)
    bdown = np.ascontiguousarray(np.asarray(inp["L1_attn_b_down"])[:, None])
    in_maps = []
    for c in range(NCORES):
        b, mq, rows = metas[c]
        in_maps.append({
            "attnout": resB[c]["attnout"], "Wo": np.asarray(inp["L0_attn_Wo"]),
            "x_fm": np.ascontiguousarray(x[b, rows, :].T),
            "xn": resA[c]["xn"], "wsum": resA[c]["wsum"], "wdenseT": resA[c]["wdenseT"],
            "ebias": np.asarray(inp["L0_ffn_expert_bias"]),
            "wdownT": wdownT, "bdown": bdown,
        })
    resC = run_bass_kernel_spmd(_get_nc("C", build_phase_C), in_maps,
                                core_ids=list(range(NCORES))).results

    cw_ = np.asarray(inp["L1_attn_hy_conv_w"])[:, 0, :]
    cb_ = np.asarray(inp["L1_attn_hy_conv_b"])
    fcd = np.asarray(inp["L1_attn_hy_fcD_w"])
    outw = np.asarray(inp["L1_attn_hy_out_w"])
    Bten = np.asarray(inp["L1_ffn_B"])
    ind4 = np.zeros((4, 256), np.float32)
    for col in range(256):
        feat = (col // 128) * 128 + (col % 128)
        ind4[feat // 64, col] = 1.0
    ind16 = np.zeros((8, 128), np.float32)
    for pp in range(128):
        ind16[pp // 16, pp] = 1.0
    common = {
        "gate1": np.asarray(inp["L1_attn_gate1"], np.float32).reshape(1, 1),
        "inwT": np.ascontiguousarray(np.asarray(inp["L1_attn_hy_in_w"]).T),
        "convw_x": np.ascontiguousarray(np.concatenate([cw_[0:128], cw_[128:256]], 1)),
        "convw_bc": np.ascontiguousarray(np.concatenate([cw_[256:272], cw_[272:288]], 1)),
        "convb_x": np.ascontiguousarray(np.stack([cb_[0:128], cb_[128:256]], 1)),
        "convb_bc": np.ascontiguousarray(np.stack([cb_[256:272], cb_[272:288]], 1)),
        "dtb": np.ascontiguousarray(np.asarray(inp["L1_attn_hy_dt_bias"])[:, None]),
        "fcdT": np.ascontiguousarray(np.concatenate([fcd.T[0:128], fcd.T[128:256]], 1)),
        "Dv": np.ascontiguousarray(np.asarray(inp["L1_attn_hy_D"])[:, None]),
        "normw": np.ascontiguousarray(
            np.stack([np.asarray(inp["L1_attn_hy_norm_w"])[0:128],
                      np.asarray(inp["L1_attn_hy_norm_w"])[128:256]], 1)),
        "outwT": np.ascontiguousarray(np.concatenate([outw.T[0:128], outw.T[128:256]], 1)),
        "ind4": ind4,
        "WbT": np.ascontiguousarray(np.asarray(inp["L1_ffn_W_base"]).T),
        "bb32": np.ascontiguousarray(np.asarray(inp["L1_ffn_b_base"]).reshape(NF, 128).T),
        "bu32": np.ascontiguousarray(np.asarray(inp["L1_attn_b_up"]).reshape(NF, 128).T),
        "wr2T": np.ascontiguousarray(np.asarray(inp["L1_ffn_W_router"]).T),
        "A2": np.ascontiguousarray(np.asarray(inp["L1_ffn_A"]).reshape(128, H).T),
        "Bm": np.ascontiguousarray(Bten.transpose(0, 2, 1).reshape(128, H)),
        "wupT": np.ascontiguousarray(np.asarray(inp["L1_attn_w_up"]).T),
        "ind16": ind16.astype(bf16), "ident": ident,
    }
    in_maps = []
    for c in range(NCORES):
        b, mq, rows = metas[c]
        partner = c ^ 1
        xmid_l0 = resC[c]["xmid"] if b == 0 else resC[partner]["xmid"]
        xmid_l1 = resC[partner]["xmid"] if b == 0 else resC[c]["xmid"]
        indsel = np.zeros((128, 2), np.float32)
        indsel[:, 0] = 1.0 if b == 0 else 0.0
        indsel[:, 1] = 1.0 - indsel[:, 0]
        m = {"x1_fm": resC[c]["x1_fm"], "xn1": resC[c]["xn1"],
             "xmid_l0": xmid_l0, "xmid_l1": xmid_l1, "indsel": indsel}
        m.update(common)
        in_maps.append(m)
    resD = run_bass_kernel_spmd(_get_nc("D", build_phase_D), in_maps,
                                core_ids=list(range(NCORES))).results

    out = np.zeros((B, S, H), np.float32)
    for c in range(NCORES):
        b, mq, rows = metas[c]
        out[b, rows, :] = resD[c]["out_fm"].T
    return out


def emit_A_fused(p, io):
    nc = p.nc
    x = io["x_full"]
    Wq, Wk, Wv = io["Wq"], io["Wk"], io["Wv"]
    WrT, rbias = io["WrT"], io["rbias"]
    cosT, sinT, cosF, sinF = io["cosT"], io["sinT"], io["cosF"], io["sinF"]
    identw = io["ident"]
    xnfull, Qh, Kfull, Vtm = io["xnfull"], io["Q_hbm"], io["Kfull"], io["Vtmfull"]
    wsum_hbm = io["wsum_hbm"]
    kmaxr, qmx, wde = io["kmax_sb"], io["qmx_sb"], io["wde_sb"]

    s_ld = p.nsem("ld")
    s_wb = p.nsem("wb")
    s_mm = p.nsem("mm")
    s_ve = p.nsem("ve")
    s_sc = p.nsem("sc")
    s_st = p.nsem("st")

    with p.scope():
        ones1 = p.sb([128, 1], BF16)
        ones1b = p.sb([128, 1], BF16)
        ones1f = p.sb([128, 1], F32)
        onesr = p.sb([1, 128], F32)
        ones64 = p.sb([128, 64], F32)
        ident = p.sb([128, 128], BF16)
        cos_s = p.sb([128, TPC], F32)
        sin_s = p.sb([128, TPC], F32)
        cosf_s = p.sb([128, S], F32)
        sinf_s = p.sb([128, S], F32)
        mrows = p.sb([1, 512], F32)
        rrows = p.sb([1, 512], F32)
        stat = p.sb([1, 512], F32)
        stat2 = p.sb([1, 512], F32)
        mB = p.sb([128, 512], F32)
        rB = p.sb([128, 512], F32)
        sq = [p.sb([128, 512], F32) for _ in range(2)]
        qf = p.sb([128, 512], F32)
        ta = p.sb([128, 512], F32)
        tb = p.sb([128, 512], F32)
        qbf = p.sb([128, 512], BF16)
        qn = p.sb([1, 512], F32)
        kmrow = p.sb([1, 512], F32)

        with nc.Block() as blk:
            @blk.sync
            def _(e):
                e.dma_start(out=cos_s[:], in_=cosT[:, :]).then_inc(s_ld.h, 16)
                e.dma_start(out=sin_s[:], in_=sinT[:, :]).then_inc(s_ld.h, 16)
                e.dma_start(out=cosf_s[:], in_=cosF[:, :]).then_inc(s_ld.h, 16)
                e.dma_start(out=sinf_s[:], in_=sinF[:, :]).then_inc(s_ld.h, 16)
            @blk.scalar
            def _(e):
                e.dma_start(out=ident[:], in_=identw[:, :]).then_inc(s_ld.h, 16)
            @blk.vector
            def _(e):
                e.memset(ones1[:], 1.0)
                e.memset(ones1b[:], 1.0)
                e.memset(ones1f[:], 1.0)
                e.memset(onesr[:], 1.0)
                e.memset(ones64[:], 1.0)
                e.memset(qmx[:], 0.0)
                e.memset(kmaxr[:], 0.0)
            s_ld.exp += 80

        s_l2a = p.nsem("l2")
        with p.scope():
            stg = [p.sb([128, 8 * 512], F32) for _ in range(2)]
            xtb = [p.sb([128, 32 * 512], BF16) for _ in range(2)]
            xnst2 = [p.sb([128, 8 * 512], BF16) for _ in range(2)]
            sqb = [p.sb([128, 512], BF16) for _ in range(2)]
            mB2 = [p.sb([128, 512], F32) for _ in range(2)]
            rB2 = [p.sb([128, 512], F32) for _ in range(2)]
            mrows2 = [p.sb([1, 512], F32) for _ in range(2)]
            rrows2 = [p.sb([1, 512], F32) for _ in range(2)]
            statb = [p.sb([1, 512], F32) for _ in range(2)]
            aps = p.sb([128, 512], F32)
            ps_sum = p.ps([128, 512])[0:1, :]
            ps_sq = p.ps([128, 512])[0:1, :]
            ps_bm = p.ps([128, 512])[:]
            ps_br = p.ps([128, 512])[:]

            NTf = 8
            b_mm, b_ve, b_sc, b_st = s_mm.exp, s_ve.exp, s_sc.exp, s_st.exp
            b_ld, b_l2 = s_ld.exp, s_l2a.exp
            K1, G1, BM, SQ1, LE, F1, CP, AP, CAST, SLD, ALD, STO1 = (
                {} for _ in range(12))
            mmc = b_mm
            for i in range(NTf + 1):
                if i >= 1:
                    mmc += 1; BM[i - 1] = mmc
                if i < NTf:
                    for fc in range(4):
                        for f in range(8):
                            mmc += 1; K1[(i, fc * 8 + f)] = mmc
                        G1[(i, fc)] = mmc
            vec = b_ve
            for i in range(NTf + 1):
                if i < NTf:
                    for fc in range(4):
                        vec += 1; CAST[(i, fc)] = vec
                if i >= 1:
                    vec += 1; CP[i - 1] = vec
                    for fc in range(4):
                        vec += 1; AP[(i - 1, fc)] = vec
                if i < NTf:
                    vec += 5; F1[i] = vec
            scc = b_sc
            for i in range(NTf):
                for k in range(32):
                    scc += 1; SQ1[(i, k)] = scc
                scc += 2; LE[i] = scc
            lde = b_ld
            ldo = b_l2
            for i in range(NTf):
                for fc in range(4):
                    if fc % 2 == 0:
                        lde += 16; SLD[(i, fc)] = lde
                    else:
                        ldo += 16; ALD[(i, fc)] = ldo
            stc = b_st
            for i in range(1, NTf + 1):
                for fc in range(4):
                    stc += 16; STO1[(i - 1, fc)] = stc

            def x_src(i, fc):
                return (x.rearrange("(f p) t -> p f t", p=128)
                        [:, fc * 8:(fc + 1) * 8, i * 512:(i + 1) * 512])

            def ld_war(e, i, fc):
                c = 4 * i + fc
                if c >= 2:
                    ip, fp_ = (c - 2) // 4, (c - 2) % 4
                    e.wait_ge(s_ve.h, CAST[(ip, fp_)])
                    e.wait_ge(s_sc.h, SQ1[(ip, fp_ * 8 + 7)])

            with nc.Block() as blk:
                @blk.sync
                def _(e):
                    for i in range(NTf):
                        for fc in (0, 2):
                            ld_war(e, i, fc)
                            e.dma_start(out=stg[fc % 2][:].rearrange(
                                "p (f t) -> p f t", f=8),
                                in_=x_src(i, fc)).then_inc(s_ld.h, 16)
                s_ld.exp += 16 * 2 * NTf

                @blk.scalar
                def _(e):
                    for i in range(NTf):
                        for fc in range(4):
                            if fc % 2:
                                ld_war(e, i, fc)
                                e.dma_start(out=stg[fc % 2][:].rearrange(
                                    "p (f t) -> p f t", f=8),
                                    in_=x_src(i, fc)).then_inc(s_l2a.h, 16)
                            q, cnt = ((s_ld, SLD) if fc % 2 == 0 else (s_l2a, ALD))
                            e.wait_ge(q.h, cnt[(i, fc)])
                            for f in range(8):
                                k = fc * 8 + f
                                if k >= 2:
                                    e.wait_ge(s_mm.h, K1[(i, k - 2)])
                                s_sc.inc(e.square(
                                    out=sqb[k % 2][:],
                                    in_=stg[fc % 2][:, f * 512:(f + 1) * 512]))
                        e.wait_ge(s_ve.h, F1[i])
                        s_sc.fence(e, e.activation(out=statb[i % 2][:],
                                                   in_=statb[i % 2][:], func=AF.Ln))
                        s_sc.inc(e.activation(out=rrows2[i % 2][:], in_=statb[i % 2][:],
                                              func=AF.Exp, scale=-0.5))
                s_l2a.exp += 16 * 2 * NTf

                @blk.tensor
                def _(e):
                    for i in range(NTf + 1):
                        if i >= 1:
                            jj = i - 1
                            e.wait_ge(s_sc.h, LE[jj])
                            if jj >= 1:
                                e.wait_ge(s_ve.h, CP[jj - 1])
                            e.matmul(ps_bm, onesr[:], mrows2[jj % 2][:],
                                     start=True, stop=True)
                            s_mm.inc(e.matmul(ps_br, onesr[:], rrows2[jj % 2][:],
                                              start=True, stop=True))
                        if i < NTf:
                            for fc in range(4):
                                e.wait_ge(s_ve.h, CAST[(i, fc)])
                                if fc == 0 and i >= 1:
                                    e.wait_ge(s_ve.h, F1[i - 1])
                                for f in range(8):
                                    k = fc * 8 + f
                                    e.matmul(ps_sum, ones1b[:],
                                             xtb[i % 2][:, k * 512:(k + 1) * 512],
                                             start=(k == 0), stop=(k == NF - 1))
                                    e.wait_ge(s_sc.h, SQ1[(i, k)])
                                    s_mm.inc(e.matmul(ps_sq, ones1b[:], sqb[k % 2][:],
                                                      start=(k == 0), stop=(k == NF - 1)))

                @blk.vector
                def _(e):
                    for i in range(NTf + 1):
                        if i < NTf:
                            for fc in range(4):
                                q, cnt = ((s_ld, SLD) if fc % 2 == 0 else (s_l2a, ALD))
                                e.wait_ge(q.h, cnt[(i, fc)])
                                if i >= 2:
                                    e.wait_ge(s_mm.h, G1[(i - 2, 3)])
                                s_ve.inc(e.tensor_copy(
                                    out=xtb[i % 2][:, fc * 4096:(fc + 1) * 4096],
                                    in_=stg[fc % 2][:]))
                        if i >= 1:
                            jj = i - 1
                            e.wait_ge(s_mm.h, BM[jj])
                            e.tensor_copy(out=mB2[jj % 2][:], in_=ps_bm[:, 0:512])
                            s_ve.inc(e.tensor_copy(out=rB2[jj % 2][:],
                                                   in_=ps_br[:, 0:512]))
                            for fc in range(4):
                                ca = 4 * jj + fc
                                if ca >= 2:
                                    cp = ca - 2
                                    e.wait_ge(s_st.h, STO1[(cp // 4, cp % 4)])
                                for f in range(8):
                                    k = fc * 8 + f
                                    e.tensor_tensor(
                                        out=aps[:],
                                        in0=xtb[jj % 2][:, k * 512:(k + 1) * 512],
                                        in1=mB2[jj % 2][:], op=ALU.subtract)
                                    mmv = e.tensor_tensor(
                                        out=xnst2[fc % 2][:, f * 512:(f + 1) * 512],
                                        in0=aps[:], in1=rB2[jj % 2][:], op=ALU.mult)
                                s_ve.inc(mmv)
                        if i < NTf:
                            e.wait_ge(s_mm.h, G1[(i, 3)])
                            if i >= 1:
                                e.wait_ge(s_sc.h, LE[i - 1])
                            s_ve.fence(e, e.tensor_scalar(out=mrows2[i % 2][:],
                                                          in0=ps_sum, scalar1=1.0 / H,
                                                          scalar2=None, op0=ALU.mult))
                            s_ve.fence(e, e.tensor_scalar(out=statb[i % 2][:], in0=ps_sq,
                                                          scalar1=1.0 / H, scalar2=None,
                                                          op0=ALU.mult))
                            s_ve.fence(e, e.tensor_tensor(out=stat2[:], in0=mrows2[i % 2][:],
                                                          in1=mrows2[i % 2][:], op=ALU.mult))
                            s_ve.fence(e, e.tensor_tensor(out=statb[i % 2][:],
                                                          in0=statb[i % 2][:],
                                                          in1=stat2[:], op=ALU.subtract))
                            s_ve.inc(e.tensor_scalar(out=statb[i % 2][:],
                                                     in0=statb[i % 2][:], scalar1=1e-5,
                                                     scalar2=None, op0=ALU.add))

                @blk.gpsimd
                def _(e):
                    for i in range(1, NTf + 1):
                        jj = i - 1
                        for fc in range(4):
                            e.wait_ge(s_ve.h, AP[(jj, fc)])
                            e.dma_start(
                                out=xnfull.rearrange("(f p) t -> p f t", p=128)
                                [:, fc * 8:(fc + 1) * 8, jj * 512:(jj + 1) * 512],
                                in_=xnst2[fc % 2][:].rearrange("p (f t) -> p f t", f=8),
                            ).then_inc(s_st.h, 16)
                s_st.exp += 16 * 4 * NTf

            assert s_mm.exp == mmc, (s_mm.exp, mmc)
            assert s_ve.exp == vec, (s_ve.exp, vec)
            assert s_sc.exp == scc, (s_sc.exp, scc)
            assert s_st.exp == stc, (s_st.exp, stc)

        s_l2 = p.nsem("l2")
        with p.scope():
            kband2 = [p.sb([128, NF * 256], BF16) for _ in range(2)]
            vband2 = [p.sb([128, NF * 256], BF16) for _ in range(2)]
            xbf2 = [p.sb([128, 8 * 512], BF16) for _ in range(4)]
            kf = [p.sb([128, 512], F32) for _ in range(4)]
            vf = [p.sb([128, 512], BF16) for _ in range(4)]
            sq2 = [p.sb([128, 512], F32) for _ in range(4)]
            kst = [p.sb([128, 512], BF16) for _ in range(4)]
            vst = [p.sb([128, 512], BF16) for _ in range(4)]
            ps_kv = [p.ps([128, 512]) for _ in range(4)]
            ps_n2g = [p.ps([128, 512])[0:1, :] for _ in range(2)]
            ps_t2 = p.ps([128, 1024], BF16)

            NJ = 32
            b_mm, b_ve, b_sc, b_st, b_wb = (s_mm.exp, s_ve.exp, s_sc.exp,
                                            s_st.exp, s_wb.exp)
            b_ld, b_l2 = s_ld.exp, s_l2.exp
            G, STT, TR, EV, KM, RP, VC, SQ, LDm, STO = ({} for _ in range(10))
            mmc = b_mm
            for j in range(NJ):
                for fc in range(4):
                    mmc += 1; G[(j, fc)] = mmc
                if j >= 1:
                    jj = j - 1
                    for gl in range(2):
                        mmc += 1; STT[(jj, gl)] = mmc
                    mmc += 1; TR[jj] = mmc
            for gl in range(2):
                mmc += 1; STT[(NJ - 1, gl)] = mmc
            mmc += 1; TR[NJ - 1] = mmc
            vec = b_ve
            for j in range(NJ + 1):
                if j < NJ:
                    vec += 1; EV[j] = vec
                if j >= 1:
                    jj = j - 1
                    for gl in range(2):
                        vec += 3; KM[(jj, gl)] = vec
                        vec += 1; RP[(jj, gl)] = vec
                    vec += 1; VC[jj] = vec
            scc = b_sc
            for j in range(NJ + 1):
                if j >= 1:
                    for gl in range(2):
                        scc += 1; SQ[(j - 1, gl)] = scc
            stc = b_st
            for j in range(NJ + 1):
                if j >= 1:
                    jj = j - 1
                    for gl in range(2):
                        stc += 16; STO[(jj, 'k', gl)] = stc
                    for gl in range(2):
                        stc += 16; STO[(jj, 'v', gl)] = stc
            lde = b_ld
            ldo = b_l2
            for j in range(NJ):
                lde += 16; LDm[(j, 0)] = lde
                lde += 16; LDm[(j, 2)] = lde
                ldo += 16; LDm[(j, 1)] = ldo
                ldo += 16; LDm[(j, 3)] = ldo
            WBS = [b_wb + 32 * (sw + 1) for sw in range(4)]

            def xn_src(j, fc):
                Tf = j % 8
                return (xnfull.rearrange("(f p) t -> p f t", p=128)
                        [:, fc * 8:(fc + 1) * 8, Tf * 512:(Tf + 1) * 512])

            with nc.Block() as blk:
                @blk.sync
                def _(e):
                    for j in range(NJ):
                        for fc in (0, 2):
                            if j >= 1:
                                e.wait_ge(s_mm.h, G[(j - 1, fc)])
                            e.dma_start(out=xbf2[fc][:].rearrange(
                                "p (f t) -> p f t", f=8),
                                in_=xn_src(j, fc)).then_inc(s_ld.h, 16)
                s_ld.exp += 32 * NJ

                @blk.scalar
                def _(e):
                    e.wait_ge(s_st.h, b_st)
                    for j in range(NJ + 1):
                        if j < NJ:
                            for fc in (1, 3):
                                if j >= 1:
                                    e.wait_ge(s_mm.h, G[(j - 1, fc)])
                                e.dma_start(out=xbf2[fc][:].rearrange(
                                    "p (f t) -> p f t", f=8),
                                    in_=xn_src(j, fc)).then_inc(s_l2.h, 16)
                        if j >= 1:
                            jj = j - 1
                            for gl in range(2):
                                e.wait_ge(s_ve.h, EV[jj])
                                if jj >= 2:
                                    e.wait_ge(s_mm.h, STT[(jj - 2, gl)])
                                s_sc.inc(e.square(out=sq2[(jj % 2) * 2 + gl][:],
                                                  in_=kf[(jj % 2) * 2 + gl][:]))
                s_l2.exp += 32 * NJ

                @blk.gpsimd
                def _(e):
                    e.dma_start(out=kband2[0][:].rearrange("p (f o) -> p f o", o=256),
                                in_=wchunks(Wk)[:, :, 0:256]).then_inc(s_wb.h, 16)
                    e.dma_start(out=vband2[0][:].rearrange("p (f o) -> p f o", o=256),
                                in_=wchunks(Wv)[:, :, 0:256]).then_inc(s_wb.h, 16)
                    for j in range(NJ + 1):
                        if j < NJ and j % 8 == 4 and j // 8 < 3:
                            swn = j // 8 + 1
                            if swn >= 2:
                                e.wait_ge(s_mm.h, G[(8 * (swn - 1) - 1, 3)])
                            e.dma_start(out=kband2[swn % 2][:].rearrange(
                                "p (f o) -> p f o", o=256),
                                in_=wchunks(Wk)[:, :, swn * 256:(swn + 1) * 256]
                                ).then_inc(s_wb.h, 16)
                            e.dma_start(out=vband2[swn % 2][:].rearrange(
                                "p (f o) -> p f o", o=256),
                                in_=wchunks(Wv)[:, :, swn * 256:(swn + 1) * 256]
                                ).then_inc(s_wb.h, 16)
                        if j >= 1:
                            jj = j - 1
                            sw, Tf = jj // 8, jj % 8
                            for gl in range(2):
                                g = sw * 2 + gl
                                e.wait_ge(s_ve.h, RP[(jj, gl)])
                                e.dma_start(out=Kfull[g * 128:(g + 1) * 128,
                                                      Tf * 512:(Tf + 1) * 512],
                                            in_=kst[(jj % 2) * 2 + gl][:]
                                            ).then_inc(s_st.h, 16)
                            e.wait_ge(s_ve.h, VC[jj])
                            for gl in range(2):
                                g = sw * 2 + gl
                                e.dma_start(
                                    out=Vtm[Tf * 512:(Tf + 1) * 512,
                                            g * 128:(g + 1) * 128]
                                    .rearrange("(k p) d -> p k d", p=128),
                                    in_=vst[(jj % 2) * 2 + gl][:]
                                    .rearrange("p (k d) -> p k d", k=4),
                                ).then_inc(s_st.h, 16)
                s_wb.exp += 128
                s_st.exp += 64 * NJ

                @blk.tensor
                def _(e):
                    for j in range(NJ + 1):
                        if j < NJ:
                            sw = j // 8
                            for fc in range(4):
                                e.wait_ge((s_ld if fc in (0, 2) else s_l2).h,
                                          LDm[(j, fc)])
                                if fc == 0:
                                    e.wait_ge(s_wb.h, WBS[sw])
                                    if j >= 1:
                                        e.wait_ge(s_ve.h, EV[j - 1])
                                for f in range(8):
                                    k = fc * 8 + f
                                    rhs = xbf2[fc][:, f * 512:(f + 1) * 512]
                                    for gl in range(2):
                                        e.matmul(ps_kv[gl][:],
                                                 kband2[sw % 2][:, k * 256 + gl * 128:
                                                                k * 256 + (gl + 1) * 128],
                                                 rhs, start=(k == 0), stop=(k == NF - 1))
                                        mm = e.matmul(ps_kv[2 + gl][:],
                                                      vband2[sw % 2][:, k * 256 + gl * 128:
                                                                     k * 256 + (gl + 1) * 128],
                                                      rhs, start=(k == 0), stop=(k == NF - 1))
                                s_mm.inc(mm)
                        if j >= 1:
                            jj = j - 1
                            for gl in range(2):
                                e.wait_ge(s_sc.h, SQ[(jj, gl)])
                                if jj >= 1:
                                    e.wait_ge(s_ve.h, KM[(jj - 1, gl)])
                                s_mm.inc(e.matmul(ps_n2g[gl], ones1f[:],
                                                  sq2[(jj % 2) * 2 + gl][:],
                                                  start=True, stop=True))
                            if jj >= 1:
                                e.wait_ge(s_ve.h, VC[jj - 1])
                            for gl in range(2):
                                for k4 in range(4):
                                    mm = e.transpose(
                                        ps_t2[:, gl * 512 + k4 * 128:
                                              gl * 512 + (k4 + 1) * 128],
                                        vf[(jj % 2) * 2 + gl][:, k4 * 128:(k4 + 1) * 128],
                                        ident[:])
                            s_mm.inc(mm)

                @blk.vector
                def _(e):
                    for j in range(NJ + 1):
                        if j < NJ:
                            e.wait_ge(s_mm.h, G[(j, 3)])
                            if j >= 2:
                                e.wait_ge(s_mm.h, TR[j - 2])
                                e.wait_ge(s_sc.h, SQ[(j - 2, 1)])
                            for gl in range(2):
                                e.tensor_copy(out=kf[(j % 2) * 2 + gl][:],
                                              in_=ps_kv[gl][:])
                                mmv = e.tensor_copy(out=vf[(j % 2) * 2 + gl][:],
                                                    in_=ps_kv[2 + gl][:])
                            s_ve.inc(mmv)
                        if j >= 1:
                            jj = j - 1
                            sw, Tf = jj // 8, jj % 8
                            for gl in range(2):
                                g = sw * 2 + gl
                                e.wait_ge(s_mm.h, STT[(jj, gl)])
                                s_ve.fence(e, e.tensor_copy(out=kmrow[:],
                                                            in_=ps_n2g[gl]))
                                s_ve.fence(e, e.reduce_max(out=qn[:, 0:1],
                                                           in_=kmrow[:], axis=AX.X))
                                s_ve.fence(e, e.tensor_tensor(
                                    out=kmaxr[:, g:g + 1], in0=kmaxr[:, g:g + 1],
                                    in1=qn[:, 0:1], op=ALU.max))
                                if jj >= 2:
                                    e.wait_ge(s_st.h, STO[(jj - 2, 'k', gl)])
                                cs = cosf_s[:, Tf * 512:(Tf + 1) * 512]
                                sn = sinf_s[:, Tf * 512:(Tf + 1) * 512]
                                kfb = kf[(jj % 2) * 2 + gl]
                                e.tensor_copy(out=tb[0:64, :], in_=kfb[64:128, :])
                                e.tensor_copy(out=tb[64:128, :], in_=kfb[0:64, :])
                                e.tensor_tensor(out=ta[:], in0=kfb[:], in1=cs,
                                                op=ALU.mult)
                                e.tensor_tensor(out=tb[:], in0=tb[:], in1=sn,
                                                op=ALU.mult)
                                s_ve.inc(e.tensor_tensor(
                                    out=kst[(jj % 2) * 2 + gl][:], in0=ta[:],
                                    in1=tb[:], op=ALU.add))
                            e.wait_ge(s_mm.h, TR[jj])
                            if jj >= 2:
                                e.wait_ge(s_st.h, STO[(jj - 2, 'v', 1)])
                            e.tensor_copy(out=vst[(jj % 2) * 2][:], in_=ps_t2[:, 0:512])
                            s_ve.inc(e.tensor_copy(out=vst[(jj % 2) * 2 + 1][:],
                                                   in_=ps_t2[:, 512:1024]))

            assert s_mm.exp == mmc, (s_mm.exp, mmc)
            assert s_ve.exp == vec, (s_ve.exp, vec)
            assert s_sc.exp == scc, (s_sc.exp, scc)
            assert s_st.exp == stc, (s_st.exp, stc)

        xn_own = p.sb([128, NF * TPC], BF16)
        with nc.Block() as blk:
            @blk.sync
            def _(e):
                e.wait_ge(s_st.h, s_st.exp)
                for arm in e.switch_core_id(n=128):
                    mqc = (arm.logical % 8) >> 1
                    for l in range(8):
                        e.dma_start(
                            out=xn_own[:].rearrange("p (f t) -> p f t", f=NF)
                            [:, :, l * 128:(l + 1) * 128],
                            in_=xnfull.rearrange("(f p) t -> p f t", p=128)
                            [:, :, (4 * l + mqc) * 128:(4 * l + mqc) * 128 + 128],
                        ).then_inc(s_ld.h, 16)
                s_ld.exp += 128
                e.wait_ge(s_ld.h, s_ld.exp)

        with p.scope():
            band2q = [p.sb([128, NF * 128], BF16) for _ in range(2)]
            qst = [qbf, p.sb([128, 512], BF16)]
            ps_q = [p.ps([128, 512]) for _ in range(2)]
            ps_n3g = [p.ps([128, 512])[0:1, :] for _ in range(2)]
            ps_mm2 = ps_q[0]

            NI = NH * NT
            b_mm, b_ve, b_sc, b_st, b_wb = (s_mm.exp, s_ve.exp, s_sc.exp,
                                            s_st.exp, s_wb.exp)
            G3, STT3, SQ3, RP3, STO3 = ({} for _ in range(5))
            mmc = b_mm
            for i in range(NI):
                mmc += 1; G3[i] = mmc
                if i >= 1:
                    mmc += 1; STT3[i - 1] = mmc
            mmc += 1; STT3[NI - 1] = mmc
            scc = b_sc
            for i in range(NI):
                scc += 1; SQ3[i] = scc
            vec = b_ve
            for i in range(1, NI + 1):
                vec += 3; vec += 1; RP3[i - 1] = vec
            stc = b_st
            for i in range(1, NI + 1):
                stc += 16; STO3[i - 1] = stc
            WBQ = [b_wb + 16 * (o + 1) for o in range(NH)]

            with nc.Block() as blk:
                @blk.gpsimd
                def _(e):
                    for o in range(NH):
                        if o >= 2:
                            e.wait_ge(s_mm.h, G3[2 * (o - 2) + 1])
                        e.dma_start(out=band2q[o % 2][:].rearrange(
                            "p (f oo) -> p f oo", oo=128),
                            in_=wchunks(Wq)[:, :, o * 128:(o + 1) * 128]
                            ).then_inc(s_wb.h, 16)
                s_wb.exp += 16 * NH

                @blk.tensor
                def _(e):
                    for i in range(NI + 1):
                        if i < NI:
                            o, T = i // 2, i % 2
                            if T == 0:
                                e.wait_ge(s_wb.h, WBQ[o])
                            if i >= 2:
                                e.wait_ge(s_sc.h, SQ3[i - 2])
                                e.wait_ge(s_ve.h, RP3[i - 2])
                            for f in range(NF):
                                mm = e.matmul(ps_q[i % 2][:],
                                              band2q[o % 2][:, f * 128:(f + 1) * 128],
                                              xn_own[:, f * TPC + T * 512:
                                                     f * TPC + T * 512 + 512],
                                              start=(f == 0), stop=(f == NF - 1))
                            s_mm.inc(mm)
                        if i >= 1:
                            jj = i - 1
                            e.wait_ge(s_sc.h, SQ3[jj])
                            s_mm.inc(e.matmul(ps_n3g[jj % 2], ones1f[:], sq[jj % 2][:],
                                              start=True, stop=True))

                @blk.scalar
                def _(e):
                    for i in range(NI):
                        e.wait_ge(s_mm.h, G3[i])
                        if i >= 2:
                            e.wait_ge(s_mm.h, STT3[i - 2])
                        s_sc.inc(e.square(out=sq[i % 2][:], in_=ps_q[i % 2][:]))

                @blk.vector
                def _(e):
                    for i in range(1, NI + 1):
                        jj = i - 1
                        o, T = jj // 2, jj % 2
                        e.wait_ge(s_mm.h, STT3[jj])
                        s_ve.fence(e, e.tensor_copy(out=qn[:], in_=ps_n3g[jj % 2]))
                        s_ve.fence(e, e.reduce_max(out=qn[:, 1:2], in_=qn[:],
                                                   axis=AX.X))
                        s_ve.fence(e, e.tensor_tensor(out=qmx[:, o:o + 1],
                                                      in0=qmx[:, o:o + 1],
                                                      in1=qn[:, 1:2], op=ALU.max))
                        if jj >= 2:
                            e.wait_ge(s_st.h, STO3[jj - 2])
                        e.tensor_copy(out=qf[:], in_=ps_q[jj % 2][:])
                        cs = cos_s[:, T * 512:(T + 1) * 512]
                        sn = sin_s[:, T * 512:(T + 1) * 512]
                        e.tensor_copy(out=tb[0:64, :], in_=qf[64:128, :])
                        e.tensor_copy(out=tb[64:128, :], in_=qf[0:64, :])
                        e.tensor_tensor(out=ta[:], in0=qf[:], in1=cs, op=ALU.mult)
                        e.tensor_tensor(out=tb[:], in0=tb[:], in1=sn, op=ALU.mult)
                        s_ve.inc(e.tensor_tensor(out=qst[jj % 2][:], in0=ta[:],
                                                 in1=tb[:], op=ALU.add))

                @blk.sync
                def _(e):
                    for i in range(1, NI + 1):
                        jj = i - 1
                        o, T = jj // 2, jj % 2
                        e.wait_ge(s_ve.h, RP3[jj])
                        e.dma_start(out=Qh[o * 128:(o + 1) * 128,
                                           T * 512:(T + 1) * 512],
                                    in_=qst[jj % 2][:]).then_inc(s_st.h, 16)
                s_st.exp += 16 * NI

            assert s_mm.exp == mmc, (s_mm.exp, mmc)
            assert s_ve.exp == vec, (s_ve.exp, vec)
            assert s_sc.exp == scc, (s_sc.exp, scc)
            assert s_st.exp == stc, (s_st.exp, stc)

            wr_full = p.sb([128, NF * 64], BF16)
            rb_row = p.sb([1, 64], F32)
            rbB = p.sb([128, 64], F32)
            probs = p.sb([128, 64], F32)
            work = p.sb([128, 64], F32)
            eqm = p.sb([128, 64], F32)
            mask8 = p.sb([128, 64], F32)
            wsv = p.sb([128, 1], F32)
            mrow1 = p.sb([128, 1], F32)
            zr = p.sb([128, 1], F32)
            wdet = p.sb([128, 64], BF16)
            ps_r = ps_mm2[:, 0:64]
            ps_wt = p.ps([128, 1024], BF16)[0:64, 0:128]

            with nc.Block() as blk:
                @blk.gpsimd
                def _(e):
                    e.dma_start(out=wr_full[:].rearrange("p (f o) -> p f o", o=64),
                                in_=wchunks(WrT)).then_inc(s_wb.h, 16)
                @blk.sync
                def _(e):
                    e.dma_start(out=rb_row[:], in_=rbias[:, :]).then_inc(s_wb.h, 16)
                s_wb.exp += 32
                wbr = s_wb.exp
                @blk.tensor
                def _(e, wbr=wbr):
                    e.wait_ge(s_wb.h, wbr)
                    s_mm.inc(e.matmul(ps_mm2[:, 0:64], onesr[:], rb_row[:],
                                      start=True, stop=True))
                mt = s_mm.exp
                @blk.vector
                def _(e, mt=mt):
                    e.wait_ge(s_mm.h, mt)
                    s_ve.inc(e.tensor_copy(out=rbB[:], in_=ps_mm2[:, 0:64]))

            for tsl in range(8):
                with nc.Block() as blk:
                    @blk.tensor
                    def _(e, tsl=tsl):
                        for f in range(NF):
                            mm = e.matmul(ps_r,
                                          xn_own[:, f * TPC + tsl * 128:
                                                 f * TPC + tsl * 128 + 128],
                                          wr_full[:, f * 64:(f + 1) * 64],
                                          start=(f == 0), stop=(f == NF - 1))
                        s_mm.inc(mm)
                    mt = s_mm.exp
                    @blk.vector
                    def _(e, mt=mt):
                        e.wait_ge(s_mm.h, mt)
                        s_ve.fence(e, e.tensor_tensor(out=probs[:], in0=ps_r, in1=rbB[:],
                                                      op=ALU.add))
                        s_ve.inc(e.reduce_max(out=mrow1[:], in_=probs[:], axis=AX.X,
                                              negate=True))
                    vt = s_ve.exp
                    @blk.scalar
                    def _(e, vt=vt):
                        e.wait_ge(s_ve.h, vt)
                        s_sc.inc(e.activation(out=probs[:], in_=probs[:], func=AF.Exp,
                                              bias=mrow1[:]))
                    st2 = s_sc.exp
                    @blk.vector
                    def _(e, st2=st2):
                        e.wait_ge(s_sc.h, st2)
                        s_ve.fence(e, e.reduce_sum(out=zr[:], in_=probs[:], axis=AX.X))
                        s_ve.fence(e, e.reciprocal(out=zr[:], in_=zr[:]))
                        e.memset(mask8[:], 0.0)
                        e.memset(wsv[:], 0.0)
                        s_ve.fence(e, e.tensor_copy(out=work[:], in_=probs[:]))
                        for it in range(8):
                            s_ve.fence(e, e.reduce_max(out=mrow1[:], in_=work[:], axis=AX.X))
                            s_ve.fence(e, e.scalar_tensor_tensor(
                                out=eqm[:], in0=work[:], scalar=mrow1[:], in1=ones64[:],
                                op0=ALU.is_ge, op1=ALU.mult))
                            e.tensor_tensor(out=mask8[:], in0=mask8[:], in1=eqm[:],
                                            op=ALU.max)
                            e.scalar_tensor_tensor(out=wsv[:], in0=mrow1[:], scalar=0.0,
                                                   in1=wsv[:], op0=ALU.max, op1=ALU.add)
                            s_ve.fence(e, e.scalar_tensor_tensor(
                                out=work[:], in0=eqm[:], scalar=-1e30, in1=work[:],
                                op0=ALU.mult, op1=ALU.add))
                        s_ve.fence(e, e.tensor_tensor(out=wsv[:], in0=wsv[:], in1=zr[:],
                                                      op=ALU.mult))
                        s_ve.inc(e.scalar_tensor_tensor(
                            out=wdet[:], in0=mask8[:], scalar=zr[:], in1=probs[:],
                            op0=ALU.mult, op1=ALU.mult))
                    vt2 = s_ve.exp
                    @blk.tensor
                    def _(e, vt2=vt2):
                        e.wait_ge(s_ve.h, vt2)
                        s_mm.inc(e.transpose(ps_wt, wdet[:, 0:64], ident[:]))
                    mt2 = s_mm.exp
                    @blk.vector
                    def _(e, tsl=tsl, mt2=mt2):
                        e.wait_ge(s_mm.h, mt2)
                        s_ve.inc(e.tensor_copy(out=wde[:, tsl * 128:(tsl + 1) * 128],
                                               in_=ps_wt))
                    vt3 = s_ve.exp
                    @blk.sync
                    def _(e, tsl=tsl, vt3=vt3):
                        e.wait_ge(s_ve.h, vt3)
                        e.dma_start(out=wsum_hbm[tsl * 128:(tsl + 1) * 128, :],
                                    in_=wsv[:]).then_inc(s_st.h, 16)
                    s_st.exp += 16

        with nc.Block() as blk:
            @blk.sync
            def _(e):
                e.dma_start(out=io["xn_hbm"].rearrange("(f p) t -> p f t", p=128),
                            in_=xn_own[:].rearrange("p (f t) -> p f t", f=NF)
                            ).then_inc(s_st.h, 16)
                s_st.exp += 16
                e.wait_ge(s_st.h, s_st.exp)


I32 = mybir.dt.int32


def emit_exchange_send(p, xmid_loc, xsh, fsh, onei, nonce):
    nc = p.nc
    s_st = p.nsem("st")
    with nc.Block() as blk:
        @blk.sync
        def _(e):
            e.dma_start(out=onei[:], in_=nonce[:, :]).then_inc(s_st.h, 16)
        s_st.exp += 16
        ot = s_st.exp
        @blk.gpsimd
        def _(e):
            e.wait_ge(s_st.h, ot)
            base = s_st.exp
            for arm in e.switch_core_id(n=128):
                b = (arm.logical % 8) & 1
                e.dma_start(out=xsh[b], in_=xmid_loc[:, :]).then_inc(s_st.h, 16)
                e.wait_ge(s_st.h, base + 16)
                e.dma_start(out=fsh[b:b + 1, :], in_=onei[:]).then_inc(s_st.h, 16)
                e.wait_ge(s_st.h, base + 32)
            s_st.exp = base + 32


def emit_exchange_recv(p, xmid_loc, xsh, fsh, xmid_lay, onei, after=None):
    nc = p.nc
    s_st = p.nsem("st")
    with nc.Block() as blk:
        @blk.gpsimd
        def _(e):
            with (e.register("rf") as rf, e.register("rc") as rc,
                  e.register("rm") as rm):
                base = s_st.exp
                for arm in e.switch_core_id(n=128):
                    b = (arm.logical % 8) & 1
                    e.load(rm, onei[0:1, 0:1])
                    e.reg_mov(rc, 1)
                    with e.While(rc):
                        e.load(rf, fsh[1 - b:2 - b, 0:1])
                        e.reg_sub(rc, rm, rf)
                    e.dma_start(out=xmid_lay[1 - b], in_=xsh[1 - b]).then_inc(s_st.h, 16)
                    e.dma_start(out=xmid_lay[b], in_=xmid_loc[:, :]).then_inc(s_st.h, 16)
                    e.wait_ge(s_st.h, base + 32)
                s_st.exp = base + 32
            if after is not None:
                after(e)


def build_fused():
    p = P()
    nc = p.nc
    ext = {}
    for nm, shape, dt in [
        ("x_full", (H, S), F32), ("x_fm", (H, TPC), F32),
        ("Wq", (H, H), F32), ("Wk", (H, 1024), F32), ("Wv", (H, 1024), F32),
        ("WrT", (H, 64), F32), ("rbias", (1, 64), F32),
        ("cosT", (HD, TPC), F32), ("sinT", (HD, TPC), F32),
        ("cosF", (HD, S), F32), ("sinF", (HD, S), F32),
        ("ident", (128, 128), BF16), ("E8", (NKV, NH), F32),
        ("masks", (4 * 128, 128), BF16),
        ("Wo", (H, H), F32), ("ebias", (64, H), F32),
        ("wdownT", (H, 128), F32), ("bdown", (128, 1), F32),
        ("gate1", (1, 1), F32), ("indsel", (128, 2), F32),
        ("inwT", (128, 552), F32), ("convw_x", (128, 14), F32),
        ("convw_bc", (16, 14), F32), ("convb_x", (128, 2), F32),
        ("convb_bc", (16, 2), F32), ("dtb", (4, 1), F32),
        ("fcdT", (128, 8), F32), ("Dv", (4, 1), F32), ("normw", (128, 2), F32),
        ("outwT", (128, 256), F32), ("ind4", (4, 256), F32),
        ("WbT", (H, H), F32), ("bb32", (128, NF), F32), ("bu32", (128, NF), F32),
        ("wr2T", (H, 8), F32), ("A2", (H, 128), F32), ("Bm", (128, H), F32),
        ("wupT", (128, H), F32), ("ind16", (8, 128), BF16),
    ]:
        ext[nm] = p.inp(nm, shape, dt)
    o_out = p.out("out_fm", (H, TPC), F32)

    def scratch(nm, shape, dt, **kw):
        return nc.dram_tensor(nm, list(shape), dt, **kw).ap()

    xnfull = scratch("s_xnfull", (H, S), BF16)
    Qh = scratch("s_Q", (H, TPC), BF16)
    Kfull = scratch("s_K", (1024, S), BF16)
    Vtm = scratch("s_V", (S, 1024), BF16)
    wsum_hbm = scratch("s_wsum", (TPC, 1), F32)
    x1s = scratch("s_x1", (H, TPC), F32)
    xn1s = scratch("s_xn1", (H, TPC), BF16)
    xmid_loc = scratch("s_xmid", (128, TPC), BF16)
    xmid_lay = scratch("s_xmlay", (2, 128, TPC), BF16)
    xsh = scratch("sh_xmid", (2, 128, TPC), BF16, addr_space="Shared")
    fsh = scratch("sh_flag", (2, 4), I32, addr_space="Shared")

    xnown_hbm = scratch("s_xnown", (H, TPC), BF16)
    base_hbm = scratch("s_base", (H, TPC), F32)
    with p.scope():
        wde = p.sb([64, TPC], BF16)
        qmx = p.sb([1, NH], F32)
        kmax = p.sb([1, NKV], F32)

        with p.scope():
            ioA = dict(ext)
            ioA.update(xnfull=xnfull, Q_hbm=Qh, Kfull=Kfull, Vtmfull=Vtm,
                       wsum_hbm=wsum_hbm, kmax_sb=kmax,
                       qmx_sb=qmx, wde_sb=wde, xn_hbm=xnown_hbm)
            emit_A_fused(p, ioA)

        with p.scope():
            attnout = p.sb([128, NH * TPC], BF16)
            with p.scope():
                ioB = dict(Q=Qh, Kfull=Kfull, Vtmfull=Vtm, E8=ext["E8"],
                           masks=ext["masks"], attnout_sb=attnout,
                           qmx_sb=qmx, kmax_sb=kmax)
                build_phase_B(p, ioB)
            with p.scope():
                ioC = dict(Wo=ext["Wo"], x_fm=ext["x_fm"], ebias=ext["ebias"],
                           wdownT=ext["wdownT"], bdown=ext["bdown"],
                           o_x1=x1s, o_xn1=xn1s, o_xmid=xmid_loc,
                           attnout_sb=attnout, xn_hbm=xnown_hbm, wde_sb=wde,
                           wsum_hbm=wsum_hbm)
                build_phase_C(p, ioC)

    onei_x = p.sb([1, 4], I32)
    ext["nonce"] = p.inp("nonce", (1, 4), I32)
    emit_exchange_send(p, xmid_loc, xsh, fsh, onei_x, ext["nonce"])

    with p.scope():
        ioD = dict(x1=x1s, xn1=xn1s, xm0=xmid_lay[0], xm1=xmid_lay[1],
                   emit_recv=lambda after: emit_exchange_recv(
                       p, xmid_loc, xsh, fsh, xmid_lay, onei_x, after=after),
                   base_hbm=base_hbm,
                   gate1=ext["gate1"], indsel=ext["indsel"], inwT=ext["inwT"],
                   convw_x=ext["convw_x"], convw_bc=ext["convw_bc"],
                   convb_x=ext["convb_x"], convb_bc=ext["convb_bc"],
                   dtb=ext["dtb"], fcdT=ext["fcdT"], Dv=ext["Dv"],
                   normw=ext["normw"], outwT=ext["outwT"], ind4=ext["ind4"],
                   WbT=ext["WbT"], bb32=ext["bb32"], bu32=ext["bu32"],
                   wr2T=ext["wr2T"], A2=ext["A2"], Bm=ext["Bm"],
                   wupT=ext["wupT"], ind16=ext["ind16"], ident=ext["ident"],
                   o_out=o_out)
        build_phase_D(p, ioD)

    p.close()
    return nc


FUSED = True


def kernel_fused(inp):
    x = np.asarray(inp["L0_attn_hidden_states"])
    ident = np.eye(128).astype(bf16)
    metas = [core_meta(c) for c in range(NCORES)]
    cosF, sinF = rope_tables(np.arange(S))
    x_full_fm = [np.ascontiguousarray(x[bb].T) for bb in range(2)]
    cw_ = np.asarray(inp["L1_attn_hy_conv_w"])[:, 0, :]
    cb_ = np.asarray(inp["L1_attn_hy_conv_b"])
    fcd = np.asarray(inp["L1_attn_hy_fcD_w"])
    outw = np.asarray(inp["L1_attn_hy_out_w"])
    Bten = np.asarray(inp["L1_ffn_B"])
    ind4 = np.zeros((4, 256), np.float32)
    for col in range(256):
        feat = (col // 128) * 128 + (col % 128)
        ind4[feat // 64, col] = 1.0
    ind16 = np.zeros((8, 128), np.float32)
    for pp in range(128):
        ind16[pp // 16, pp] = 1.0
    E8 = np.repeat(np.eye(NKV, dtype=np.float32), 4, axis=1)
    nwv = np.asarray(inp["L1_attn_hy_norm_w"])
    common = {
        "Wq": np.asarray(inp["L0_attn_Wq"]), "Wk": np.asarray(inp["L0_attn_Wk"]),
        "Wv": np.asarray(inp["L0_attn_Wv"]),
        "WrT": np.ascontiguousarray(np.asarray(inp["L0_ffn_router_weight"]).T),
        "rbias": np.ascontiguousarray(np.asarray(inp["L0_ffn_router_bias"])[None, :]),
        "cosF": cosF, "sinF": sinF, "ident": ident, "E8": E8,
        "Wo": np.asarray(inp["L0_attn_Wo"]),
        "ebias": np.asarray(inp["L0_ffn_expert_bias"]),
        "wdownT": np.ascontiguousarray(np.asarray(inp["L1_attn_w_down"]).T),
        "bdown": np.ascontiguousarray(np.asarray(inp["L1_attn_b_down"])[:, None]),
        "gate1": np.asarray(inp["L1_attn_gate1"], np.float32).reshape(1, 1),
        "inwT": np.ascontiguousarray(np.asarray(inp["L1_attn_hy_in_w"]).T),
        "convw_x": np.ascontiguousarray(np.concatenate([cw_[0:128], cw_[128:256]], 1)),
        "convw_bc": np.ascontiguousarray(np.concatenate([cw_[256:272], cw_[272:288]], 1)),
        "convb_x": np.ascontiguousarray(np.stack([cb_[0:128], cb_[128:256]], 1)),
        "convb_bc": np.ascontiguousarray(np.stack([cb_[256:272], cb_[272:288]], 1)),
        "dtb": np.ascontiguousarray(np.asarray(inp["L1_attn_hy_dt_bias"])[:, None]),
        "fcdT": np.ascontiguousarray(np.concatenate([fcd.T[0:128], fcd.T[128:256]], 1)),
        "Dv": np.ascontiguousarray(np.asarray(inp["L1_attn_hy_D"])[:, None]),
        "normw": np.ascontiguousarray(np.stack([nwv[0:128], nwv[128:256]], 1)),
        "outwT": np.ascontiguousarray(np.concatenate([outw.T[0:128], outw.T[128:256]], 1)),
        "ind4": ind4,
        "WbT": np.ascontiguousarray(np.asarray(inp["L1_ffn_W_base"]).T),
        "bb32": np.ascontiguousarray(np.asarray(inp["L1_ffn_b_base"]).reshape(NF, 128).T),
        "bu32": np.ascontiguousarray(np.asarray(inp["L1_attn_b_up"]).reshape(NF, 128).T),
        "wr2T": np.ascontiguousarray(np.asarray(inp["L1_ffn_W_router"]).T),
        "A2": np.ascontiguousarray(np.asarray(inp["L1_ffn_A"]).reshape(128, H).T),
        "Bm": np.ascontiguousarray(Bten.transpose(0, 2, 1).reshape(128, H)),
        "wupT": np.ascontiguousarray(np.asarray(inp["L1_attn_w_up"]).T),
        "ind16": ind16.astype(bf16),
    }
    _ = np.asarray(inp["L1_attn_hy_A_log"])
    nonce = np.random.randint(1, 2 ** 30, size=(1, 4)).astype(np.int32)
    in_maps = []
    for c in range(NCORES):
        b, mq, rows = metas[c]
        cos, sin = rope_tables(rows)
        indsel = np.zeros((128, 2), np.float32)
        indsel[:, 0] = 1.0 if b == 0 else 0.0
        indsel[:, 1] = 1.0 - indsel[:, 0]
        m = {"x_full": x_full_fm[b],
             "x_fm": np.ascontiguousarray(x[b, rows, :].T),
             "cosT": cos, "sinT": sin,
             "masks": attn_masks(mq), "indsel": indsel, "nonce": nonce}
        m.update(common)
        in_maps.append(m)
    res = run_bass_kernel_spmd(_get_nc("F", build_fused), in_maps,
                               core_ids=list(range(NCORES))).results
    global LAST_RES
    LAST_RES = res
    out = np.zeros((B, S, H), np.float32)
    for c in range(NCORES):
        b, mq, rows = metas[c]
        out[b, rows, :] = res[c]["out_fm"].T
    return out
